# version 11
# baseline (speedup 1.0000x reference)
import contextlib
import os
import numpy as np
import ml_dtypes
import concourse.bass as bass
import concourse.mybir as mybir
from concourse.bass_utils import run_bass_kernel_spmd
from concourse.alu_op_type import AluOpType as OP

F32 = mybir.dt.float32
BF16 = mybir.dt.bfloat16
AF = mybir.ActivationFunctionType
AX = mybir.AxisListType.X
BF = ml_dtypes.bfloat16

D = 1024
T = 2048
S = 4096
NCH = 8
NT = 4
LAT = 256
EXP = 8
FMOE = 768
FCH = 6
PAIRS = [[0, 1], [2, 3], [4, 5], [6, 7]]

ENGS = ("tensor", "vector", "scalar", "gpsimd", "sync")


def segs(core):
    if core % 2 == 0:
        return [(0, 1024), (3072, 1024)]
    return [(1024, 2048)]


class Prog:
    def __init__(self):
        self.nc = bass.Bass("TRN2", target_bir_lowering=False)
        self.q = {e: [] for e in ENGS}
        self.es = contextlib.ExitStack()
        self.sems = {}
        self.ctr = {}

    def sem(self, name):
        if name not in self.sems:
            self.sems[name] = self.es.enter_context(self.nc.semaphore(name))
            self.ctr[name] = 0
        return self.sems[name]

    _uid = 0

    def sb(self, name, shape, dt, scope=None):
        Prog._uid += 1
        return (scope or self.es).enter_context(
            self.nc.sbuf_tensor(f"{name}_u{Prog._uid}", shape, dt))

    def psum(self, name, shape, dt):
        return self.es.enter_context(self.nc.psum_tensor(name, shape, dt))

    def inp(self, name, shape, dt):
        return self.nc.dram_tensor(name, shape, dt, kind="ExternalInput").ap()

    def outp(self, name, shape, dt):
        return self.nc.dram_tensor(name, shape, dt, kind="ExternalOutput").ap()

    def idram(self, name, shape, dt):
        return self.nc.dram_tensor(name, shape, dt).ap()

    def wait(self, eng, semname, val):
        if val is None or val <= 0:
            return
        s = self.sem(semname)
        self.q[eng].append(lambda e, s=s, v=val: e.wait_ge(s, v))

    SELFWAIT = {"vector", "scalar", "gpsimd"}

    def op(self, eng, fn, inc=None, n=1, nowait=False):
        tgt, s = None, None
        if inc is None and eng in self.SELFWAIT and not nowait:
            inc = {"vector": "dv", "scalar": "sc", "gpsimd": "gp"}[eng]
        if inc is not None:
            s = self.sem(inc)
            self.ctr[inc] += n
            tgt = self.ctr[inc]
        sw = (eng in self.SELFWAIT) and not nowait
        def thunk(e, fn=fn, s=s, n=n, sw=sw, tgt=tgt):
            ins = fn(e)
            if s is not None:
                ins.then_inc(s, n)
                if sw:
                    e.wait_ge(s, tgt)
        self.q[eng].append(thunk)
        return tgt

    def dma(self, eng, out, in_, inc):
        return self.op(eng, lambda e, o=out, i=in_: e.dma_start(o, i), inc, 16)

    def barrier(self):
        s = self.sem("bar")
        self.ctr["bar"] += len(ENGS)
        tgt = self.ctr["bar"]
        for eng in ENGS:
            self.q[eng].append(lambda e, s=s: e.nop().then_inc(s, 1))
            self.q[eng].append(lambda e, s=s, v=tgt: e.wait_ge(s, v))

    def finalize(self):
        with self.nc.Block() as blk:
            for eng in ENGS:
                q = self.q[eng]
                def run(e, q=q):
                    for f in q:
                        f(e)
                getattr(blk, eng)(run)
        self.es.close()
        return self.nc



def make_common(p):
    c = {}
    c["pf"] = [p.psum(f"pf{i}", [128, 512], F32) for i in range(7)]
    c["pb16"] = p.psum("pb16", [128, 1024], BF16)
    for nm, shape, dt in [
        ("id_bf", [128, 128], BF16), ("id_f", [128, 128], F32),
        ("id8_f", [8, 8], F32), ("ones1_bf", [128, 1], BF16),
        ("onesr_f", [1, 128], F32), ("onesc_f", [128, 1], F32),
        ("sel2_bf", [128, 2], BF16),
        ("bsel_f", [2, 128], F32), ("h01_f", [2, 256], F32),
        ("tri_bf", [128, 128], BF16), ("esel_f", [8, 1024], F32),
    ]:
        c[nm] = p.sb(nm, shape, dt)
        c["in_" + nm] = p.inp("c_" + nm, shape, dt)
    c["rowpool"] = p.sb("rowpool", [128, 6144], F32)
    c["xhat"] = [p.sb(f"xh{k}", [128, T], BF16) for k in range(NCH)]
    c["ln_count"] = 0
    c["dbg"] = []
    return c


def dbg_dump(p, c, name, ap, shape, dt):
    if not os.environ.get("DBG"):
        return
    for sname in ("dv", "gp", "sc", "pe"):
        p.wait("sync", sname, p.ctr[sname])
    o = p.outp(name, shape, dt)
    p.dma("sync", o, ap, "st")
    p.wait("sync", "st", p.ctr["st"])


def row1(c, i):
    assert i < 9
    base = (i // 3) * 32
    col = (i % 3) * 2048
    return c["rowpool"][base:base + 1, col:col + 2048]


def row2(c, i):
    assert i == 0
    return c["rowpool"][0:2, 4096:6144]


def host_consts():
    id128 = np.eye(128)
    sel2 = np.zeros((128, 2), np.float32); sel2[0:64, 0] = 1; sel2[64:128, 1] = 1
    bsel = np.zeros((2, 128), np.float32); bsel[0, 0:64] = 1; bsel[1, 64:128] = 1
    h01 = np.zeros((2, 256), np.float32); h01[0, 0:128] = 1; h01[1, 128:256] = 1
    tri = (np.arange(128)[:, None] <= np.arange(128)[None, :]).astype(BF)
    esel = np.zeros((8, 1024), np.float32)
    for e in range(8):
        esel[e, e * 128:(e + 1) * 128] = 1
    return {
        "c_id_bf": id128.astype(BF), "c_id_f": id128.astype(np.float32),
        "c_id8_f": np.eye(8, dtype=np.float32),
        "c_ones1_bf": np.ones((128, 1), BF),
        "c_onesr_f": np.ones((1, 128), np.float32),
        "c_onesc_f": np.ones((128, 1), np.float32),
        "c_sel2_bf": sel2.astype(BF), "c_bsel_f": bsel,
        "c_h01_f": h01, "c_tri_bf": tri, "c_esel_f": esel,
    }


def emit_load_consts(p, c):
    for nm in ["id_bf", "id_f", "id8_f", "ones1_bf", "onesr_f", "onesc_f",
               "sel2_bf", "bsel_f", "h01_f", "tri_bf", "esel_f"]:
        p.dma("sync", c[nm][:], c["in_" + nm], "lds")
    c["consts_ready"] = p.ctr["lds"]



def emit_ln(p, c):
    c["ln_count"] += 1
    x, xh, pf = c["x"], c["xhat"], c["pf"]
    with contextlib.ExitStack() as sc:
        rb = p.sb("ln_rb", [128, T], F32, sc)
        mrb = p.sb("ln_mrb", [128, T], F32, sc)
        x2b = [p.sb(f"ln_x2_{i}", [128, 512], BF16, sc) for i in range(2)]
        ntmp = [p.sb(f"ln_nt_{i}", [128, 512], F32, sc) for i in range(2)]
        RR, RMR = row1(c, 0), row1(c, 1)
        RA, RB_, RM = row1(c, 3), row1(c, 4), row1(c, 5)
        for k in range(NCH):
            xk = x[k]
            p.op("vector", lambda e, a=xh[k][:], b=xk: e.tensor_copy(a, b), "dv", nowait=(k < NCH - 1))
        xb_done = p.ctr["dv"]
        p.wait("gpsimd", "dv", xb_done)
        war = [0, 0]
        x2war = [0, 0]
        for n in range(NT):
            sl = slice(n * 512, (n + 1) * 512)
            psA = pf[(n % 2) * 2][0:1, :]
            psB = pf[(n % 2) * 2 + 1][0:1, :]
            x2tgt = []
            for k in range(NCH):
                j = (n * NCH + k) % 2
                p.wait("gpsimd", "pe", x2war[j])
                p.op("gpsimd", lambda e, b=x2b[j][:], a=xh[k][:, sl]: e.tensor_mul(b, a, a), "gp")
                x2tgt.append(p.ctr["gp"])
            p.wait("tensor", "dv", max(xb_done, war[n % 2]))
            for k in range(NCH):
                j = (n * NCH + k) % 2
                p.op("tensor", lambda e, ps=psA, a=xh[k][:, sl], s=(k == 0), t=(k == NCH - 1):
                     e.matmul(ps, c["ones1_bf"][:], a, start=s, stop=t), "pe")
                p.wait("tensor", "gp", x2tgt[k])
                p.op("tensor", lambda e, ps=psB, b=x2b[j][:], s=(k == 0), t=(k == NCH - 1):
                     e.matmul(ps, c["ones1_bf"][:], b, start=s, stop=t), "pe")
                x2war[j] = p.ctr["pe"]
            mm = p.ctr["pe"]
            p.wait("vector", "pe", mm)
            p.op("vector", lambda e, ps=psA, d=RA[0:1, sl]: e.tensor_copy(d, ps), "dv")
            p.op("vector", lambda e, ps=psB, d=RB_[0:1, sl]: e.tensor_copy(d, ps), "dv")
            war[n % 2] = p.ctr["dv"]
        p.op("vector", lambda e: e.tensor_scalar(RM, RA, 1.0 / D, None, OP.mult), "dv")
        p.op("vector", lambda e: e.tensor_scalar(RB_, RB_, 1.0 / D, None, OP.mult), "dv")
        p.op("vector", lambda e: e.tensor_tensor(RA, RM, RM, OP.mult), "dv")
        p.op("vector", lambda e: e.tensor_tensor(RB_, RB_, RA, OP.subtract), "dv")
        p.op("vector", lambda e: e.tensor_scalar(RB_, RB_, 1e-5, None, OP.add), "dv")
        p.wait("scalar", "dv", p.ctr["dv"])
        p.op("scalar", lambda e: e.sqrt(RB_, RB_), "sc")
        p.wait("vector", "sc", p.ctr["sc"])
        p.op("vector", lambda e: e.reciprocal(RA, RB_), "dv")
        p.op("vector", lambda e: e.tensor_tensor(RM, RM, RA, OP.mult), "dv")
        p.op("vector", lambda e: e.tensor_copy(RR, RA), "dv")
        p.op("vector", lambda e: e.tensor_copy(RMR, RM), "dv")
        rows_done = p.ctr["dv"]
        p.wait("tensor", "dv", rows_done)
        bcwar = [0, 0]
        for n in range(NT):
            sl = slice(n * 512, (n + 1) * 512)
            ps = pf[4 + (n % 2)]
            p.wait("tensor", "dv", bcwar[n % 2])
            p.op("tensor", lambda e, ps=ps[0:128, :], r=RR[0:1, sl]:
                 e.matmul(ps, c["onesr_f"][:], r, start=True, stop=True), "pe")
            p.wait("vector", "pe", p.ctr["pe"])
            p.op("vector", lambda e, ps=ps[0:128, :], d=rb[:, sl]: e.tensor_copy(d, ps), "dv")
            p.wait("tensor", "dv", p.ctr["dv"])
            p.op("tensor", lambda e, ps=ps[0:128, :], r=RMR[0:1, sl]:
                 e.matmul(ps, c["onesr_f"][:], r, start=True, stop=True), "pe")
            p.wait("vector", "pe", p.ctr["pe"])
            p.op("vector", lambda e, ps=ps[0:128, :], d=mrb[:, sl]: e.tensor_copy(d, ps), "dv")
            bcwar[n % 2] = p.ctr["dv"]
        bc_done = p.ctr["dv"]
        stats_pe = p.ctr["pe"]
        p.wait("gpsimd", "dv", bc_done)
        p.wait("gpsimd", "pe", stats_pe)
        p.wait("vector", "pe", stats_pe)
        for k in range(NCH):
            xk = x[k]
            for n in range(NT):
                sl = slice(n * 512, (n + 1) * 512)
                eng, sem = ("vector", "dv") if k % 2 == 0 else ("gpsimd", "gp")
                j = 0 if k % 2 == 0 else 1
                tmp = ntmp[j][:]
                p.op(eng, lambda e, t=tmp, a=xk[:, sl], r=rb[:, sl]:
                     e.tensor_tensor(t, a, r, OP.mult), sem)
                p.op(eng, lambda e, t=tmp, d=xh[k][:, sl], m=mrb[:, sl]:
                     e.tensor_tensor(d, t, m, OP.subtract), sem)
        c["ln_dv"] = p.ctr["dv"]
        c["ln_gp"] = p.ctr["gp"]
    p.wait("tensor", "dv", c["ln_dv"])
    p.wait("tensor", "gp", c["ln_gp"])



def emit_l0_attn(p, c):
    pf, pb16 = c["pf"], c["pb16"]
    xh = c["xhat"]
    in_wqkv = p.inp("w_qkv", [D, 3072], F32)
    in_wout = p.inp("w_out", [D, D], F32)
    in_bout = p.inp("b_out_col", [128, NCH], F32)
    with contextlib.ExitStack() as sc:
        qb = [p.sb(f"l0q{i}", [128, T], BF16, sc) for i in range(NCH)]
        kb = p.sb("l0k", [128, T], BF16, sc)
        vb = p.sb("l0v", [128, T], BF16, sc)
        kT = [p.sb(f"l0kT{i}", [128, 128], BF16, sc) for i in range(2)]
        vT = [p.sb(f"l0vT{i}", [128, 130], BF16, sc) for i in range(2)]
        elu = [p.sb(f"l0e{i}", [128, 512], F32, sc) for i in range(4)]
        NW = 130
        ccbuf = p.sb("l0cc", [128, NCH * NW + 8], F32, sc)
        kvb = p.sb("l0kvb", [128, NCH * 128], BF16, sc)
        qzb = [p.sb(f"l0qz{i}", [128, 512], BF16, sc) for i in range(2)]
        wq_s = [p.sb(f"l0wq{i}", [128, 8 * 384], BF16, sc) for i in range(2)]
        wo_s = [p.sb(f"l0wo{i}", [128, 8 * 128], BF16, sc) for i in range(2)]
        boutc = p.sb("l0bout", [128, NCH], F32, sc)
        tmpE = [p.sb(f"l0te{i}", [128, 512], F32, sc) for i in range(2)]
        attn = xh
        den = row2(c, 0)
        cc_in = p.idram("l0cc_in", [128, NCH * NW + 8], F32)
        cc_out = p.idram("l0cc_out", [128, NCH * NW + 8], F32)

        p.dma("sync", boutc[:], in_bout, "lds")
        b_ready = p.ctr["lds"]

        def load_wqkv(cc):
            buf = wq_s[cc % 2]
            for k in range(NCH):
                for j, ofs in enumerate((cc * 128, 1024 + cc * 128, 2048 + cc * 128)):
                    src = in_wqkv[k * 128:(k + 1) * 128, ofs:ofs + 128]
                    p.dma("gpsimd", buf[:, k * 384 + j * 128: k * 384 + (j + 1) * 128], src, "ldg")
            return p.ctr["ldg"]

        wq_ready = [load_wqkv(0)]
        wq_war = [0, 0]
        p.op("vector", lambda e: e.memset(vT[0][:, 128:129], 1.0), "dv", nowait=True)
        p.op("vector", lambda e: e.memset(vT[1][:, 128:129], 1.0), "dv")
        ones_set = p.ctr["dv"]
        elu_sc = [0, 0, 0, 0]
        evwar = [0, 0, 0, 0]
        twar = 0
        for cc in range(NCH):
            if cc + 1 < NCH:
                p.wait("gpsimd", "pe", wq_war[(cc + 1) % 2])
                wq_ready.append(load_wqkv(cc + 1))
            wbuf = wq_s[cc % 2]
            p.wait("tensor", "ldg", wq_ready[cc])
            for wj, dst in ((0, qb[cc]), (1, kb), (2, vb)):
                for n in range(NT):
                    sl = slice(n * 512, (n + 1) * 512)
                    jr = (wj * NT + n) % 4
                    ps = pf[jr]
                    p.wait("tensor", "dv", evwar[jr])
                    for k in range(NCH):
                        p.op("tensor", lambda e, ps=ps[:], w=wbuf[:, k * 384 + wj * 128: k * 384 + wj * 128 + 128],
                             a=xh[k][:, sl], s=(k == 0), t=(k == NCH - 1):
                             e.matmul(ps, w, a, start=s, stop=t), "pe")
                    mm = p.ctr["pe"]
                    if wj == 2:
                        p.wait("vector", "pe", mm)
                        p.op("vector", lambda e, ps=ps[:], d=dst[:, sl]: e.tensor_copy(d, ps), "dv")
                        evwar[jr] = p.ctr["dv"]
                    else:
                        eb = elu[jr][:]
                        p.wait("vector", "pe", mm)
                        p.wait("vector", "sc", elu_sc[jr])
                        p.op("vector", lambda e, ps=ps[:], b=eb: e.tensor_scalar(b, ps, 0.0, None, OP.min), "dv")
                        p.wait("scalar", "dv", p.ctr["dv"])
                        p.op("scalar", lambda e, b=eb: e.activation(b, b, AF.Exp), "sc")
                        elu_sc[jr] = p.ctr["sc"]
                        p.wait("vector", "sc", elu_sc[jr])
                        p.op("vector", lambda e, ps=ps[:], b=eb, d=dst[:, sl]:
                             e.scalar_tensor_tensor(d, ps, 0.0, b, OP.max, OP.add), "dv")
                        evwar[jr] = p.ctr["dv"]
            wq_war[cc % 2] = p.ctr["pe"]
            kvev = p.ctr["dv"]
            p.wait("vector", "dv", kvev)
            p.op("vector", lambda e, d=ccbuf[:, NCH * NW + cc: NCH * NW + cc + 1]:
                 e.reduce_sum(d, kb[:], AX), "dv")
            kvps = pf[4]
            tev = [0, 0]
            tkv = [0, 0]
            for tc in range(16):
                tsl = slice(tc * 128, (tc + 1) * 128)
                j = tc % 2
                p.wait("tensor", "dv", tev[j])
                p.op("tensor", lambda e, a=kb[:, tsl], d=pb16[:, j * 256: j * 256 + 128]:
                     e.transpose(d, a, c["id_bf"][:]), "pe")
                p.op("tensor", lambda e, a=vb[:, tsl], d=pb16[:, j * 256 + 128: j * 256 + 256]:
                     e.transpose(d, a, c["id_bf"][:]), "pe")
                ttr = p.ctr["pe"]
                p.wait("vector", "pe", ttr)
                if tc < 2:
                    p.wait("vector", "dv", ones_set)
                p.op("vector", lambda e, d=kT[j][:, 0:128], ps=pb16[:, j * 256: j * 256 + 128]:
                     e.tensor_copy(d, ps), "dv")
                p.op("vector", lambda e, d=vT[j][:, 0:128], ps=pb16[:, j * 256 + 128: j * 256 + 256]:
                     e.tensor_copy(d, ps), "dv")
                tev[j] = p.ctr["dv"]
                p.wait("tensor", "dv", tev[j])
                p.op("tensor", lambda e, s=(tc == 0), t=(tc == 15), l=kT[j][:], r=vT[j][:, 0:129]:
                     e.matmul(kvps[0:128, 0:129], l, r, start=s, stop=t), "pe")
                tkv[j] = p.ctr["pe"]
            p.wait("vector", "pe", p.ctr["pe"])
            p.op("vector", lambda e, d=ccbuf[:, cc * NW: cc * NW + 129]:
                 e.tensor_copy(d, kvps[0:128, 0:129]), "dv")
        p.wait("gpsimd", "dv", p.ctr["dv"])
        p.dma("gpsimd", cc_in, ccbuf[:], "ldg")
        p.wait("gpsimd", "ldg", p.ctr["ldg"])
        p.op("gpsimd", lambda e: e.collective_compute(
            "AllReduce", OP.add, replica_groups=PAIRS,
            ins=[cc_in.opt()], outs=[cc_out.opt()]), "cc0")
        p.wait("gpsimd", "cc0", 1)
        p.dma("gpsimd", ccbuf[:], cc_out, "ldg")
        p.wait("vector", "ldg", p.ctr["ldg"])
        for cc in range(NCH):
            p.op("vector", lambda e, d=ccbuf[0:64, cc * NW + 64: cc * NW + 128]: e.memset(d, 0.0), "dv")
            p.op("vector", lambda e, d=ccbuf[64:128, cc * NW: cc * NW + 64]: e.memset(d, 0.0), "dv")
            p.op("vector", lambda e, d=kvb[:, cc * 128:(cc + 1) * 128], a=ccbuf[:, cc * NW: cc * NW + 128]:
                 e.tensor_copy(d, a), "dv")
        kv_ready = p.ctr["dv"]
        p.wait("tensor", "dv", kv_ready)
        nwar = [0, 0]
        qz_war = 0
        for cc in range(NCH):
            for n in range(NT):
                sl = slice(n * 512, (n + 1) * 512)
                p.wait("vector", "pe", qz_war if n == 0 else 0)
                p.op("vector", lambda e, b=qzb[n % 2][:], a=qb[cc][:, sl],
                     z=ccbuf[:, NCH * NW + cc: NCH * NW + cc + 1]:
                     e.tensor_scalar(b, a, z, None, OP.mult), "dv")
                p.wait("tensor", "dv", p.ctr["dv"])
                dps = pf[5][0:2, :]
                p.op("tensor", lambda e, dps=dps, a=qzb[n % 2][:]:
                     e.matmul(dps, c["sel2_bf"][:], a, start=True, stop=True), "pe")
                p.wait("vector", "pe", p.ctr["pe"])
                p.op("vector", lambda e, dps=dps, d=den[:, sl]:
                     e.tensor_scalar(d, dps, 1e-6, None, OP.add), "dv")
                p.wait("tensor", "dv", p.ctr["dv"])
            p.op("vector", lambda e, d=den: e.reciprocal(d, d), "dv")
            rec_done = p.ctr["dv"]
            p.wait("tensor", "dv", rec_done)
            for n in range(NT):
                sl = slice(n * 512, (n + 1) * 512)
                nps = pf[n % 2]
                bps = pf[2 + n % 2]
                p.wait("tensor", "dv", nwar[n % 2])
                p.op("tensor", lambda e, ps=nps[:], w=kvb[:, cc * 128:(cc + 1) * 128], a=qb[cc][:, sl]:
                     e.matmul(ps, w, a, start=True, stop=True), "pe")
                p.op("tensor", lambda e, ps=bps[:], r=den[:, sl]:
                     e.matmul(ps, c["bsel_f"][:], r, start=True, stop=True), "pe")
                nmm = p.ctr["pe"]
                p.wait("vector", "pe", nmm)
                p.op("vector", lambda e, ps=bps[:]: e.tensor_copy(tmpE[0][:], ps), "dv")
                p.op("vector", lambda e, ps=nps[:], d=attn[cc][:, sl]:
                     e.tensor_tensor(d, ps, tmpE[0][:], OP.mult), "dv")
                nwar[n % 2] = p.ctr["dv"]
            qz_war = p.ctr["pe"]
        attn_done = p.ctr["dv"]
        def load_wout(oc):
            buf = wo_s[oc % 2]
            for k in range(NCH):
                src = in_wout[k * 128:(k + 1) * 128, oc * 128:(oc + 1) * 128]
                p.dma("gpsimd", buf[:, k * 128:(k + 1) * 128], src, "ldg")
            return p.ctr["ldg"]
        wo_ready = [load_wout(0)]
        wo_war = [0, 0]
        p.wait("tensor", "dv", attn_done)
        p.wait("vector", "lds", b_ready)
        gwar = [0, 0]
        for oc in range(NCH):
            if oc + 1 < NCH:
                p.wait("gpsimd", "pe", wo_war[(oc + 1) % 2])
                wo_ready.append(load_wout(oc + 1))
            p.wait("tensor", "ldg", wo_ready[oc])
            for n in range(NT):
                sl = slice(n * 512, (n + 1) * 512)
                ps = pf[n % 2]
                p.wait("tensor", "dv", nwar[n % 2])
                for k in range(NCH):
                    p.op("tensor", lambda e, ps=ps[:], w=wo_s[oc % 2][:, k * 128:(k + 1) * 128],
                         a=attn[k][:, sl], s=(k == 0), t=(k == NCH - 1):
                         e.matmul(ps, w, a, start=s, stop=t), "pe")
                omm = p.ctr["pe"]
                p.wait("vector", "pe", omm)
                p.wait("vector", "gp", gwar[n % 2])
                p.op("vector", lambda e, ps=ps[:], t=tmpE[n % 2][:], b=boutc[:, oc:oc + 1]:
                     e.tensor_scalar(t, ps, b, None, OP.add), "dv")
                nwar[n % 2] = p.ctr["dv"]
                p.wait("gpsimd", "dv", p.ctr["dv"])
                xoc = c["x"][oc]
                p.op("gpsimd", lambda e, d=xoc[:, sl], t=tmpE[n % 2][:]:
                     e.tensor_add(d, d, t), "gp")
                gwar[n % 2] = p.ctr["gp"]
            wo_war[oc % 2] = p.ctr["pe"]
        p.wait("tensor", "gp", p.ctr["gp"])
        p.wait("vector", "gp", p.ctr["gp"])
        p.wait("sync", "gp", p.ctr["gp"])



def emit_l0_ffn(p, c):
    pf = c["pf"]
    xh = c["xhat"]
    in_w = p.inp("ffn_wT", [D, D], F32)
    in_alpha = p.inp("ffn_alpha", [1, 1], F32)
    in_bias = p.inp("ffn_bias_col", [128, NCH], F32)
    with contextlib.ExitStack() as sc:
        wf = [p.sb(f"f_w{i}", [128, D], F32, sc) for i in range(2)]
        tern = [p.sb(f"f_t{k}", [128, D], BF16, sc) for k in range(NCH)]
        t1 = p.sb("f_t1", [128, D], F32, sc)
        asum = p.sb("f_asum", [128, 8], F32, sc)
        row8 = p.sb("f_row8", [8, 128], F32, sc)
        col8 = p.sb("f_col8", [8, 1], F32, sc)
        rowt = p.sb("f_rowt", [1, 8], F32, sc)
        sc3 = p.sb("f_sc3", [1, 3], F32, sc)
        bc3 = p.sb("f_bc3", [128, 3], F32, sc)
        biasc = p.sb("f_bias", [128, NCH], F32, sc)
        tmpE = [p.sb(f"f_te{i}", [128, 512], F32, sc) for i in range(4)]
        p.dma("sync", biasc[:], in_bias, "lds")
        p.dma("sync", sc3[0:1, 1:2], in_alpha, "lds")
        misc_ready = p.ctr["lds"]
        red_tgt = [0, 0]
        for k in range(NCH):
            if k >= 2:
                p.wait("sync", "dv", red_tgt[k % 2])
            wt = p.dma("sync", wf[k % 2][:], in_w[k * 128:(k + 1) * 128, :], "lds")
            p.wait("vector", "lds", wt)
            p.op("vector", lambda e, d=asum[:, k:k + 1], a=wf[k % 2][:]:
                 e.reduce_sum(d, a, AX, apply_absolute_value=True), "dv")
            red_tgt[k % 2] = p.ctr["dv"]
        s1 = p.ctr["dv"]
        p.wait("tensor", "dv", s1)
        p.op("tensor", lambda e: e.matmul(pf[0][0:1, 0:8], c["onesc_f"][:], asum[:], start=True, stop=True), "pe")
        p.wait("vector", "pe", p.ctr["pe"])
        p.wait("vector", "lds", misc_ready)
        p.op("vector", lambda e: e.tensor_copy(rowt[:], pf[0][0:1, 0:8]), "dv")
        p.op("vector", lambda e: e.reduce_sum(sc3[0:1, 0:1], rowt[:], AX), "dv")
        p.op("vector", lambda e: e.tensor_scalar(sc3[0:1, 0:1], sc3[0:1, 0:1], 0.7 / (D * D), None, OP.mult), "dv")
        p.op("vector", lambda e: e.tensor_scalar(sc3[0:1, 2:3], sc3[0:1, 0:1], -1.0, None, OP.mult), "dv")
        p.wait("tensor", "dv", p.ctr["dv"])
        p.op("tensor", lambda e: e.matmul(pf[2][0:128, 0:3], c["onesr_f"][:], sc3[0:1, 0:3], start=True, stop=True), "pe")
        p.wait("vector", "pe", p.ctr["pe"])
        p.op("vector", lambda e: e.tensor_copy(bc3[:], pf[2][0:128, 0:3]), "dv")
        bc_done = p.ctr["dv"]
        dbg_dump(p, c, "dbg_bc3", bc3[:], [128, 3], F32)
        dbg_dump(p, c, "dbg_asum", asum[:], [128, 8], F32)
        tern_tgt = [bc_done, bc_done]
        for k in range(NCH):
            p.wait("sync", "dv", tern_tgt[k % 2])
            wt = p.dma("sync", wf[k % 2][:], in_w[k * 128:(k + 1) * 128, :], "lds")
            p.wait("vector", "lds", wt)
            p.op("vector", lambda e, a=wf[k % 2][:]: e.tensor_scalar(t1[:], a, bc3[:, 0:1], None, OP.is_gt), "dv")
            p.op("vector", lambda e, a=wf[k % 2][:]: e.tensor_scalar(a, a, bc3[:, 2:3], None, OP.is_lt), "dv")
            p.op("vector", lambda e, a=wf[k % 2][:], d=tern[k][:]: e.tensor_tensor(d, t1[:], a, OP.subtract), "dv")
            tern_tgt[k % 2] = p.ctr["dv"]
        tern_done = p.ctr["dv"]
        dbg_dump(p, c, "dbg_tern0", tern[0][:], [128, D], BF16)
        p.wait("tensor", "dv", tern_done)
        gwar = [0, 0, 0, 0]
        dwar = [0, 0, 0, 0]
        for oc in range(NCH):
            for n in range(NT):
                sl = slice(n * 512, (n + 1) * 512)
                jr = (oc * NT + n) % 4
                ps = pf[jr]
                p.wait("tensor", "dv", dwar[jr])
                for k in range(NCH):
                    p.op("tensor", lambda e, ps=ps[:], w=tern[k][:, oc * 128:(oc + 1) * 128],
                         a=xh[k][:, sl], s=(k == 0), t=(k == NCH - 1):
                         e.matmul(ps, w, a, start=s, stop=t), "pe")
                mm = p.ctr["pe"]
                p.wait("vector", "pe", mm)
                p.wait("vector", "gp", gwar[jr])
                p.op("vector", lambda e, ps=ps[:], t=tmpE[jr][:], b=biasc[:, oc:oc + 1]:
                     e.tensor_scalar(t, ps, bc3[:, 1:2], b, OP.mult, OP.add), "dv")
                dwar[jr] = p.ctr["dv"]
                p.wait("gpsimd", "dv", p.ctr["dv"])
                xoc = c["x"][oc]
                p.op("gpsimd", lambda e, d=xoc[:, sl], t=tmpE[jr][:]: e.tensor_add(d, d, t), "gp")
                gwar[jr] = p.ctr["gp"]
        p.wait("tensor", "gp", p.ctr["gp"])
        p.wait("vector", "gp", p.ctr["gp"])
        p.wait("sync", "gp", p.ctr["gp"])




def emit_l1_attn(p, c, core, x_scope, x_reload):
    pf, xh = c["pf"], c["xhat"]
    NKC = [12, 16, 28, 32]
    UMAP = {0: [0, 1, 2, 3, 8, 9, 10, 11], 1: [4, 5, 6, 7, 12, 13, 14, 15],
            2: [16, 17, 18, 19, 24, 25, 26, 27], 3: [20, 21, 22, 23, 28, 29, 30, 31]}
    ZOF = [0, 12, 28, 56]

    in_wq = p.inp("wq", [D, D], F32)
    in_wdkv = p.inp("wdkv", [D, LAT], F32).rearrange("(k p) f -> p k f", p=128)
    in_wuk = p.inp("wuk", [LAT, D], F32).rearrange("(k p) f -> p k f", p=128)
    in_wuv = p.inp("wuv", [LAT, D], F32).rearrange("(k p) f -> p k f", p=128)
    in_wo = p.inp("wo", [D, D], F32)
    in_gb = p.inp("l1_gam", [128, 2], F32)
    in_bb = p.inp("l1_bet", [128, 2], F32)
    in_c4 = p.inp("ropeC", [128, S], BF16)
    in_s4 = p.inp("ropeS", [128, S], BF16)
    in_qc4 = p.inp("qropeC", [128, T], BF16)
    in_qs4 = p.inp("qropeS", [128, T], BF16)
    in_masks = p.inp("cmasks", [128, 2 * 8 * 512], BF16)
    in_zvb = p.inp("zvb", [128, 88], F32)
    ag_in = p.idram("ag_in", [2 * 128, T], BF16)
    ag_out = p.idram("ag_out", [4 * 128, T], BF16)

    x_scope.close()
    with contextlib.ExitStack() as sc:
        c4 = p.sb("c4", [128, S], BF16, sc)
        s4 = p.sb("s4", [128, S], BF16, sc)
        wuk_sb = p.sb("wuk", [128, 2 * D], BF16, sc)
        wuv_sb = p.sb("wuv", [128, 2 * D], BF16, sc)
        latful = [p.sb(f"latf{i}", [128, S], BF16, sc) for i in range(2)]
        gbc = p.sb("gbc", [128, 2], F32, sc)
        bbc = p.sb("bbc", [128, 2], F32, sc)
        masks = p.sb("cmask", [128, 2 * 8 * 512], BF16, sc)
        zvb = p.sb("zvb", [128, 88], F32, sc)
        q_all = [p.sb(f"q{i}", [128, T], BF16, sc) for i in range(NCH)]
        ktmp = [p.sb(f"ktmp{i}", [128, 512], F32, sc) for i in range(2)]
        ksw = [p.sb(f"ksw{i}", [128, 512], F32, sc) for i in range(2)]
        q2b = [p.sb(f"q2b{i}", [128, 512], BF16, sc) for i in range(2)]
        p.dma("gpsimd", c4[:], in_c4, "ldg")
        p.dma("gpsimd", s4[:], in_s4, "ldg")
        p.dma("gpsimd", wuk_sb[:].rearrange("p (k f) -> p k f", f=D), in_wuk, "ldg")
        p.dma("gpsimd", wuv_sb[:].rearrange("p (k f) -> p k f", f=D), in_wuv, "ldg")
        p.dma("gpsimd", masks[:], in_masks, "ldg")
        p.dma("sync", gbc[:], in_gb, "lds")
        p.dma("sync", bbc[:], in_bb, "lds")
        p.dma("sync", zvb[:], in_zvb, "lds")
        w1rdy = p.ctr["ldg"]
        g_rdy = p.ctr["lds"]

        def rope_evac(ps, dst, ctab, stab, off, j):
            kt, kw = ktmp[j][:], ksw[j][:]
            csl = slice(off, off + 512)
            p.op("vector", lambda e, ps=ps[:], t=kt: e.tensor_copy(t, ps), "dv")
            for g in range(4):
                src = slice((g ^ 1) * 32, (g ^ 1) * 32 + 32)
                dstp = slice(g * 32, g * 32 + 32)
                p.op("vector", lambda e, a=kw[dstp, :], b=kt[src, :]: e.tensor_copy(a, b),
                     "dv", nowait=(g < 3))
            p.wait("gpsimd", "dv", p.ctr["dv"])
            p.op("gpsimd", lambda e, t=kt, cc=ctab[:, csl]: e.tensor_mul(t, t, cc), "gp")
            p.op("gpsimd", lambda e, w=kw, ss=stab[:, csl]: e.tensor_mul(w, w, ss), "gp")
            p.wait("vector", "gp", p.ctr["gp"])
            p.op("vector", lambda e, d=dst, t=kt, w=kw: e.tensor_tensor(d, t, w, OP.add), "dv")
            return p.ctr["dv"]
        with contextlib.ExitStack() as ssc:
            wdkv_sb = p.sb("wdkv", [128, 8 * LAT], BF16, ssc)
            p.dma("gpsimd", wdkv_sb[:].rearrange("p (k f) -> p k f", f=LAT), in_wdkv, "ldg")
            w1rdy = p.ctr["ldg"]
            latf = [p.sb(f"latraw{i}", [128, T], F32, ssc) for i in range(2)]
            latloc = [p.sb(f"latloc{i}", [128, T], BF16, ssc) for i in range(2)]
            rb2 = p.sb("l1rb", [128, T], F32, ssc)
            mrb2 = p.sb("l1mrb", [128, T], F32, ssc)
            x2b = [p.sb(f"l1x2_{i}", [128, 512], BF16, ssc) for i in range(2)]
            RR, RMR = row1(c, 0), row1(c, 1)
            RA, RB_, RM = row1(c, 3), row1(c, 4), row1(c, 5)
            p.wait("tensor", "ldg", w1rdy)
            evwar = [0, 0]
            for ch in range(2):
                for n in range(NT):
                    sl = slice(n * 512, (n + 1) * 512)
                    ps = pf[n % 2]
                    p.wait("tensor", "dv", evwar[n % 2])
                    for k in range(NCH):
                        p.op("tensor", lambda e, ps=ps[:], w=wdkv_sb[:, k * LAT + ch * 128: k * LAT + ch * 128 + 128],
                             a=xh[k][:, sl], s=(k == 0), t=(k == NCH - 1):
                             e.matmul(ps, w, a, start=s, stop=t), "pe")
                    p.wait("vector", "pe", p.ctr["pe"])
                    p.op("vector", lambda e, ps=ps[:], d=latf[ch][:, sl]: e.tensor_copy(d, ps), "dv")
                    evwar[n % 2] = p.ctr["dv"]
            lat_done = p.ctr["dv"]
            for ch in range(2):
                p.op("vector", lambda e, a=latloc[ch][:], b=latf[ch][:]: e.tensor_copy(a, b), "dv")
            p.wait("gpsimd", "dv", p.ctr["dv"])
            war = [0]
            x2war = [0, 0]
            for n in range(NT):
                sl = slice(n * 512, (n + 1) * 512)
                psA = pf[2][0:1, :]
                psB = pf[3][0:1, :]
                x2t = []
                for ch in range(2):
                    p.wait("gpsimd", "pe", x2war[ch])
                    p.op("gpsimd", lambda e, b=x2b[ch][:], a=latloc[ch][:, sl]: e.tensor_mul(b, a, a), "gp")
                    x2t.append(p.ctr["gp"])
                p.wait("tensor", "gp", x2t[-1])
                p.wait("tensor", "dv", war[0])
                for ch in range(2):
                    p.op("tensor", lambda e, ps=psA, a=latloc[ch][:, sl], s=(ch == 0), t=(ch == 1):
                         e.matmul(ps, c["ones1_bf"][:], a, start=s, stop=t), "pe")
                    p.op("tensor", lambda e, ps=psB, b=x2b[ch][:], s=(ch == 0), t=(ch == 1):
                         e.matmul(ps, c["ones1_bf"][:], b, start=s, stop=t), "pe")
                x2war[0] = x2war[1] = p.ctr["pe"]
                p.wait("vector", "pe", p.ctr["pe"])
                p.op("vector", lambda e, ps=psA, d=RA[0:1, sl]: e.tensor_copy(d, ps), "dv")
                p.op("vector", lambda e, ps=psB, d=RB_[0:1, sl]: e.tensor_copy(d, ps), "dv")
                war[0] = p.ctr["dv"]
            p.op("vector", lambda e: e.tensor_scalar(RM, RA, 1.0 / LAT, None, OP.mult), "dv")
            p.op("vector", lambda e: e.tensor_scalar(RB_, RB_, 1.0 / LAT, None, OP.mult), "dv")
            p.op("vector", lambda e: e.tensor_tensor(RA, RM, RM, OP.mult), "dv")
            p.op("vector", lambda e: e.tensor_tensor(RB_, RB_, RA, OP.subtract), "dv")
            p.op("vector", lambda e: e.tensor_scalar(RB_, RB_, 1e-5, None, OP.add), "dv")
            p.wait("scalar", "dv", p.ctr["dv"])
            p.op("scalar", lambda e: e.sqrt(RB_, RB_), "sc")
            p.wait("vector", "sc", p.ctr["sc"])
            p.op("vector", lambda e: e.reciprocal(RA, RB_), "dv")
            p.op("vector", lambda e: e.tensor_tensor(RM, RM, RA, OP.mult), "dv")
            p.op("vector", lambda e: e.tensor_copy(RR, RA), "dv")
            p.op("vector", lambda e: e.tensor_copy(RMR, RM), "dv")
            rows_done = p.ctr["dv"]
            p.wait("tensor", "dv", rows_done)
            for n in range(NT):
                sl = slice(n * 512, (n + 1) * 512)
                ps = pf[4]
                p.op("tensor", lambda e, ps=ps[0:128, :], r=RR[0:1, sl]:
                     e.matmul(ps, c["onesr_f"][:], r, start=True, stop=True), "pe")
                p.wait("vector", "pe", p.ctr["pe"])
                p.op("vector", lambda e, ps=ps[0:128, :], d=rb2[:, sl]: e.tensor_copy(d, ps), "dv")
                p.wait("tensor", "dv", p.ctr["dv"])
                p.op("tensor", lambda e, ps=ps[0:128, :], r=RMR[0:1, sl]:
                     e.matmul(ps, c["onesr_f"][:], r, start=True, stop=True), "pe")
                p.wait("vector", "pe", p.ctr["pe"])
                p.op("vector", lambda e, ps=ps[0:128, :], d=mrb2[:, sl]: e.tensor_copy(d, ps), "dv")
                p.wait("tensor", "dv", p.ctr["dv"])
            bc_done = p.ctr["dv"]
            p.wait("vector", "lds", g_rdy)
            p.wait("gpsimd", "dv", bc_done)
            for ch in range(2):
                eng, sem = ("vector", "dv") if ch == 0 else ("gpsimd", "gp")
                tb = latf[ch]
                p.op(eng, lambda e, t=tb[:], r=rb2[:]: e.tensor_tensor(t, t, r, OP.mult), sem)
                p.op(eng, lambda e, t=tb[:], m=mrb2[:]: e.tensor_tensor(t, t, m, OP.subtract), sem)
                p.op(eng, lambda e, d=latloc[ch][:], t=tb[:], g=gbc[:, ch:ch + 1], b=bbc[:, ch:ch + 1]:
                     e.tensor_scalar(d, t, g, b, OP.mult, OP.add), sem)
            p.wait("gpsimd", "dv", p.ctr["dv"])
            for ch in range(2):
                p.dma("gpsimd", ag_in[ch * 128:(ch + 1) * 128, :], latloc[ch][:], "ldg")
            p.wait("gpsimd", "ldg", p.ctr["ldg"])
            cc_tgt = p.op("gpsimd", lambda e: e.collective_compute(
                "AllGather", OP.bypass, replica_groups=PAIRS,
                ins=[ag_in.opt()], outs=[ag_out.opt()]), "cc0", nowait=True)
            p.wait("gpsimd", "cc0", cc_tgt)
            for ch in range(2):
                p.dma("gpsimd", latful[ch][:, 0:1024], ag_out[ch * 128:(ch + 1) * 128, 0:1024], "ldg")
                p.dma("gpsimd", latful[ch][:, 3072:4096], ag_out[ch * 128:(ch + 1) * 128, 1024:2048], "ldg")
                p.dma("gpsimd", latful[ch][:, 1024:3072], ag_out[256 + ch * 128: 256 + (ch + 1) * 128, :], "ldg")
            lat_ready = p.ctr["ldg"]
            p.wait("tensor", "ldg", lat_ready)
            p.barrier()
            if os.environ.get("L1SUB", "") == "lat":
                dbg_dump(p, c, "dbg_lat0", latful[0][:], [128, S], BF16)
                dbg_dump(p, c, "dbg_lat1", latful[1][:], [128, S], BF16)
        with contextlib.ExitStack() as qsc:
            qc4 = p.sb("qc4", [128, T], BF16, qsc)
            qs4 = p.sb("qs4", [128, T], BF16, qsc)
            wq_s = [p.sb(f"wqs{i}", [128, 1024], BF16, qsc) for i in range(2)]
            p.dma("gpsimd", qc4[:], in_qc4, "ldg")
            p.dma("gpsimd", qs4[:], in_qs4, "ldg")
            qt_rdy = p.ctr["ldg"]

            def load_wq(hp):
                buf = wq_s[hp % 2]
                for k in range(NCH):
                    p.dma("gpsimd", buf[:, k * 128:(k + 1) * 128],
                          in_wq[k * 128:(k + 1) * 128, hp * 128:(hp + 1) * 128], "ldg")
                return p.ctr["ldg"]

            wq_rdy = [load_wq(0)]
            wq_war = [0, 0]
            qwar = [0, 0]
            p.wait("gpsimd", "ldg", qt_rdy)
            for hp in range(NCH):
                if hp + 1 < NCH:
                    p.wait("gpsimd", "pe", wq_war[(hp + 1) % 2])
                    wq_rdy.append(load_wq(hp + 1))
                p.wait("tensor", "ldg", wq_rdy[hp])
                for n in range(NT):
                    sl = slice(n * 512, (n + 1) * 512)
                    ps = pf[n % 2]
                    p.wait("tensor", "dv", qwar[n % 2])
                    for k in range(NCH):
                        p.op("tensor", lambda e, ps=ps[:], w=wq_s[hp % 2][:, k * 128:(k + 1) * 128],
                             a=xh[k][:, sl], s=(k == 0), t=(k == NCH - 1):
                             e.matmul(ps, w, a, start=s, stop=t), "pe")
                    p.wait("vector", "pe", p.ctr["pe"])
                    p.wait("gpsimd", "pe", p.ctr["pe"])
                    qwar[n % 2] = rope_evac(ps, q_all[hp][:, sl], qc4, qs4, n * 512, n % 2)
                wq_war[hp % 2] = p.ctr["pe"]
            p.barrier()
            if os.environ.get("L1SUB", "") == "q":
                for i in range(NCH):
                    dbg_dump(p, c, f"dbg_q{i}", q_all[i][:], [128, T], BF16)
        vtm = [p.sb(f"vtm{i}", [128, 130], BF16, sc) for i in range(32)]
        kbuf = p.sb("kbuf", [128, S], BF16, sc)
        k0c = p.sb("k0c", [128, 1], F32, sc)
        sh2 = p.sb("sh2", [2, 8], F32, sc)
        kmx = p.sb("kmx", [2, 32], F32, sc)
        biasb = p.sb("biasb", [128, 8], F32, sc)
        biasM = p.sb("biasM", [128, 2 * 88], F32, sc)
        linb = p.sb("linb", [128, 512], F32, sc)
        ptb = [p.sb(f"pt{i}", [128, 512], BF16, sc) for i in range(3)]
        NQ2 = c["rowpool"][0:2, 0:2048]
        D0 = c["rowpool"][0:2, 2048:4096]
        BROW = c["rowpool"][0:2, 4096:6144]
        l1sub = os.environ.get("L1SUB", "")
        hp_list = [] if l1sub in ("lat", "q") else ([0] if l1sub == "hp0" else list(range(NCH)))
        for hp in hp_list:
            p.wait("tensor", "dv", p.ctr["dv"])
            p.wait("tensor", "sc", p.ctr["sc"])
            vwar = [0, 0]
            for tc in range(32):
                ps = pf[tc % 2]
                p.wait("tensor", "dv", vwar[tc % 2])
                for ch in range(2):
                    p.op("tensor", lambda e, ps=ps[0:128, 0:128], a=latful[ch][:, tc * 128:(tc + 1) * 128],
                         w=wuv_sb[:, ch * D + hp * 128: ch * D + hp * 128 + 128],
                         s=(ch == 0), t=(ch == 1):
                         e.matmul(ps, a, w, start=s, stop=t), "pe")
                p.wait("vector", "pe", p.ctr["pe"])
                v3 = vtm[tc][:].rearrange("p (h w) -> p h w", w=65)
                p.op("vector", lambda e, d=v3[:, :, 64:65]: e.memset(d, 1.0), "dv")
                p.op("vector", lambda e, d=v3[:, :, 0:64],
                     ps=ps[0:128, 0:128].rearrange("p (h w) -> p h w", w=64):
                     e.tensor_copy(d, ps), "dv")
                vwar[tc % 2] = p.ctr["dv"]
            if l1sub == "hp0v":
                dbg_dump(p, c, "dbg_v0", vtm[0][:], [128, 130], BF16)
                dbg_dump(p, c, "dbg_v5", vtm[5][:], [128, 130], BF16)
                break
            kwar2 = [vwar[0], vwar[1]]
            for kt in range(8):
                ps = pf[kt % 2]
                p.wait("tensor", "dv", kwar2[kt % 2])
                p.wait("tensor", "gp", p.ctr["gp"])
                for ch in range(2):
                    p.op("tensor", lambda e, ps=ps[:],
                         w=wuk_sb[:, ch * D + hp * 128: ch * D + hp * 128 + 128],
                         a=latful[ch][:, kt * 512:(kt + 1) * 512],
                         s=(ch == 0), t=(ch == 1):
                         e.matmul(ps, w, a, start=s, stop=t), "pe")
                p.wait("vector", "pe", p.ctr["pe"])
                p.wait("gpsimd", "pe", p.ctr["pe"])
                kwar2[kt % 2] = rope_evac(ps, kbuf[:, kt * 512:(kt + 1) * 512], c4, s4, kt * 512, kt % 2)
                p.wait("gpsimd", "dv", kwar2[kt % 2])
                p.op("gpsimd", lambda e, b=q2b[kt % 2][:], a=kbuf[:, kt * 512:(kt + 1) * 512]:
                     e.tensor_mul(b, a, a), "gp")
                p.wait("tensor", "gp", p.ctr["gp"])
                p.op("tensor", lambda e, ps=pf[6][0:2, :], a=q2b[kt % 2][:]:
                     e.matmul(ps, c["sel2_bf"][:], a, start=True, stop=True), "pe")
                p.wait("vector", "pe", p.ctr["pe"])
                p.op("vector", lambda e, d=kmx[0:2, kt:kt + 1], ps=pf[6][0:2, :]:
                     e.reduce_max(d, ps, AX), "dv")
                p.wait("tensor", "dv", p.ctr["dv"])
            for n in range(NT):
                sl = slice(n * 512, (n + 1) * 512)
                p.wait("gpsimd", "pe", p.ctr["pe"])
                p.op("gpsimd", lambda e, b=q2b[0][:], a=q_all[hp][:, sl]: e.tensor_mul(b, a, a), "gp")
                p.wait("tensor", "gp", p.ctr["gp"])
                p.op("tensor", lambda e, ps=pf[6][0:2, :], a=q2b[0][:]:
                     e.matmul(ps, c["sel2_bf"][:], a, start=True, stop=True), "pe")
                p.wait("vector", "pe", p.ctr["pe"])
                p.op("vector", lambda e, d=NQ2[:, sl], ps=pf[6][0:2, :]: e.tensor_copy(d, ps), "dv")
                p.wait("tensor", "dv", p.ctr["dv"])
                if n == 0:
                    p.op("vector", lambda e: e.tensor_copy(k0c[:], kbuf[:, 0:1]), "dv")
                p.op("vector", lambda e, b=q2b[1][:], a=q_all[hp][:, sl]:
                     e.tensor_scalar(b, a, k0c[:], None, OP.mult), "dv")
                p.wait("tensor", "dv", p.ctr["dv"])
                p.op("tensor", lambda e, ps=pf[6][0:2, :], a=q2b[1][:]:
                     e.matmul(ps, c["sel2_bf"][:], a, start=True, stop=True), "pe")
                p.wait("vector", "pe", p.ctr["pe"])
                p.op("vector", lambda e, d=D0[:, sl], ps=pf[6][0:2, :]: e.tensor_copy(d, ps), "dv")
                p.wait("tensor", "dv", p.ctr["dv"])
            p.op("vector", lambda e: e.reduce_max(kmx[0:2, 31:32], kmx[0:2, 0:8], AX), "dv")
            p.wait("scalar", "dv", p.ctr["dv"])
            p.op("scalar", lambda e: e.sqrt(kmx[0:2, 31:32], kmx[0:2, 31:32]), "sc")
            p.op("scalar", lambda e: e.sqrt(NQ2, NQ2), "sc")
            p.wait("vector", "sc", p.ctr["sc"])
            p.op("vector", lambda e: e.tensor_scalar(D0, D0, 0.125, None, OP.mult), "dv")
            p.op("vector", lambda e: e.tensor_scalar(BROW, NQ2, kmx[0:2, 31:32], 0.125, OP.mult, OP.mult), "dv")
            p.op("vector", lambda e: e.tensor_scalar(BROW, BROW, -80.0, None, OP.add), "dv")
            p.op("vector", lambda e: e.tensor_tensor(BROW, BROW, D0, OP.max), "dv")
            for qt in range(NT):
                p.op("vector", lambda e, d=sh2[0:2, qt:qt + 1], a=BROW[:, qt * 512:(qt + 1) * 512]:
                     e.reduce_max(d, a, AX), "dv")
            p.op("vector", lambda e: e.tensor_scalar(sh2[0:2, 0:4], sh2[0:2, 0:4], -1.0, None, OP.mult), "dv")
            nsh_done = p.ctr["dv"]
            p.wait("tensor", "dv", nsh_done)
            p.wait("vector", "sc", p.ctr["sc"])
            for h in range(2):
                for qt in range(NT):
                    p.op("tensor", lambda e, ps=pf[6][0:128, 0:1],
                         l=c["h01_f"][:, h * 128:(h + 1) * 128], r=sh2[0:2, qt:qt + 1]:
                         e.matmul(ps, l, r, start=True, stop=True), "pe")
                    p.wait("vector", "pe", p.ctr["pe"])
                    p.op("vector", lambda e, d=biasb[:, h * 4 + qt: h * 4 + qt + 1], ps=pf[6][0:128, 0:1]:
                         e.tensor_copy(d, ps), "dv")
                    p.wait("tensor", "dv", p.ctr["dv"])
            for h in range(2):
                for qt in range(NT):
                    nk = NKC[qt]
                    p.op("vector", lambda e, d=biasM[:, h * 88 + ZOF[qt]: h * 88 + ZOF[qt] + nk],
                         a=zvb[:, ZOF[qt]:ZOF[qt] + nk],
                         b=biasb[:, h * 4 + qt: h * 4 + qt + 1].broadcast_to((128, nk)):
                         e.tensor_tensor(d, a, b, OP.add), "dv")
            bias_done = p.ctr["dv"]
            if l1sub == "hp0k":
                dbg_dump(p, c, "dbg_k", kbuf[:], [128, S], BF16)
                dbg_dump(p, c, "dbg_brow", BROW, [2, 2048], F32)
                break
            p.wait("scalar", "dv", bias_done)
            sc_war = [0, 0, 0]
            pt_war = [0, 0, 0]
            av_war = [0, 0]
            for qt in range(NT):
                nkc = NKC[qt]
                umap = UMAP[qt]
                for h in range(2):
                    avps = pf[4 + h]
                    p.wait("tensor", "dv", av_war[h])
                    pend = []

                    def emit_av(kc, j, rsem, rtgt, last, avps=avps, h=h):
                        p.wait("tensor", rsem, rtgt)
                        p.op("tensor", lambda e, ps=avps[0:65, 0:512],
                             l=vtm[kc][:, h * 65: h * 65 + 65], r=ptb[j][:],
                             s=(kc == 0), t=last:
                             e.matmul(ps, l, r, start=s, stop=t), "pe")
                        pt_war[j] = p.ctr["pe"]
                    for kc in range(nkc):
                        j = kc % 3
                        sps = pf[j]
                        pt = ptb[j][:]
                        p.wait("tensor", "sc", sc_war[j])
                        p.op("tensor", lambda e, ps=sps[:], l=kbuf[h * 64:(h + 1) * 64, kc * 128:(kc + 1) * 128],
                             r=q_all[hp][h * 64:(h + 1) * 64, qt * 512:(qt + 1) * 512], tp=(h * 64, 0):
                             e.matmul(ps, l, r, start=True, stop=True, tile_position=tp), "pe")
                        mmt = p.ctr["pe"]
                        if len(pend) >= 2:
                            emit_av(*pend.pop(0), False)
                        p.wait("scalar", "pe", mmt)
                        p.wait("scalar", "pe", pt_war[j])
                        p.op("scalar", lambda e, d=pt[:], ps=sps[:],
                             b=biasM[:, h * 88 + ZOF[qt] + kc: h * 88 + ZOF[qt] + kc + 1]:
                             e.activation(d, ps, AF.Exp, bias=b, scale=0.125), "sc")
                        sc_war[j] = p.ctr["sc"]
                        if kc in umap:
                            u = umap.index(kc)
                            moff = ((qt // 2) * 8 + u) * 512
                            p.wait("vector", "sc", p.ctr["sc"])
                            p.op("vector", lambda e, d=pt[:], m=masks[:, moff:moff + 512]:
                                 e.tensor_tensor(d, d, m, OP.mult), "dv")
                            pend.append((kc, j, "dv", p.ctr["dv"]))
                        else:
                            pend.append((kc, j, "sc", p.ctr["sc"]))
                    while pend:
                        emit_av(*pend.pop(0), len(pend) == 0)
                    avt = p.ctr["pe"]
                    p.wait("vector", "pe", avt)
                    lb = c["rowpool"][0:1, h * 2048 + qt * 512: h * 2048 + (qt + 1) * 512]
                    p.op("vector", lambda e, d=lb, ps=avps[64:65, 0:512]: e.tensor_copy(d, ps),
                         "dv", nowait=True)
                    p.op("vector", lambda e, d=xh[hp][h * 64:(h + 1) * 64, qt * 512:(qt + 1) * 512],
                         ps=avps[0:64, 0:512]: e.tensor_copy(d, ps), "dv")
                    av_war[h] = p.ctr["dv"]
                    p.wait("tensor", "dv", p.ctr["dv"])
            p.wait("tensor", "dv", p.ctr["dv"])
            for qt in range(NT):
                p.op("tensor", lambda e, r=c["rowpool"][0:1, qt * 512:(qt + 1) * 512]:
                     e.matmul(pf[6][0:64, :], c["onesr_f"][0:1, 0:64], r, start=True, stop=True), "pe")
                p.op("tensor", lambda e, r=c["rowpool"][0:1, 2048 + qt * 512: 2048 + (qt + 1) * 512]:
                     e.matmul(pf[6][64:128, :], c["onesr_f"][0:1, 0:64], r, start=True, stop=True), "pe")
                p.wait("vector", "pe", p.ctr["pe"])
                p.op("vector", lambda e: e.reciprocal(linb[:], pf[6][0:128, :]), "dv")
                p.op("vector", lambda e, d=xh[hp][:, qt * 512:(qt + 1) * 512]:
                     e.tensor_tensor(d, d, linb[:], OP.mult), "dv")
                p.wait("tensor", "dv", p.ctr["dv"])
            if l1sub == "hp0":
                dbg_dump(p, c, "dbg_o0", xh[0][:], [128, T], BF16)
        if l1sub == "oall":
            for i in range(NCH):
                dbg_dump(p, c, f"dbg_o{i}", xh[i][:], [128, T], BF16)
        p.barrier()
    xt2 = [p.sb(f"x2_{k}", [128, T], F32) for k in range(NCH)]
    c["x"] = [t[:] for t in xt2]
    with contextlib.ExitStack() as sc2:
        wo_s = [p.sb(f"wos{i}", [128, 1024], BF16, sc2) for i in range(2)]
        tmpE = [p.sb(f"l1te{i}", [128, 512], F32, sc2) for i in range(2)]
        for k in range(NCH):
            p.dma("sync", c["x"][k], x_reload[k * 128:(k + 1) * 128, :], "lds")
        x_rdy = p.ctr["lds"]
        p.wait("gpsimd", "lds", x_rdy)

        def load_wo(oc):
            buf = wo_s[oc % 2]
            for k in range(NCH):
                p.dma("gpsimd", buf[:, k * 128:(k + 1) * 128],
                      in_wo[k * 128:(k + 1) * 128, oc * 128:(oc + 1) * 128], "ldg")
            return p.ctr["ldg"]
        wo_rdy = [load_wo(0)]
        wo_war = [0, 0]
        nwar = [0, 0]
        gwar = [0, 0]
        for oc in ([] if os.environ.get("WO_XONLY") else range(NCH)):
            if oc + 1 < NCH:
                p.wait("gpsimd", "pe", wo_war[(oc + 1) % 2])
                wo_rdy.append(load_wo(oc + 1))
            p.wait("tensor", "ldg", wo_rdy[oc])
            for n in range(NT):
                sl = slice(n * 512, (n + 1) * 512)
                ps = pf[n % 2]
                p.wait("tensor", "dv", nwar[n % 2])
                for k in range(NCH):
                    p.op("tensor", lambda e, ps=ps[:], w=wo_s[oc % 2][:, k * 128:(k + 1) * 128],
                         a=xh[k][:, sl], s=(k == 0), t=(k == NCH - 1):
                         e.matmul(ps, w, a, start=s, stop=t), "pe")
                omm = p.ctr["pe"]
                p.wait("vector", "pe", omm)
                p.wait("vector", "gp", gwar[n % 2])
                p.op("vector", lambda e, ps=ps[:], t=tmpE[n % 2][:]: e.tensor_copy(t, ps), "dv")
                nwar[n % 2] = p.ctr["dv"]
                p.wait("gpsimd", "dv", p.ctr["dv"])
                xoc = c["x"][oc]
                if os.environ.get("WO_NORES"):
                    p.op("gpsimd", lambda e, d=xoc[:, sl], t=tmpE[n % 2][:]: e.tensor_copy(d, t), "gp")
                else:
                    p.op("gpsimd", lambda e, d=xoc[:, sl], t=tmpE[n % 2][:]: e.tensor_add(d, d, t), "gp")
                gwar[n % 2] = p.ctr["gp"]
            wo_war[oc % 2] = p.ctr["pe"]
        p.wait("tensor", "gp", p.ctr["gp"])
        p.wait("vector", "gp", p.ctr["gp"])
        p.wait("sync", "gp", p.ctr["gp"])



def emit_l1_moe(p, c):
    pf, xh = c["pf"], c["xhat"]
    in_wr = p.inp("moe_wr", [D, EXP], F32).rearrange("(k p) f -> p k f", p=128)
    in_eb = p.inp("moe_ebias", [128, EXP], F32)
    in_wg = p.inp("moe_wg", [EXP, D, FMOE], F32)
    in_wu = p.inp("moe_wu", [EXP, D, FMOE], F32)
    in_wd = p.inp("moe_wd", [EXP, FMOE, D], F32)
    in_wgs = p.inp("moe_wgs", [D, FMOE], F32).rearrange("(k p) f -> p k f", p=128)
    in_wus = p.inp("moe_wus", [D, FMOE], F32).rearrange("(k p) f -> p k f", p=128)
    in_wds = p.inp("moe_wds", [FMOE, D], F32).rearrange("(k p) f -> p k f", p=128)
    with contextlib.ExitStack() as sc:
        wr_sb = p.sb("wr", [128, 64], BF16, sc)
        eb_sb = p.sb("eb", [128, 8], F32, sc)
        lfm = p.sb("lfm", [8, T], BF16, sc)
        ltm = p.sb("ltm", [128, 128], F32, sc)
        sctm = p.sb("sctm", [128, 128], F32, sc)
        val = p.sb("val", [128, 128], F32, sc)
        mA = p.sb("mA", [128, 128], F32, sc)
        mB = p.sb("mB", [128, 128], F32, sc)
        cs = p.sb("cs", [128, 128], F32, sc)
        m12 = p.sb("m12", [128, 32], F32, sc)
        ninf = p.sb("ninf", [128, 128], F32, sc)
        g_fm = p.sb("gfm", [8, T], F32, sc)
        wg_sb = p.sb("wg", [128, NCH * FMOE], BF16, sc)
        wu_sb = p.sb("wu", [128, NCH * FMOE], BF16, sc)
        wd_sb = p.sb("wd", [128, FCH * D], BF16, sc)
        gate_sb = [p.sb(f"gate{i}", [128, 512], BF16, sc) for i in range(FCH)]
        h_sb = [p.sb(f"hs{i}", [128, 512], BF16, sc) for i in range(FCH)]
        Gb = [p.sb("Gb0", [128, T], F32, sc)]
        tmpE = [p.sb(f"m_te{i}", [128, 512], F32, sc) for i in range(2)]
        p.dma("gpsimd", wr_sb[:].rearrange("p (k f) -> p k f", f=8), in_wr, "ldg")
        p.dma("sync", eb_sb[:], in_eb, "lds")
        r_rdy = p.ctr["ldg"]
        e_rdy = p.ctr["lds"]
        p.wait("tensor", "ldg", r_rdy)
        for n in range(NT):
            sl = slice(n * 512, (n + 1) * 512)
            ps = pf[0][0:8, :]
            for k in range(NCH):
                p.op("tensor", lambda e, ps=ps, w=wr_sb[:, k * 8:(k + 1) * 8], a=xh[k][:, sl],
                     s=(k == 0), t=(k == NCH - 1): e.matmul(ps, w, a, start=s, stop=t), "pe")
            p.wait("vector", "pe", p.ctr["pe"])
            p.op("vector", lambda e, d=lfm[0:8, sl], ps=ps: e.tensor_copy(d, ps), "dv")
            p.wait("tensor", "dv", p.ctr["dv"])
        for tc in range(16):
            p.op("tensor", lambda e, ps=c["pb16"][0:128, tc * 8:(tc + 1) * 8], a=lfm[0:8, tc * 128:(tc + 1) * 128]:
                 e.transpose(ps, a, c["id_bf"][0:8, 0:8]), "pe")
        p.wait("vector", "pe", p.ctr["pe"])
        p.op("vector", lambda e: e.tensor_copy(ltm[:], c["pb16"][0:128, 0:128]), "dv")
        p.wait("tensor", "dv", p.ctr["dv"])
        p.wait("scalar", "dv", p.ctr["dv"])
        p.op("scalar", lambda e: e.activation(sctm[:], ltm[:], AF.Sigmoid), "sc")
        p.wait("vector", "sc", p.ctr["sc"])
        p.wait("vector", "lds", e_rdy)
        v3 = val[:].rearrange("p (a b) -> p a b", b=8)
        s3 = sctm[:].rearrange("p (a b) -> p a b", b=8)
        p.op("vector", lambda e: e.tensor_tensor(
            v3, s3, eb_sb[:].unsqueeze(1).broadcast_to((128, 16, 8)), OP.add), "dv")
        p.op("vector", lambda e: e.memset(ninf[:], -1e30), "dv")
        p.op("vector", lambda e: e.tensor_reduce(
            m12[:, 0:16].rearrange("p (a o) -> p a o", o=1), v3, AX, OP.max), "dv")
        p.op("vector", lambda e: e.tensor_tensor(
            mA[:].rearrange("p (a b) -> p a b", b=8), v3,
            m12[:, 0:16].rearrange("p (a o) -> p a o", o=1).broadcast_to((128, 16, 8)), OP.is_ge), "dv")
        for tc in range(16):
            p.op("vector", lambda e, d=cs[:, tc * 8:(tc + 1) * 8], a=mA[:, tc * 8:(tc + 1) * 8]:
                 e.tensor_tensor_scan(d, a, a, 0.0, OP.add, OP.bypass), "dv", nowait=(tc < 15))
        p.op("vector", lambda e: e.tensor_scalar(cs[:], cs[:], 1.0, None, OP.is_equal), "dv")
        p.op("vector", lambda e: e.tensor_tensor(mA[:], mA[:], cs[:], OP.logical_and), "dv")
        p.op("vector", lambda e: e.tensor_scalar(cs[:], mA[:], 1e30, None, OP.mult), "dv")
        p.op("vector", lambda e: e.tensor_tensor(val[:], val[:], cs[:], OP.subtract), "dv")
        p.op("vector", lambda e: e.tensor_reduce(
            m12[:, 16:32].rearrange("p (a o) -> p a o", o=1), v3, AX, OP.max), "dv")
        p.op("vector", lambda e: e.tensor_tensor(
            mB[:].rearrange("p (a b) -> p a b", b=8), v3,
            m12[:, 16:32].rearrange("p (a o) -> p a o", o=1).broadcast_to((128, 16, 8)), OP.is_ge), "dv")
        for tc in range(16):
            p.op("vector", lambda e, d=cs[:, tc * 8:(tc + 1) * 8], a=mB[:, tc * 8:(tc + 1) * 8]:
                 e.tensor_tensor_scan(d, a, a, 0.0, OP.add, OP.bypass), "dv", nowait=(tc < 15))
        p.op("vector", lambda e: e.tensor_scalar(cs[:], cs[:], 1.0, None, OP.is_equal), "dv")
        p.op("vector", lambda e: e.tensor_tensor(mB[:], mB[:], cs[:], OP.logical_and), "dv")
        p.op("vector", lambda e: e.tensor_tensor(mA[:], mA[:], mB[:], OP.add), "dv")
        p.op("vector", lambda e: e.tensor_tensor(val[:], sctm[:], mA[:], OP.mult), "dv")
        p.op("vector", lambda e: e.tensor_reduce(
            m12[:, 0:16].rearrange("p (a o) -> p a o", o=1), v3, AX, OP.add), "dv")
        p.op("vector", lambda e: e.tensor_scalar(m12[:, 0:16], m12[:, 0:16], 1e-20, None, OP.add), "dv")
        p.op("vector", lambda e: e.reciprocal(m12[:, 0:16], m12[:, 0:16]), "dv")
        p.op("vector", lambda e: e.tensor_tensor(
            v3, v3, m12[:, 0:16].rearrange("p (a o) -> p a o", o=1).broadcast_to((128, 16, 8)), OP.mult), "dv")
        g_done = p.ctr["dv"]
        p.wait("tensor", "dv", g_done)
        for tc in range(16):
            p.op("tensor", lambda e, ps=pf[2][0:8, 0:128], a=val[:, tc * 8:(tc + 1) * 8]:
                 e.transpose(ps, a, c["id_f"][:]), "pe")
            p.wait("vector", "pe", p.ctr["pe"])
            p.op("vector", lambda e, d=g_fm[0:8, tc * 128:(tc + 1) * 128], ps=pf[2][0:8, 0:128]:
                 e.tensor_copy(d, ps), "dv")
            p.wait("tensor", "dv", p.ctr["dv"])
        p.barrier()
        def load_expert(e_idx):
            if e_idx == 0:
                p.dma("gpsimd", wg_sb[:].rearrange("p (k f) -> p k f", f=FMOE), in_wgs, "ldg")
                p.dma("gpsimd", wu_sb[:].rearrange("p (k f) -> p k f", f=FMOE), in_wus, "ldg")
                p.dma("gpsimd", wd_sb[:].rearrange("p (k f) -> p k f", f=D), in_wds, "ldg")
            else:
                ee = e_idx - 1
                p.dma("gpsimd", wg_sb[:].rearrange("p (k f) -> p k f", f=FMOE), in_wg[ee].rearrange("(k p) f -> p k f", p=128), "ldg")
                p.dma("gpsimd", wu_sb[:].rearrange("p (k f) -> p k f", f=FMOE), in_wu[ee].rearrange("(k p) f -> p k f", p=128), "ldg")
                p.dma("gpsimd", wd_sb[:].rearrange("p (k f) -> p k f", f=D), in_wd[ee].rearrange("(k p) f -> p k f", p=128), "ldg")
            return p.ctr["ldg"]

        w_rdy = load_expert(0)
        for eix in range(9):
            routed = eix > 0
            Gt = Gb[0]
            if routed:
                ee = eix - 1
                p.wait("tensor", "dv", p.ctr["dv"])
                for n in range(NT):
                    sl = slice(n * 512, (n + 1) * 512)
                    p.op("tensor", lambda e, ps=pf[6][0:128, :], l=c["esel_f"][:, ee * 128:(ee + 1) * 128],
                         r=g_fm[0:8, sl]: e.matmul(ps, l, r, start=True, stop=True), "pe")
                    p.wait("vector", "pe", p.ctr["pe"])
                    p.op("vector", lambda e, d=Gt[:, sl], ps=pf[6][0:128, :]: e.tensor_copy(d, ps), "dv")
                    p.wait("tensor", "dv", p.ctr["dv"])
            p.wait("tensor", "ldg", w_rdy)
            for n in range(NT):
                sl = slice(n * 512, (n + 1) * 512)
                for oc in range(FCH):
                    psA = pf[oc % 2]
                    psB = pf[2 + oc % 2]
                    p.wait("tensor", "sc", c.get("moe_scwar", 0) if oc < 2 and n == 0 and eix == 0 else c.get(f"moe_g{oc%2}", 0))
                    p.wait("tensor", "dv", c.get(f"moe_h{oc%2}", 0))
                    for k in range(NCH):
                        p.op("tensor", lambda e, ps=psA[:], w=wg_sb[:, k * FMOE + oc * 128: k * FMOE + oc * 128 + 128],
                             a=xh[k][:, sl], s=(k == 0), t=(k == NCH - 1):
                             e.matmul(ps, w, a, start=s, stop=t), "pe")
                    gm = p.ctr["pe"]
                    for k in range(NCH):
                        p.op("tensor", lambda e, ps=psB[:], w=wu_sb[:, k * FMOE + oc * 128: k * FMOE + oc * 128 + 128],
                             a=xh[k][:, sl], s=(k == 0), t=(k == NCH - 1):
                             e.matmul(ps, w, a, start=s, stop=t), "pe")
                    um = p.ctr["pe"]
                    p.wait("scalar", "pe", gm)
                    p.op("scalar", lambda e, d=gate_sb[oc][:], ps=psA[:]:
                         e.activation(d, ps, AF.Silu), "sc")
                    c[f"moe_g{oc%2}"] = p.ctr["sc"]
                    p.wait("vector", "pe", um)
                    p.wait("vector", "sc", p.ctr["sc"])
                    p.op("vector", lambda e, d=h_sb[oc][:], ps=psB[:], g=gate_sb[oc][:]:
                         e.scalar_tensor_tensor(d, ps, 1.0, g, OP.mult, OP.mult), "dv")
                    c[f"moe_h{oc%2}"] = p.ctr["dv"]
                if n == NT - 1:
                    c["moe_last_gu"] = p.ctr["pe"]
                h_done = p.ctr["dv"]
                p.wait("tensor", "dv", h_done)
                if n == 0:
                    p.wait("tensor", "ldg", c.get("moe_wd_rdy", 0))
                for oc2 in range(NCH):
                    psC = pf[4 + oc2 % 2]
                    p.wait("tensor", "dv", c.get(f"moe_d{oc2%2}", 0))
                    p.wait("tensor", "gp", c.get(f"moe_dg{oc2%2}", 0))
                    for kf in range(FCH):
                        p.op("tensor", lambda e, ps=psC[:], w=wd_sb[:, kf * D + oc2 * 128: kf * D + oc2 * 128 + 128],
                             a=h_sb[kf][:], s=(kf == 0), t=(kf == FCH - 1):
                             e.matmul(ps, w, a, start=s, stop=t), "pe")
                    dm = p.ctr["pe"]
                    p.wait("vector", "pe", dm)
                    xoc = c["x"][oc2]
                    if routed:
                        p.wait("vector", "gp", c.get(f"moe_dg{oc2%2}", 0))
                        p.op("vector", lambda e, t=tmpE[oc2 % 2][:], ps=psC[:], g=Gt[:, sl]:
                             e.tensor_tensor(t, ps, g, OP.mult), "dv")
                        c[f"moe_d{oc2%2}"] = p.ctr["dv"]
                        p.wait("gpsimd", "dv", p.ctr["dv"])
                        p.op("gpsimd", lambda e, d=xoc[:, sl], t=tmpE[oc2 % 2][:]:
                             e.tensor_add(d, d, t), "gp")
                        c[f"moe_dg{oc2%2}"] = p.ctr["gp"]
                    else:
                        p.op("vector", lambda e, d=xoc[:, sl], ps=psC[:]:
                             e.tensor_tensor(d, d, ps, OP.add), "dv")
                        c[f"moe_d{oc2%2}"] = p.ctr["dv"]
            if eix + 1 < 9:
                p.wait("gpsimd", "pe", c["moe_last_gu"])
                ee2 = eix
                p.dma("gpsimd", wg_sb[:].rearrange("p (k f) -> p k f", f=FMOE),
                      in_wg[ee2].rearrange("(k p) f -> p k f", p=128), "ldg")
                p.dma("gpsimd", wu_sb[:].rearrange("p (k f) -> p k f", f=FMOE),
                      in_wu[ee2].rearrange("(k p) f -> p k f", p=128), "ldg")
                w_rdy = p.ctr["ldg"]
                p.wait("gpsimd", "pe", p.ctr["pe"])
                p.dma("gpsimd", wd_sb[:].rearrange("p (k f) -> p k f", f=D),
                      in_wd[ee2].rearrange("(k p) f -> p k f", p=128), "ldg")
                c["moe_wd_rdy"] = p.ctr["ldg"]
        p.wait("tensor", "gp", p.ctr["gp"])
        p.wait("vector", "gp", p.ctr["gp"])
        p.wait("sync", "gp", p.ctr["gp"])


def build_l1(core, upto=4):
    p = Prog()
    for s in ["lds", "ldg", "lda", "pe", "dv", "sc", "gp", "st", "cc0", "bar"]:
        p.sem(s)
    c = make_common(p)
    x_scope = contextlib.ExitStack()
    xt = [p.sb(f"x{k}", [128, T], F32, x_scope) for k in range(NCH)]
    c["x"] = [t[:] for t in xt]
    x_in = p.inp("x_in", [D, T], F32)
    x_out = p.outp("x_out", [D, T], F32)
    emit_load_consts(p, c)
    for k in range(NCH):
        p.dma("sync", c["x"][k], x_in[k * 128:(k + 1) * 128, :], "lds")
    for eng in ENGS:
        p.wait(eng, "lds", p.ctr["lds"])
    dump_xhat = False
    if upto >= 1:
        emit_ln(p, c)
        p.barrier()
        dump_xhat = (upto == 1)
    if upto >= 2:
        emit_l1_attn(p, c, core, x_scope, x_in)
        p.barrier()
    if upto >= 3:
        emit_ln(p, c)
        p.barrier()
        dump_xhat = (upto == 3)
    if upto >= 4:
        emit_l1_moe(p, c)
        p.barrier()
    if dump_xhat:
        for k in range(NCH):
            p.dma("gpsimd", x_out[k * 128:(k + 1) * 128, :], c["xhat"][k][:], "st")
    else:
        for k in range(NCH):
            p.dma("sync", x_out[k * 128:(k + 1) * 128, :], c["x"][k], "st")
    p.wait("sync", "st", p.ctr["st"])
    p.wait("gpsimd", "st", p.ctr["st"])
    if upto < 2:
        x_scope.close()
    return p.finalize()


def build_fused():
    p = Prog()
    for sm in ["lds", "ldg", "lda", "pe", "dv", "sc", "gp", "st", "sp", "cc0", "bar"]:
        p.sem(sm)
    c = make_common(p)
    x_scope = contextlib.ExitStack()
    xt = [p.sb(f"x{k}", [128, T], F32, x_scope) for k in range(NCH)]
    c["x"] = [t[:] for t in xt]
    x_in = p.inp("x_in", [D, T], F32)
    x_out = p.outp("x_out", [D, T], F32)
    x_spill = p.idram("x_spill", [D, T], F32)
    emit_load_consts(p, c)
    for k in range(NCH):
        p.dma("sync", c["x"][k], x_in[k * 128:(k + 1) * 128, :], "lds")
    for eng in ENGS:
        p.wait(eng, "lds", p.ctr["lds"])
    emit_ln(p, c)
    p.barrier()
    emit_l0_attn(p, c)
    p.barrier()
    emit_ln(p, c)
    p.barrier()
    emit_l0_ffn(p, c)
    p.barrier()
    emit_ln(p, c)
    p.barrier()
    for k in range(NCH):
        p.dma("sync", x_spill[k * 128:(k + 1) * 128, :], c["x"][k], "sp")
    p.wait("sync", "sp", p.ctr["sp"])
    p.barrier()
    emit_l1_attn(p, c, 0, x_scope, x_spill)
    p.barrier()
    emit_ln(p, c)
    p.barrier()
    emit_l1_moe(p, c)
    p.barrier()
    for k in range(NCH):
        p.dma("sync", x_out[k * 128:(k + 1) * 128, :], c["x"][k], "st")
    p.wait("sync", "st", p.ctr["st"])
    return p.finalize()


def rope_tables():
    inv = 10000.0 ** (-np.arange(32) / 32.0)
    ang = np.arange(S)[None, :] * inv[:, None]
    cs, sn = np.cos(ang), np.sin(ang)
    C4 = np.concatenate([cs, cs, cs, cs], axis=0)
    S4 = np.concatenate([-sn, sn, -sn, sn], axis=0)
    return C4.astype(BF), S4.astype(BF)


def prep_core_l1(inputs, core, x_mid):
    d = dict(host_consts())
    d["x_in"] = x_mid
    d["wq"] = np.ascontiguousarray(inputs["L1_attn_W_q"])
    d["wdkv"] = np.ascontiguousarray(inputs["L1_attn_W_dkv"])
    d["wuk"] = np.ascontiguousarray(inputs["L1_attn_W_uk"])
    d["wuv"] = np.ascontiguousarray(inputs["L1_attn_W_uv"])
    d["wo"] = np.ascontiguousarray(inputs["L1_attn_W_o"])
    d["l1_gam"] = np.ascontiguousarray(np.asarray(inputs["L1_attn_ln_gamma"]).reshape(2, 128).T)
    d["l1_bet"] = np.ascontiguousarray(np.asarray(inputs["L1_attn_ln_beta"]).reshape(2, 128).T)
    C4, S4 = rope_tables()
    d["ropeC"], d["ropeS"] = C4, S4
    sg = segs(core)
    d["qropeC"] = np.ascontiguousarray(np.concatenate([C4[:, a:a + l] for (a, l) in sg], axis=1))
    d["qropeS"] = np.ascontiguousarray(np.concatenate([S4[:, a:a + l] for (a, l) in sg], axis=1))
    Q0s = [0, 512, 3072, 3584] if core % 2 == 0 else [1024, 1536, 2048, 2560]
    NKC = [12, 16, 28, 32]
    UMAP = {0: [0, 1, 2, 3, 8, 9, 10, 11], 1: [4, 5, 6, 7, 12, 13, 14, 15],
            2: [16, 17, 18, 19, 24, 25, 26, 27], 3: [20, 21, 22, 23, 28, 29, 30, 31]}
    ZOF = [0, 12, 28, 56]
    zv = np.zeros((88,), np.float32)
    for qt in range(4):
        for kc in range(NKC[qt]):
            if kc * 128 - Q0s[qt] >= 512:
                zv[ZOF[qt] + kc] = -1e30
    d["zvb"] = np.ascontiguousarray(np.broadcast_to(zv[None, :], (128, 88)).copy())
    masks = np.zeros((128, 2 * 8 * 512), np.float32)
    kk = np.arange(128)[:, None]
    jj = np.arange(512)[None, :]
    for g in range(2):
        qt = 2 * g
        for u in range(8):
            kc = UMAP[qt][u]
            m = ((kc * 128 + kk) <= (Q0s[qt] + jj)).astype(np.float32)
            masks[:, (g * 8 + u) * 512:(g * 8 + u + 1) * 512] = m
    d["cmasks"] = masks.astype(BF)
    d["moe_wr"] = np.ascontiguousarray(inputs["L1_ffn_W_gate_router"])
    d["moe_ebias"] = np.ascontiguousarray(
        np.broadcast_to(np.asarray(inputs["L1_ffn_expert_bias"])[None, :], (128, 8)).copy())
    d["moe_wg"] = np.ascontiguousarray(inputs["L1_ffn_Wg"])
    d["moe_wu"] = np.ascontiguousarray(inputs["L1_ffn_Wu"])
    d["moe_wd"] = np.ascontiguousarray(inputs["L1_ffn_Wd"])
    d["moe_wgs"] = np.ascontiguousarray(inputs["L1_ffn_Wg_s"])
    d["moe_wus"] = np.ascontiguousarray(inputs["L1_ffn_Wu_s"])
    d["moe_wds"] = np.ascontiguousarray(inputs["L1_ffn_Wd_s"])
    return d



def build_l0(core, upto=4):
    p = Prog()
    for s in ["lds", "ldg", "lda", "pe", "dv", "sc", "gp", "st", "cc0", "bar"]:
        p.sem(s)
    c = make_common(p)
    x_scope = contextlib.ExitStack()
    xt = [p.sb(f"x{k}", [128, T], F32, x_scope) for k in range(NCH)]
    c["x"] = [t[:] for t in xt]
    x_in = p.inp("x_in", [D, T], F32)
    x_out = p.outp("x_out", [D, T], F32)
    emit_load_consts(p, c)
    for k in range(NCH):
        p.dma("sync", c["x"][k], x_in[k * 128:(k + 1) * 128, :], "lds")
    for eng in ENGS:
        p.wait(eng, "lds", p.ctr["lds"])
    dump_xhat = False
    if upto >= 1:
        emit_ln(p, c)
        p.barrier()
        dump_xhat = (upto == 1)
    if upto >= 2:
        emit_l0_attn(p, c)
        p.barrier()
    if upto >= 3:
        emit_ln(p, c)
        p.barrier()
        dump_xhat = (upto == 3)
    if upto >= 4:
        emit_l0_ffn(p, c)
        p.barrier()
    if dump_xhat:
        for k in range(NCH):
            p.dma("gpsimd", x_out[k * 128:(k + 1) * 128, :], c["xhat"][k][:], "st")
    else:
        for k in range(NCH):
            p.dma("sync", x_out[k * 128:(k + 1) * 128, :], c["x"][k], "st")
    p.wait("sync", "st", p.ctr["st"])
    p.wait("gpsimd", "st", p.ctr["st"])
    x_scope.close()
    return p.finalize()


def prep_core_l0(inputs, core):
    b = core // 2
    x = np.asarray(inputs["L0_attn_x"])
    xs = np.concatenate([x[b, s:s + l, :] for (s, l) in segs(core)], axis=0)
    d = dict(host_consts())
    d["x_in"] = np.ascontiguousarray(xs.T)
    d["w_qkv"] = np.ascontiguousarray(inputs["L0_attn_w_qkv"])
    d["w_out"] = np.ascontiguousarray(inputs["L0_attn_w_out"])
    d["b_out_col"] = np.ascontiguousarray(np.asarray(inputs["L0_attn_b_out"]).reshape(8, 128).T)
    d["ffn_wT"] = np.ascontiguousarray(np.asarray(inputs["L0_ffn_weight"]).T)
    d["ffn_alpha"] = np.ascontiguousarray(np.asarray(inputs["L0_ffn_alpha"]).reshape(1, 1))
    d["ffn_bias_col"] = np.ascontiguousarray(np.asarray(inputs["L0_ffn_bias"]).reshape(8, 128).T)
    return d


import os
def assemble(results, cores):
    out = np.zeros((4, S, D), np.float32)
    for cc in cores:
        b = cc // 2
        xo = results[cc]["x_out"]
        pos = 0
        for (s, l) in segs(cc):
            out[b, s:s + l, :] = xo[:, pos:pos + l].T
            pos += l
    return out


def kernel(**inputs):
    cores = list(range(8))
    if os.environ.get("UNFUSED", "") != "1":
        maps = []
        for cc in cores:
            m = prep_core_l0(inputs, cc)
            m1 = prep_core_l1(inputs, cc, None)
            del m1["x_in"]
            m.update(m1)
            maps.append(m)
        nc = build_fused()
        res = run_bass_kernel_spmd(nc, maps, core_ids=cores)
        return assemble(res.results, cores)
    stage = os.environ.get("STAGE", "all")
    upto0 = int(os.environ.get("UPTO0", os.environ.get("UPTO", "4")))
    upto1 = int(os.environ.get("UPTO1", os.environ.get("UPTO", "4")))
    if stage in ("all", "l0"):
        maps0 = [prep_core_l0(inputs, cc) for cc in cores]
        nc0 = build_l0(0, upto=upto0)
        res0 = run_bass_kernel_spmd(nc0, maps0, core_ids=cores)
        if os.environ.get("DBG"):
            kernel.last_results = res0.results
        xmid = [np.ascontiguousarray(res0.results[cc]["x_out"]) for cc in cores]
        if stage == "l0":
            return assemble(res0.results, cores)
    else:
        xmid = None
    maps1 = [prep_core_l1(inputs, cc, xmid[cc]) for cc in cores]
    nc1 = build_l1(0, upto=upto1)
    res1 = run_bass_kernel_spmd(nc1, maps1, core_ids=cores)
    if os.environ.get("DBG"):
        kernel.last_results = res1.results
    return assemble(res1.results, cores)


# version 12
# speedup vs baseline: 1.0307x; 1.0307x over previous
import contextlib
import os
import numpy as np
import ml_dtypes
import concourse.bass as bass
import concourse.mybir as mybir
from concourse.bass_utils import run_bass_kernel_spmd
from concourse.alu_op_type import AluOpType as OP

F32 = mybir.dt.float32
BF16 = mybir.dt.bfloat16
AF = mybir.ActivationFunctionType
AX = mybir.AxisListType.X
BF = ml_dtypes.bfloat16

D = 1024
T = 2048
S = 4096
NCH = 8
NT = 4
LAT = 256
EXP = 8
FMOE = 768
FCH = 6
PAIRS = [[0, 1], [2, 3], [4, 5], [6, 7]]

ENGS = ("tensor", "vector", "scalar", "gpsimd", "sync")


def segs(core):
    if core % 2 == 0:
        return [(0, 1024), (3072, 1024)]
    return [(1024, 2048)]


class Prog:
    def __init__(self):
        self.nc = bass.Bass("TRN2", target_bir_lowering=False)
        self.q = {e: [] for e in ENGS}
        self.es = contextlib.ExitStack()
        self.sems = {}
        self.ctr = {}

    def sem(self, name):
        if name not in self.sems:
            self.sems[name] = self.es.enter_context(self.nc.semaphore(name))
            self.ctr[name] = 0
        return self.sems[name]

    _uid = 0

    def sb(self, name, shape, dt, scope=None):
        Prog._uid += 1
        return (scope or self.es).enter_context(
            self.nc.sbuf_tensor(f"{name}_u{Prog._uid}", shape, dt))

    def psum(self, name, shape, dt):
        return self.es.enter_context(self.nc.psum_tensor(name, shape, dt))

    def inp(self, name, shape, dt):
        return self.nc.dram_tensor(name, shape, dt, kind="ExternalInput").ap()

    def outp(self, name, shape, dt):
        return self.nc.dram_tensor(name, shape, dt, kind="ExternalOutput").ap()

    def idram(self, name, shape, dt):
        return self.nc.dram_tensor(name, shape, dt).ap()

    def wait(self, eng, semname, val):
        if val is None or val <= 0:
            return
        s = self.sem(semname)
        self.q[eng].append(lambda e, s=s, v=val: e.wait_ge(s, v))

    SELFWAIT = {"vector", "scalar", "gpsimd"}

    def op(self, eng, fn, inc=None, n=1, nowait=False):
        tgt, s = None, None
        if inc is None and eng in self.SELFWAIT and not nowait:
            inc = {"vector": "dv", "scalar": "sc", "gpsimd": "gp"}[eng]
        if inc is not None:
            s = self.sem(inc)
            self.ctr[inc] += n
            tgt = self.ctr[inc]
        sw = (eng in self.SELFWAIT) and not nowait
        def thunk(e, fn=fn, s=s, n=n, sw=sw, tgt=tgt):
            ins = fn(e)
            if s is not None:
                ins.then_inc(s, n)
                if sw:
                    e.wait_ge(s, tgt)
        self.q[eng].append(thunk)
        return tgt

    def dma(self, eng, out, in_, inc):
        return self.op(eng, lambda e, o=out, i=in_: e.dma_start(o, i), inc, 16)

    def barrier(self):
        s = self.sem("bar")
        self.ctr["bar"] += len(ENGS)
        tgt = self.ctr["bar"]
        for eng in ENGS:
            self.q[eng].append(lambda e, s=s: e.nop().then_inc(s, 1))
            self.q[eng].append(lambda e, s=s, v=tgt: e.wait_ge(s, v))

    def finalize(self):
        with self.nc.Block() as blk:
            for eng in ENGS:
                q = self.q[eng]
                def run(e, q=q):
                    for f in q:
                        f(e)
                getattr(blk, eng)(run)
        self.es.close()
        return self.nc



def make_common(p):
    c = {}
    c["pf"] = [p.psum(f"pf{i}", [128, 512], F32) for i in range(7)]
    c["pb16"] = p.psum("pb16", [128, 1024], BF16)
    for nm, shape, dt in [
        ("id_bf", [128, 128], BF16), ("id_f", [128, 128], F32),
        ("id8_f", [8, 8], F32), ("ones1_bf", [128, 1], BF16),
        ("onesr_f", [1, 128], F32), ("onesc_f", [128, 1], F32),
        ("sel2_bf", [128, 2], BF16),
        ("bsel_f", [2, 128], F32), ("h01_f", [2, 256], F32),
        ("tri_bf", [128, 128], BF16), ("esel_f", [8, 1024], F32),
    ]:
        c[nm] = p.sb(nm, shape, dt)
        c["in_" + nm] = p.inp("c_" + nm, shape, dt)
    c["rowpool"] = p.sb("rowpool", [128, 6144], F32)
    c["xhat"] = [p.sb(f"xh{k}", [128, T], BF16) for k in range(NCH)]
    c["ln_count"] = 0
    c["dbg"] = []
    return c


def dbg_dump(p, c, name, ap, shape, dt):
    if not os.environ.get("DBG"):
        return
    for sname in ("dv", "gp", "sc", "pe"):
        p.wait("sync", sname, p.ctr[sname])
    o = p.outp(name, shape, dt)
    p.dma("sync", o, ap, "st")
    p.wait("sync", "st", p.ctr["st"])


def row1(c, i):
    assert i < 9
    base = (i // 3) * 32
    col = (i % 3) * 2048
    return c["rowpool"][base:base + 1, col:col + 2048]


def row2(c, i):
    assert i == 0
    return c["rowpool"][0:2, 4096:6144]


def host_consts():
    id128 = np.eye(128)
    sel2 = np.zeros((128, 2), np.float32); sel2[0:64, 0] = 1; sel2[64:128, 1] = 1
    bsel = np.zeros((2, 128), np.float32); bsel[0, 0:64] = 1; bsel[1, 64:128] = 1
    h01 = np.zeros((2, 256), np.float32); h01[0, 0:128] = 1; h01[1, 128:256] = 1
    tri = (np.arange(128)[:, None] <= np.arange(128)[None, :]).astype(BF)
    esel = np.zeros((8, 1024), np.float32)
    for e in range(8):
        esel[e, e * 128:(e + 1) * 128] = 1
    return {
        "c_id_bf": id128.astype(BF), "c_id_f": id128.astype(np.float32),
        "c_id8_f": np.eye(8, dtype=np.float32),
        "c_ones1_bf": np.ones((128, 1), BF),
        "c_onesr_f": np.ones((1, 128), np.float32),
        "c_onesc_f": np.ones((128, 1), np.float32),
        "c_sel2_bf": sel2.astype(BF), "c_bsel_f": bsel,
        "c_h01_f": h01, "c_tri_bf": tri, "c_esel_f": esel,
    }


def emit_load_consts(p, c):
    for nm in ["id_bf", "id_f", "id8_f", "ones1_bf", "onesr_f", "onesc_f",
               "sel2_bf", "bsel_f", "h01_f", "tri_bf", "esel_f"]:
        p.dma("sync", c[nm][:], c["in_" + nm], "lds")
    c["consts_ready"] = p.ctr["lds"]



def emit_ln(p, c):
    c["ln_count"] += 1
    x, xh, pf = c["x"], c["xhat"], c["pf"]
    with contextlib.ExitStack() as sc:
        rb = p.sb("ln_rb", [128, T], F32, sc)
        mrb = p.sb("ln_mrb", [128, T], F32, sc)
        x2b = [p.sb(f"ln_x2_{i}", [128, 512], BF16, sc) for i in range(2)]
        ntmp = [p.sb(f"ln_nt_{i}", [128, 512], F32, sc) for i in range(2)]
        RR, RMR = row1(c, 0), row1(c, 1)
        RA, RB_, RM = row1(c, 3), row1(c, 4), row1(c, 5)
        for k in range(NCH):
            xk = x[k]
            p.op("vector", lambda e, a=xh[k][:], b=xk: e.tensor_copy(a, b), "dv", nowait=(k < NCH - 1))
        xb_done = p.ctr["dv"]
        p.wait("gpsimd", "dv", xb_done)
        war = [0, 0]
        x2war = [0, 0]
        for n in range(NT):
            sl = slice(n * 512, (n + 1) * 512)
            psA = pf[(n % 2) * 2][0:1, :]
            psB = pf[(n % 2) * 2 + 1][0:1, :]
            x2tgt = []
            for k in range(NCH):
                j = (n * NCH + k) % 2
                p.wait("gpsimd", "pe", x2war[j])
                p.op("gpsimd", lambda e, b=x2b[j][:], a=xh[k][:, sl]: e.tensor_mul(b, a, a), "gp")
                x2tgt.append(p.ctr["gp"])
            p.wait("tensor", "dv", max(xb_done, war[n % 2]))
            for k in range(NCH):
                j = (n * NCH + k) % 2
                p.op("tensor", lambda e, ps=psA, a=xh[k][:, sl], s=(k == 0), t=(k == NCH - 1):
                     e.matmul(ps, c["ones1_bf"][:], a, start=s, stop=t), "pe")
                p.wait("tensor", "gp", x2tgt[k])
                p.op("tensor", lambda e, ps=psB, b=x2b[j][:], s=(k == 0), t=(k == NCH - 1):
                     e.matmul(ps, c["ones1_bf"][:], b, start=s, stop=t), "pe")
                x2war[j] = p.ctr["pe"]
            mm = p.ctr["pe"]
            p.wait("vector", "pe", mm)
            p.op("vector", lambda e, ps=psA, d=RA[0:1, sl]: e.tensor_copy(d, ps), "dv")
            p.op("vector", lambda e, ps=psB, d=RB_[0:1, sl]: e.tensor_copy(d, ps), "dv")
            war[n % 2] = p.ctr["dv"]
        p.op("vector", lambda e: e.tensor_scalar(RM, RA, 1.0 / D, None, OP.mult), "dv")
        p.op("vector", lambda e: e.tensor_scalar(RB_, RB_, 1.0 / D, None, OP.mult), "dv")
        p.op("vector", lambda e: e.tensor_tensor(RA, RM, RM, OP.mult), "dv")
        p.op("vector", lambda e: e.tensor_tensor(RB_, RB_, RA, OP.subtract), "dv")
        p.op("vector", lambda e: e.tensor_scalar(RB_, RB_, 1e-5, None, OP.add), "dv")
        p.wait("scalar", "dv", p.ctr["dv"])
        p.op("scalar", lambda e: e.sqrt(RB_, RB_), "sc")
        p.wait("vector", "sc", p.ctr["sc"])
        p.op("vector", lambda e: e.reciprocal(RA, RB_), "dv")
        p.op("vector", lambda e: e.tensor_tensor(RM, RM, RA, OP.mult), "dv")
        p.op("vector", lambda e: e.tensor_copy(RR, RA), "dv")
        p.op("vector", lambda e: e.tensor_copy(RMR, RM), "dv")
        rows_done = p.ctr["dv"]
        p.wait("tensor", "dv", rows_done)
        bcwar = [0, 0]
        for n in range(NT):
            sl = slice(n * 512, (n + 1) * 512)
            ps = pf[4 + (n % 2)]
            p.wait("tensor", "dv", bcwar[n % 2])
            p.op("tensor", lambda e, ps=ps[0:128, :], r=RR[0:1, sl]:
                 e.matmul(ps, c["onesr_f"][:], r, start=True, stop=True), "pe")
            p.wait("vector", "pe", p.ctr["pe"])
            p.op("vector", lambda e, ps=ps[0:128, :], d=rb[:, sl]: e.tensor_copy(d, ps), "dv")
            p.wait("tensor", "dv", p.ctr["dv"])
            p.op("tensor", lambda e, ps=ps[0:128, :], r=RMR[0:1, sl]:
                 e.matmul(ps, c["onesr_f"][:], r, start=True, stop=True), "pe")
            p.wait("vector", "pe", p.ctr["pe"])
            p.op("vector", lambda e, ps=ps[0:128, :], d=mrb[:, sl]: e.tensor_copy(d, ps), "dv")
            bcwar[n % 2] = p.ctr["dv"]
        bc_done = p.ctr["dv"]
        stats_pe = p.ctr["pe"]
        p.wait("gpsimd", "dv", bc_done)
        p.wait("gpsimd", "pe", stats_pe)
        p.wait("vector", "pe", stats_pe)
        for k in range(NCH):
            xk = x[k]
            for n in range(NT):
                sl = slice(n * 512, (n + 1) * 512)
                eng, sem = ("vector", "dv") if k % 2 == 0 else ("gpsimd", "gp")
                j = 0 if k % 2 == 0 else 1
                tmp = ntmp[j][:]
                p.op(eng, lambda e, t=tmp, a=xk[:, sl], r=rb[:, sl]:
                     e.tensor_tensor(t, a, r, OP.mult), sem)
                p.op(eng, lambda e, t=tmp, d=xh[k][:, sl], m=mrb[:, sl]:
                     e.tensor_tensor(d, t, m, OP.subtract), sem)
        c["ln_dv"] = p.ctr["dv"]
        c["ln_gp"] = p.ctr["gp"]
    p.wait("tensor", "dv", c["ln_dv"])
    p.wait("tensor", "gp", c["ln_gp"])



def emit_l0_attn(p, c):
    pf, pb16 = c["pf"], c["pb16"]
    xh = c["xhat"]
    in_wqkv = p.inp("w_qkv", [D, 3072], F32)
    in_wout = p.inp("w_out", [D, D], F32)
    in_bout = p.inp("b_out_col", [128, NCH], F32)
    with contextlib.ExitStack() as sc:
        qb = [p.sb(f"l0q{i}", [128, T], BF16, sc) for i in range(NCH)]
        kb = p.sb("l0k", [128, T], BF16, sc)
        vb = p.sb("l0v", [128, T], BF16, sc)
        kT = [p.sb(f"l0kT{i}", [128, 128], BF16, sc) for i in range(2)]
        vT = [p.sb(f"l0vT{i}", [128, 130], BF16, sc) for i in range(2)]
        elu = [p.sb(f"l0e{i}", [128, 512], F32, sc) for i in range(4)]
        NW = 130
        ccbuf = p.sb("l0cc", [128, NCH * NW + 8], F32, sc)
        kvb = p.sb("l0kvb", [128, NCH * 128], BF16, sc)
        qzb = [p.sb(f"l0qz{i}", [128, 512], BF16, sc) for i in range(2)]
        wq_s = [p.sb(f"l0wq{i}", [128, 8 * 384], BF16, sc) for i in range(2)]
        wo_s = [p.sb(f"l0wo{i}", [128, 8 * 128], BF16, sc) for i in range(2)]
        boutc = p.sb("l0bout", [128, NCH], F32, sc)
        tmpE = [p.sb(f"l0te{i}", [128, 512], F32, sc) for i in range(2)]
        attn = xh
        den = row2(c, 0)
        cc_in = p.idram("l0cc_in", [128, NCH * NW + 8], F32)
        cc_out = p.idram("l0cc_out", [128, NCH * NW + 8], F32)

        p.dma("sync", boutc[:], in_bout, "lds")
        b_ready = p.ctr["lds"]

        def load_wqkv(cc):
            buf = wq_s[cc % 2]
            for k in range(NCH):
                for j, ofs in enumerate((cc * 128, 1024 + cc * 128, 2048 + cc * 128)):
                    src = in_wqkv[k * 128:(k + 1) * 128, ofs:ofs + 128]
                    p.dma("gpsimd", buf[:, k * 384 + j * 128: k * 384 + (j + 1) * 128], src, "ldg")
            return p.ctr["ldg"]

        wq_ready = [load_wqkv(0)]
        wq_war = [0, 0]
        p.op("vector", lambda e: e.memset(vT[0][:, 128:129], 1.0), "dv", nowait=True)
        p.op("vector", lambda e: e.memset(vT[1][:, 128:129], 1.0), "dv")
        ones_set = p.ctr["dv"]
        elu_sc = [0, 0, 0, 0]
        evwar = [0, 0, 0, 0]
        twar = 0
        for cc in range(NCH):
            if cc + 1 < NCH:
                p.wait("gpsimd", "pe", wq_war[(cc + 1) % 2])
                wq_ready.append(load_wqkv(cc + 1))
            wbuf = wq_s[cc % 2]
            p.wait("tensor", "ldg", wq_ready[cc])
            for wj, dst in ((0, qb[cc]), (1, kb), (2, vb)):
                for n in range(NT):
                    sl = slice(n * 512, (n + 1) * 512)
                    jr = (wj * NT + n) % 4
                    ps = pf[jr]
                    p.wait("tensor", "dv", evwar[jr])
                    for k in range(NCH):
                        p.op("tensor", lambda e, ps=ps[:], w=wbuf[:, k * 384 + wj * 128: k * 384 + wj * 128 + 128],
                             a=xh[k][:, sl], s=(k == 0), t=(k == NCH - 1):
                             e.matmul(ps, w, a, start=s, stop=t), "pe")
                    mm = p.ctr["pe"]
                    if wj == 2:
                        p.wait("vector", "pe", mm)
                        p.op("vector", lambda e, ps=ps[:], d=dst[:, sl]: e.tensor_copy(d, ps), "dv")
                        evwar[jr] = p.ctr["dv"]
                    else:
                        eb = elu[jr][:]
                        p.wait("vector", "pe", mm)
                        p.wait("vector", "sc", elu_sc[jr])
                        p.op("vector", lambda e, ps=ps[:], b=eb: e.tensor_scalar(b, ps, 0.0, None, OP.min), "dv")
                        p.wait("scalar", "dv", p.ctr["dv"])
                        p.op("scalar", lambda e, b=eb: e.activation(b, b, AF.Exp), "sc")
                        elu_sc[jr] = p.ctr["sc"]
                        p.wait("vector", "sc", elu_sc[jr])
                        p.op("vector", lambda e, ps=ps[:], b=eb, d=dst[:, sl]:
                             e.scalar_tensor_tensor(d, ps, 0.0, b, OP.max, OP.add), "dv")
                        evwar[jr] = p.ctr["dv"]
            wq_war[cc % 2] = p.ctr["pe"]
            kvev = p.ctr["dv"]
            p.wait("vector", "dv", kvev)
            p.op("vector", lambda e, d=ccbuf[:, NCH * NW + cc: NCH * NW + cc + 1]:
                 e.reduce_sum(d, kb[:], AX), "dv")
            kvps = pf[4]
            tev = [0, 0]
            tkv = [0, 0]
            for tc in range(16):
                tsl = slice(tc * 128, (tc + 1) * 128)
                j = tc % 2
                p.wait("tensor", "dv", tev[j])
                p.op("tensor", lambda e, a=kb[:, tsl], d=pb16[:, j * 256: j * 256 + 128]:
                     e.transpose(d, a, c["id_bf"][:]), "pe")
                p.op("tensor", lambda e, a=vb[:, tsl], d=pb16[:, j * 256 + 128: j * 256 + 256]:
                     e.transpose(d, a, c["id_bf"][:]), "pe")
                ttr = p.ctr["pe"]
                p.wait("vector", "pe", ttr)
                if tc < 2:
                    p.wait("vector", "dv", ones_set)
                p.op("vector", lambda e, d=kT[j][:, 0:128], ps=pb16[:, j * 256: j * 256 + 128]:
                     e.tensor_copy(d, ps), "dv")
                p.op("vector", lambda e, d=vT[j][:, 0:128], ps=pb16[:, j * 256 + 128: j * 256 + 256]:
                     e.tensor_copy(d, ps), "dv")
                tev[j] = p.ctr["dv"]
                p.wait("tensor", "dv", tev[j])
                p.op("tensor", lambda e, s=(tc == 0), t=(tc == 15), l=kT[j][:], r=vT[j][:, 0:129]:
                     e.matmul(kvps[0:128, 0:129], l, r, start=s, stop=t), "pe")
                tkv[j] = p.ctr["pe"]
            p.wait("vector", "pe", p.ctr["pe"])
            p.op("vector", lambda e, d=ccbuf[:, cc * NW: cc * NW + 129]:
                 e.tensor_copy(d, kvps[0:128, 0:129]), "dv")
        p.wait("gpsimd", "dv", p.ctr["dv"])
        p.dma("gpsimd", cc_in, ccbuf[:], "ldg")
        p.wait("gpsimd", "ldg", p.ctr["ldg"])
        p.op("gpsimd", lambda e: e.collective_compute(
            "AllReduce", OP.add, replica_groups=PAIRS,
            ins=[cc_in.opt()], outs=[cc_out.opt()]), "cc0")
        p.wait("gpsimd", "cc0", 1)
        p.dma("gpsimd", ccbuf[:], cc_out, "ldg")
        p.wait("vector", "ldg", p.ctr["ldg"])
        for cc in range(NCH):
            p.op("vector", lambda e, d=ccbuf[0:64, cc * NW + 64: cc * NW + 128]: e.memset(d, 0.0), "dv")
            p.op("vector", lambda e, d=ccbuf[64:128, cc * NW: cc * NW + 64]: e.memset(d, 0.0), "dv")
            p.op("vector", lambda e, d=kvb[:, cc * 128:(cc + 1) * 128], a=ccbuf[:, cc * NW: cc * NW + 128]:
                 e.tensor_copy(d, a), "dv")
        kv_ready = p.ctr["dv"]
        p.wait("tensor", "dv", kv_ready)
        nwar = [0, 0]
        qz_war = 0
        for cc in range(NCH):
            for n in range(NT):
                sl = slice(n * 512, (n + 1) * 512)
                p.wait("vector", "pe", qz_war if n == 0 else 0)
                p.op("vector", lambda e, b=qzb[n % 2][:], a=qb[cc][:, sl],
                     z=ccbuf[:, NCH * NW + cc: NCH * NW + cc + 1]:
                     e.tensor_scalar(b, a, z, None, OP.mult), "dv")
                p.wait("tensor", "dv", p.ctr["dv"])
                dps = pf[5][0:2, :]
                p.op("tensor", lambda e, dps=dps, a=qzb[n % 2][:]:
                     e.matmul(dps, c["sel2_bf"][:], a, start=True, stop=True), "pe")
                p.wait("vector", "pe", p.ctr["pe"])
                p.op("vector", lambda e, dps=dps, d=den[:, sl]:
                     e.tensor_scalar(d, dps, 1e-6, None, OP.add), "dv")
                p.wait("tensor", "dv", p.ctr["dv"])
            p.op("vector", lambda e, d=den: e.reciprocal(d, d), "dv")
            rec_done = p.ctr["dv"]
            p.wait("tensor", "dv", rec_done)
            for n in range(NT):
                sl = slice(n * 512, (n + 1) * 512)
                nps = pf[n % 2]
                bps = pf[2 + n % 2]
                p.wait("tensor", "dv", nwar[n % 2])
                p.op("tensor", lambda e, ps=nps[:], w=kvb[:, cc * 128:(cc + 1) * 128], a=qb[cc][:, sl]:
                     e.matmul(ps, w, a, start=True, stop=True), "pe")
                p.op("tensor", lambda e, ps=bps[:], r=den[:, sl]:
                     e.matmul(ps, c["bsel_f"][:], r, start=True, stop=True), "pe")
                nmm = p.ctr["pe"]
                p.wait("vector", "pe", nmm)
                p.op("vector", lambda e, ps=bps[:]: e.tensor_copy(tmpE[0][:], ps), "dv")
                p.op("vector", lambda e, ps=nps[:], d=attn[cc][:, sl]:
                     e.tensor_tensor(d, ps, tmpE[0][:], OP.mult), "dv")
                nwar[n % 2] = p.ctr["dv"]
            qz_war = p.ctr["pe"]
        attn_done = p.ctr["dv"]
        def load_wout(oc):
            buf = wo_s[oc % 2]
            for k in range(NCH):
                src = in_wout[k * 128:(k + 1) * 128, oc * 128:(oc + 1) * 128]
                p.dma("gpsimd", buf[:, k * 128:(k + 1) * 128], src, "ldg")
            return p.ctr["ldg"]
        wo_ready = [load_wout(0)]
        wo_war = [0, 0]
        p.wait("tensor", "dv", attn_done)
        p.wait("vector", "lds", b_ready)
        gwar = [0, 0]
        for oc in range(NCH):
            if oc + 1 < NCH:
                p.wait("gpsimd", "pe", wo_war[(oc + 1) % 2])
                wo_ready.append(load_wout(oc + 1))
            p.wait("tensor", "ldg", wo_ready[oc])
            for n in range(NT):
                sl = slice(n * 512, (n + 1) * 512)
                ps = pf[n % 2]
                p.wait("tensor", "dv", nwar[n % 2])
                for k in range(NCH):
                    p.op("tensor", lambda e, ps=ps[:], w=wo_s[oc % 2][:, k * 128:(k + 1) * 128],
                         a=attn[k][:, sl], s=(k == 0), t=(k == NCH - 1):
                         e.matmul(ps, w, a, start=s, stop=t), "pe")
                omm = p.ctr["pe"]
                p.wait("vector", "pe", omm)
                p.wait("vector", "gp", gwar[n % 2])
                p.op("vector", lambda e, ps=ps[:], t=tmpE[n % 2][:], b=boutc[:, oc:oc + 1]:
                     e.tensor_scalar(t, ps, b, None, OP.add), "dv")
                nwar[n % 2] = p.ctr["dv"]
                p.wait("gpsimd", "dv", p.ctr["dv"])
                xoc = c["x"][oc]
                p.op("gpsimd", lambda e, d=xoc[:, sl], t=tmpE[n % 2][:]:
                     e.tensor_add(d, d, t), "gp")
                gwar[n % 2] = p.ctr["gp"]
            wo_war[oc % 2] = p.ctr["pe"]
        p.wait("tensor", "gp", p.ctr["gp"])
        p.wait("vector", "gp", p.ctr["gp"])
        p.wait("sync", "gp", p.ctr["gp"])



def emit_l0_ffn(p, c):
    pf = c["pf"]
    xh = c["xhat"]
    in_w = p.inp("ffn_wT", [D, D], F32)
    in_alpha = p.inp("ffn_alpha", [1, 1], F32)
    in_bias = p.inp("ffn_bias_col", [128, NCH], F32)
    with contextlib.ExitStack() as sc:
        wf = [p.sb(f"f_w{i}", [128, D], F32, sc) for i in range(2)]
        tern = [p.sb(f"f_t{k}", [128, D], BF16, sc) for k in range(NCH)]
        t1 = p.sb("f_t1", [128, D], F32, sc)
        asum = p.sb("f_asum", [128, 8], F32, sc)
        row8 = p.sb("f_row8", [8, 128], F32, sc)
        col8 = p.sb("f_col8", [8, 1], F32, sc)
        rowt = p.sb("f_rowt", [1, 8], F32, sc)
        sc3 = p.sb("f_sc3", [1, 3], F32, sc)
        bc3 = p.sb("f_bc3", [128, 3], F32, sc)
        biasc = p.sb("f_bias", [128, NCH], F32, sc)
        tmpE = [p.sb(f"f_te{i}", [128, 512], F32, sc) for i in range(4)]
        p.dma("sync", biasc[:], in_bias, "lds")
        p.dma("sync", sc3[0:1, 1:2], in_alpha, "lds")
        misc_ready = p.ctr["lds"]
        red_tgt = [0, 0]
        for k in range(NCH):
            if k >= 2:
                p.wait("sync", "dv", red_tgt[k % 2])
            wt = p.dma("sync", wf[k % 2][:], in_w[k * 128:(k + 1) * 128, :], "lds")
            p.wait("vector", "lds", wt)
            p.op("vector", lambda e, d=asum[:, k:k + 1], a=wf[k % 2][:]:
                 e.reduce_sum(d, a, AX, apply_absolute_value=True), "dv")
            red_tgt[k % 2] = p.ctr["dv"]
        s1 = p.ctr["dv"]
        p.wait("tensor", "dv", s1)
        p.op("tensor", lambda e: e.matmul(pf[0][0:1, 0:8], c["onesc_f"][:], asum[:], start=True, stop=True), "pe")
        p.wait("vector", "pe", p.ctr["pe"])
        p.wait("vector", "lds", misc_ready)
        p.op("vector", lambda e: e.tensor_copy(rowt[:], pf[0][0:1, 0:8]), "dv")
        p.op("vector", lambda e: e.reduce_sum(sc3[0:1, 0:1], rowt[:], AX), "dv")
        p.op("vector", lambda e: e.tensor_scalar(sc3[0:1, 0:1], sc3[0:1, 0:1], 0.7 / (D * D), None, OP.mult), "dv")
        p.op("vector", lambda e: e.tensor_scalar(sc3[0:1, 2:3], sc3[0:1, 0:1], -1.0, None, OP.mult), "dv")
        p.wait("tensor", "dv", p.ctr["dv"])
        p.op("tensor", lambda e: e.matmul(pf[2][0:128, 0:3], c["onesr_f"][:], sc3[0:1, 0:3], start=True, stop=True), "pe")
        p.wait("vector", "pe", p.ctr["pe"])
        p.op("vector", lambda e: e.tensor_copy(bc3[:], pf[2][0:128, 0:3]), "dv")
        bc_done = p.ctr["dv"]
        dbg_dump(p, c, "dbg_bc3", bc3[:], [128, 3], F32)
        dbg_dump(p, c, "dbg_asum", asum[:], [128, 8], F32)
        tern_tgt = [bc_done, bc_done]
        for k in range(NCH):
            p.wait("sync", "dv", tern_tgt[k % 2])
            wt = p.dma("sync", wf[k % 2][:], in_w[k * 128:(k + 1) * 128, :], "lds")
            p.wait("vector", "lds", wt)
            p.op("vector", lambda e, a=wf[k % 2][:]: e.tensor_scalar(t1[:], a, bc3[:, 0:1], None, OP.is_gt), "dv")
            p.op("vector", lambda e, a=wf[k % 2][:]: e.tensor_scalar(a, a, bc3[:, 2:3], None, OP.is_lt), "dv")
            p.op("vector", lambda e, a=wf[k % 2][:], d=tern[k][:]: e.tensor_tensor(d, t1[:], a, OP.subtract), "dv")
            tern_tgt[k % 2] = p.ctr["dv"]
        tern_done = p.ctr["dv"]
        dbg_dump(p, c, "dbg_tern0", tern[0][:], [128, D], BF16)
        p.wait("tensor", "dv", tern_done)
        gwar = [0, 0, 0, 0]
        dwar = [0, 0, 0, 0]
        for oc in range(NCH):
            for n in range(NT):
                sl = slice(n * 512, (n + 1) * 512)
                jr = (oc * NT + n) % 4
                ps = pf[jr]
                p.wait("tensor", "dv", dwar[jr])
                for k in range(NCH):
                    p.op("tensor", lambda e, ps=ps[:], w=tern[k][:, oc * 128:(oc + 1) * 128],
                         a=xh[k][:, sl], s=(k == 0), t=(k == NCH - 1):
                         e.matmul(ps, w, a, start=s, stop=t), "pe")
                mm = p.ctr["pe"]
                p.wait("vector", "pe", mm)
                p.wait("vector", "gp", gwar[jr])
                p.op("vector", lambda e, ps=ps[:], t=tmpE[jr][:], b=biasc[:, oc:oc + 1]:
                     e.tensor_scalar(t, ps, bc3[:, 1:2], b, OP.mult, OP.add), "dv")
                dwar[jr] = p.ctr["dv"]
                p.wait("gpsimd", "dv", p.ctr["dv"])
                xoc = c["x"][oc]
                p.op("gpsimd", lambda e, d=xoc[:, sl], t=tmpE[jr][:]: e.tensor_add(d, d, t), "gp")
                gwar[jr] = p.ctr["gp"]
        p.wait("tensor", "gp", p.ctr["gp"])
        p.wait("vector", "gp", p.ctr["gp"])
        p.wait("sync", "gp", p.ctr["gp"])




def emit_l1_attn(p, c, core, x_scope, x_reload):
    pf, xh = c["pf"], c["xhat"]
    NKC = [12, 16, 28, 32]
    UMAP = {0: [0, 1, 2, 3, 8, 9, 10, 11], 1: [4, 5, 6, 7, 12, 13, 14, 15],
            2: [16, 17, 18, 19, 24, 25, 26, 27], 3: [20, 21, 22, 23, 28, 29, 30, 31]}
    ZOF = [0, 12, 28, 56]

    in_wq = p.inp("wq", [D, D], F32)
    in_wdkv = p.inp("wdkv", [D, LAT], F32).rearrange("(k p) f -> p k f", p=128)
    in_wuk = p.inp("wuk", [LAT, D], F32).rearrange("(k p) f -> p k f", p=128)
    in_wuv = p.inp("wuv", [LAT, D], F32).rearrange("(k p) f -> p k f", p=128)
    in_wo = p.inp("wo", [D, D], F32)
    in_gb = p.inp("l1_gam", [128, 2], F32)
    in_bb = p.inp("l1_bet", [128, 2], F32)
    in_c4 = p.inp("ropeC", [128, S], BF16)
    in_s4 = p.inp("ropeS", [128, S], BF16)
    in_qc4 = p.inp("qropeC", [128, T], BF16)
    in_qs4 = p.inp("qropeS", [128, T], BF16)
    in_masks = p.inp("cmasks", [128, 2 * 8 * 512], BF16)
    in_zvb = p.inp("zvb", [128, 88], F32)
    ag_in = p.idram("ag_in", [2 * 128, T], BF16)
    ag_out = p.idram("ag_out", [4 * 128, T], BF16)

    x_scope.close()
    with contextlib.ExitStack() as sc:
        c4 = p.sb("c4", [128, S], BF16, sc)
        s4 = p.sb("s4", [128, S], BF16, sc)
        wuk_sb = p.sb("wuk", [128, 2 * D], BF16, sc)
        wuv_sb = p.sb("wuv", [128, 2 * D], BF16, sc)
        latful = [p.sb(f"latf{i}", [128, S], BF16, sc) for i in range(2)]
        gbc = p.sb("gbc", [128, 2], F32, sc)
        bbc = p.sb("bbc", [128, 2], F32, sc)
        masks = p.sb("cmask", [128, 2 * 8 * 512], BF16, sc)
        zvb = p.sb("zvb", [128, 88], F32, sc)
        q_all = [p.sb(f"q{i}", [128, T], BF16, sc) for i in range(NCH)]
        ktmp = [p.sb(f"ktmp{i}", [128, 512], F32, sc) for i in range(2)]
        ksw = [p.sb(f"ksw{i}", [128, 512], F32, sc) for i in range(2)]
        q2b = [p.sb(f"q2b{i}", [128, 512], BF16, sc) for i in range(2)]
        p.dma("gpsimd", c4[:], in_c4, "ldg")
        p.dma("gpsimd", s4[:], in_s4, "ldg")
        p.dma("gpsimd", wuk_sb[:].rearrange("p (k f) -> p k f", f=D), in_wuk, "ldg")
        p.dma("gpsimd", wuv_sb[:].rearrange("p (k f) -> p k f", f=D), in_wuv, "ldg")
        p.dma("gpsimd", masks[:], in_masks, "ldg")
        p.dma("sync", gbc[:], in_gb, "lds")
        p.dma("sync", bbc[:], in_bb, "lds")
        p.dma("sync", zvb[:], in_zvb, "lds")
        w1rdy = p.ctr["ldg"]
        g_rdy = p.ctr["lds"]

        def rope_evac(ps, dst, ctab, stab, off, j):
            kt, kw = ktmp[j][:], ksw[j][:]
            csl = slice(off, off + 512)
            p.op("vector", lambda e, ps=ps[:], t=kt: e.tensor_copy(t, ps), "dv")
            for g in range(4):
                src = slice((g ^ 1) * 32, (g ^ 1) * 32 + 32)
                dstp = slice(g * 32, g * 32 + 32)
                p.op("vector", lambda e, a=kw[dstp, :], b=kt[src, :]: e.tensor_copy(a, b),
                     "dv", nowait=True)
            p.wait("gpsimd", "dv", p.ctr["dv"])
            p.op("gpsimd", lambda e, t=kt, cc=ctab[:, csl]: e.tensor_mul(t, t, cc), "gp", nowait=True)
            p.op("gpsimd", lambda e, w=kw, ss=stab[:, csl]: e.tensor_mul(w, w, ss), "gp")
            p.wait("vector", "gp", p.ctr["gp"])
            p.op("vector", lambda e, d=dst, t=kt, w=kw: e.tensor_tensor(d, t, w, OP.add), "dv")
            return p.ctr["dv"]
        with contextlib.ExitStack() as ssc:
            wdkv_sb = p.sb("wdkv", [128, 8 * LAT], BF16, ssc)
            p.dma("gpsimd", wdkv_sb[:].rearrange("p (k f) -> p k f", f=LAT), in_wdkv, "ldg")
            w1rdy = p.ctr["ldg"]
            latf = [p.sb(f"latraw{i}", [128, T], F32, ssc) for i in range(2)]
            latloc = [p.sb(f"latloc{i}", [128, T], BF16, ssc) for i in range(2)]
            rb2 = p.sb("l1rb", [128, T], F32, ssc)
            mrb2 = p.sb("l1mrb", [128, T], F32, ssc)
            x2b = [p.sb(f"l1x2_{i}", [128, 512], BF16, ssc) for i in range(2)]
            RR, RMR = row1(c, 0), row1(c, 1)
            RA, RB_, RM = row1(c, 3), row1(c, 4), row1(c, 5)
            p.wait("tensor", "ldg", w1rdy)
            evwar = [0, 0]
            for ch in range(2):
                for n in range(NT):
                    sl = slice(n * 512, (n + 1) * 512)
                    ps = pf[n % 2]
                    p.wait("tensor", "dv", evwar[n % 2])
                    for k in range(NCH):
                        p.op("tensor", lambda e, ps=ps[:], w=wdkv_sb[:, k * LAT + ch * 128: k * LAT + ch * 128 + 128],
                             a=xh[k][:, sl], s=(k == 0), t=(k == NCH - 1):
                             e.matmul(ps, w, a, start=s, stop=t), "pe")
                    p.wait("vector", "pe", p.ctr["pe"])
                    p.op("vector", lambda e, ps=ps[:], d=latf[ch][:, sl]: e.tensor_copy(d, ps), "dv")
                    evwar[n % 2] = p.ctr["dv"]
            lat_done = p.ctr["dv"]
            for ch in range(2):
                p.op("vector", lambda e, a=latloc[ch][:], b=latf[ch][:]: e.tensor_copy(a, b), "dv")
            p.wait("gpsimd", "dv", p.ctr["dv"])
            war = [0]
            x2war = [0, 0]
            for n in range(NT):
                sl = slice(n * 512, (n + 1) * 512)
                psA = pf[2][0:1, :]
                psB = pf[3][0:1, :]
                x2t = []
                for ch in range(2):
                    p.wait("gpsimd", "pe", x2war[ch])
                    p.op("gpsimd", lambda e, b=x2b[ch][:], a=latloc[ch][:, sl]: e.tensor_mul(b, a, a), "gp")
                    x2t.append(p.ctr["gp"])
                p.wait("tensor", "gp", x2t[-1])
                p.wait("tensor", "dv", war[0])
                for ch in range(2):
                    p.op("tensor", lambda e, ps=psA, a=latloc[ch][:, sl], s=(ch == 0), t=(ch == 1):
                         e.matmul(ps, c["ones1_bf"][:], a, start=s, stop=t), "pe")
                    p.op("tensor", lambda e, ps=psB, b=x2b[ch][:], s=(ch == 0), t=(ch == 1):
                         e.matmul(ps, c["ones1_bf"][:], b, start=s, stop=t), "pe")
                x2war[0] = x2war[1] = p.ctr["pe"]
                p.wait("vector", "pe", p.ctr["pe"])
                p.op("vector", lambda e, ps=psA, d=RA[0:1, sl]: e.tensor_copy(d, ps), "dv")
                p.op("vector", lambda e, ps=psB, d=RB_[0:1, sl]: e.tensor_copy(d, ps), "dv")
                war[0] = p.ctr["dv"]
            p.op("vector", lambda e: e.tensor_scalar(RM, RA, 1.0 / LAT, None, OP.mult), "dv")
            p.op("vector", lambda e: e.tensor_scalar(RB_, RB_, 1.0 / LAT, None, OP.mult), "dv")
            p.op("vector", lambda e: e.tensor_tensor(RA, RM, RM, OP.mult), "dv")
            p.op("vector", lambda e: e.tensor_tensor(RB_, RB_, RA, OP.subtract), "dv")
            p.op("vector", lambda e: e.tensor_scalar(RB_, RB_, 1e-5, None, OP.add), "dv")
            p.wait("scalar", "dv", p.ctr["dv"])
            p.op("scalar", lambda e: e.sqrt(RB_, RB_), "sc")
            p.wait("vector", "sc", p.ctr["sc"])
            p.op("vector", lambda e: e.reciprocal(RA, RB_), "dv")
            p.op("vector", lambda e: e.tensor_tensor(RM, RM, RA, OP.mult), "dv")
            p.op("vector", lambda e: e.tensor_copy(RR, RA), "dv")
            p.op("vector", lambda e: e.tensor_copy(RMR, RM), "dv")
            rows_done = p.ctr["dv"]
            p.wait("tensor", "dv", rows_done)
            for n in range(NT):
                sl = slice(n * 512, (n + 1) * 512)
                ps = pf[4]
                p.op("tensor", lambda e, ps=ps[0:128, :], r=RR[0:1, sl]:
                     e.matmul(ps, c["onesr_f"][:], r, start=True, stop=True), "pe")
                p.wait("vector", "pe", p.ctr["pe"])
                p.op("vector", lambda e, ps=ps[0:128, :], d=rb2[:, sl]: e.tensor_copy(d, ps), "dv")
                p.wait("tensor", "dv", p.ctr["dv"])
                p.op("tensor", lambda e, ps=ps[0:128, :], r=RMR[0:1, sl]:
                     e.matmul(ps, c["onesr_f"][:], r, start=True, stop=True), "pe")
                p.wait("vector", "pe", p.ctr["pe"])
                p.op("vector", lambda e, ps=ps[0:128, :], d=mrb2[:, sl]: e.tensor_copy(d, ps), "dv")
                p.wait("tensor", "dv", p.ctr["dv"])
            bc_done = p.ctr["dv"]
            p.wait("vector", "lds", g_rdy)
            p.wait("gpsimd", "dv", bc_done)
            for ch in range(2):
                eng, sem = ("vector", "dv") if ch == 0 else ("gpsimd", "gp")
                tb = latf[ch]
                p.op(eng, lambda e, t=tb[:], r=rb2[:]: e.tensor_tensor(t, t, r, OP.mult), sem)
                p.op(eng, lambda e, t=tb[:], m=mrb2[:]: e.tensor_tensor(t, t, m, OP.subtract), sem)
                p.op(eng, lambda e, d=latloc[ch][:], t=tb[:], g=gbc[:, ch:ch + 1], b=bbc[:, ch:ch + 1]:
                     e.tensor_scalar(d, t, g, b, OP.mult, OP.add), sem)
            p.wait("gpsimd", "dv", p.ctr["dv"])
            for ch in range(2):
                p.dma("gpsimd", ag_in[ch * 128:(ch + 1) * 128, :], latloc[ch][:], "ldg")
            p.wait("gpsimd", "ldg", p.ctr["ldg"])
            cc_tgt = p.op("gpsimd", lambda e: e.collective_compute(
                "AllGather", OP.bypass, replica_groups=PAIRS,
                ins=[ag_in.opt()], outs=[ag_out.opt()]), "cc0", nowait=True)
            p.wait("gpsimd", "cc0", cc_tgt)
            for ch in range(2):
                p.dma("gpsimd", latful[ch][:, 0:1024], ag_out[ch * 128:(ch + 1) * 128, 0:1024], "ldg")
                p.dma("gpsimd", latful[ch][:, 3072:4096], ag_out[ch * 128:(ch + 1) * 128, 1024:2048], "ldg")
                p.dma("gpsimd", latful[ch][:, 1024:3072], ag_out[256 + ch * 128: 256 + (ch + 1) * 128, :], "ldg")
            lat_ready = p.ctr["ldg"]
            p.wait("tensor", "ldg", lat_ready)
            p.barrier()
            if os.environ.get("L1SUB", "") == "lat":
                dbg_dump(p, c, "dbg_lat0", latful[0][:], [128, S], BF16)
                dbg_dump(p, c, "dbg_lat1", latful[1][:], [128, S], BF16)
        with contextlib.ExitStack() as qsc:
            qc4 = p.sb("qc4", [128, T], BF16, qsc)
            qs4 = p.sb("qs4", [128, T], BF16, qsc)
            wq_s = [p.sb(f"wqs{i}", [128, 1024], BF16, qsc) for i in range(2)]
            p.dma("gpsimd", qc4[:], in_qc4, "ldg")
            p.dma("gpsimd", qs4[:], in_qs4, "ldg")
            qt_rdy = p.ctr["ldg"]

            def load_wq(hp):
                buf = wq_s[hp % 2]
                for k in range(NCH):
                    p.dma("gpsimd", buf[:, k * 128:(k + 1) * 128],
                          in_wq[k * 128:(k + 1) * 128, hp * 128:(hp + 1) * 128], "ldg")
                return p.ctr["ldg"]

            wq_rdy = [load_wq(0)]
            wq_war = [0, 0]
            qwar = [0, 0]
            p.wait("gpsimd", "ldg", qt_rdy)
            for hp in range(NCH):
                if hp + 1 < NCH:
                    p.wait("gpsimd", "pe", wq_war[(hp + 1) % 2])
                    wq_rdy.append(load_wq(hp + 1))
                p.wait("tensor", "ldg", wq_rdy[hp])
                for n in range(NT):
                    sl = slice(n * 512, (n + 1) * 512)
                    ps = pf[n % 2]
                    p.wait("tensor", "dv", qwar[n % 2])
                    for k in range(NCH):
                        p.op("tensor", lambda e, ps=ps[:], w=wq_s[hp % 2][:, k * 128:(k + 1) * 128],
                             a=xh[k][:, sl], s=(k == 0), t=(k == NCH - 1):
                             e.matmul(ps, w, a, start=s, stop=t), "pe")
                    p.wait("vector", "pe", p.ctr["pe"])
                    p.wait("gpsimd", "pe", p.ctr["pe"])
                    qwar[n % 2] = rope_evac(ps, q_all[hp][:, sl], qc4, qs4, n * 512, n % 2)
                wq_war[hp % 2] = p.ctr["pe"]
            p.barrier()
            if os.environ.get("L1SUB", "") == "q":
                for i in range(NCH):
                    dbg_dump(p, c, f"dbg_q{i}", q_all[i][:], [128, T], BF16)
        vtm = [p.sb(f"vtm{i}", [128, 130], BF16, sc) for i in range(32)]
        kbuf = p.sb("kbuf", [128, S], BF16, sc)
        k0c = p.sb("k0c", [128, 1], F32, sc)
        sh2 = p.sb("sh2", [2, 8], F32, sc)
        kmx = p.sb("kmx", [2, 32], F32, sc)
        biasb = p.sb("biasb", [128, 8], F32, sc)
        biasM = p.sb("biasM", [128, 2 * 88], F32, sc)
        linb = p.sb("linb", [128, 512], F32, sc)
        ptb = [p.sb(f"pt{i}", [128, 512], BF16, sc) for i in range(3)]
        NQ2 = c["rowpool"][0:2, 0:2048]
        D0 = c["rowpool"][0:2, 2048:4096]
        BROW = c["rowpool"][0:2, 4096:6144]
        l1sub = os.environ.get("L1SUB", "")
        hp_list = [] if l1sub in ("lat", "q") else ([0] if l1sub == "hp0" else list(range(NCH)))
        for hp in hp_list:
            p.wait("tensor", "dv", p.ctr["dv"])
            p.wait("tensor", "sc", p.ctr["sc"])
            vwar = [0, 0]
            for tc in range(32):
                ps = pf[tc % 2]
                p.wait("tensor", "dv", vwar[tc % 2])
                for ch in range(2):
                    p.op("tensor", lambda e, ps=ps[0:128, 0:128], a=latful[ch][:, tc * 128:(tc + 1) * 128],
                         w=wuv_sb[:, ch * D + hp * 128: ch * D + hp * 128 + 128],
                         s=(ch == 0), t=(ch == 1):
                         e.matmul(ps, a, w, start=s, stop=t), "pe")
                p.wait("vector", "pe", p.ctr["pe"])
                v3 = vtm[tc][:].rearrange("p (h w) -> p h w", w=65)
                p.op("vector", lambda e, d=v3[:, :, 64:65]: e.memset(d, 1.0), "dv")
                p.op("vector", lambda e, d=v3[:, :, 0:64],
                     ps=ps[0:128, 0:128].rearrange("p (h w) -> p h w", w=64):
                     e.tensor_copy(d, ps), "dv")
                vwar[tc % 2] = p.ctr["dv"]
            if l1sub == "hp0v":
                dbg_dump(p, c, "dbg_v0", vtm[0][:], [128, 130], BF16)
                dbg_dump(p, c, "dbg_v5", vtm[5][:], [128, 130], BF16)
                break
            kwar2 = [vwar[0], vwar[1]]
            for kt in range(8):
                ps = pf[kt % 2]
                p.wait("tensor", "dv", kwar2[kt % 2])
                p.wait("tensor", "gp", p.ctr["gp"])
                for ch in range(2):
                    p.op("tensor", lambda e, ps=ps[:],
                         w=wuk_sb[:, ch * D + hp * 128: ch * D + hp * 128 + 128],
                         a=latful[ch][:, kt * 512:(kt + 1) * 512],
                         s=(ch == 0), t=(ch == 1):
                         e.matmul(ps, w, a, start=s, stop=t), "pe")
                p.wait("vector", "pe", p.ctr["pe"])
                p.wait("gpsimd", "pe", p.ctr["pe"])
                kwar2[kt % 2] = rope_evac(ps, kbuf[:, kt * 512:(kt + 1) * 512], c4, s4, kt * 512, kt % 2)
                p.wait("gpsimd", "dv", kwar2[kt % 2])
                p.op("gpsimd", lambda e, b=q2b[kt % 2][:], a=kbuf[:, kt * 512:(kt + 1) * 512]:
                     e.tensor_mul(b, a, a), "gp")
                p.wait("tensor", "gp", p.ctr["gp"])
                p.op("tensor", lambda e, ps=pf[6][0:2, :], a=q2b[kt % 2][:]:
                     e.matmul(ps, c["sel2_bf"][:], a, start=True, stop=True), "pe")
                p.wait("vector", "pe", p.ctr["pe"])
                p.op("vector", lambda e, d=kmx[0:2, kt:kt + 1], ps=pf[6][0:2, :]:
                     e.reduce_max(d, ps, AX), "dv")
                p.wait("tensor", "dv", p.ctr["dv"])
            for n in range(NT):
                sl = slice(n * 512, (n + 1) * 512)
                p.wait("gpsimd", "pe", p.ctr["pe"])
                p.op("gpsimd", lambda e, b=q2b[0][:], a=q_all[hp][:, sl]: e.tensor_mul(b, a, a), "gp")
                p.wait("tensor", "gp", p.ctr["gp"])
                p.op("tensor", lambda e, ps=pf[6][0:2, :], a=q2b[0][:]:
                     e.matmul(ps, c["sel2_bf"][:], a, start=True, stop=True), "pe")
                p.wait("vector", "pe", p.ctr["pe"])
                p.op("vector", lambda e, d=NQ2[:, sl], ps=pf[6][0:2, :]: e.tensor_copy(d, ps), "dv")
                p.wait("tensor", "dv", p.ctr["dv"])
                if n == 0:
                    p.op("vector", lambda e: e.tensor_copy(k0c[:], kbuf[:, 0:1]), "dv")
                p.op("vector", lambda e, b=q2b[1][:], a=q_all[hp][:, sl]:
                     e.tensor_scalar(b, a, k0c[:], None, OP.mult), "dv")
                p.wait("tensor", "dv", p.ctr["dv"])
                p.op("tensor", lambda e, ps=pf[6][0:2, :], a=q2b[1][:]:
                     e.matmul(ps, c["sel2_bf"][:], a, start=True, stop=True), "pe")
                p.wait("vector", "pe", p.ctr["pe"])
                p.op("vector", lambda e, d=D0[:, sl], ps=pf[6][0:2, :]: e.tensor_copy(d, ps), "dv")
                p.wait("tensor", "dv", p.ctr["dv"])
            p.op("vector", lambda e: e.reduce_max(kmx[0:2, 31:32], kmx[0:2, 0:8], AX), "dv")
            p.wait("scalar", "dv", p.ctr["dv"])
            p.op("scalar", lambda e: e.sqrt(kmx[0:2, 31:32], kmx[0:2, 31:32]), "sc")
            p.op("scalar", lambda e: e.sqrt(NQ2, NQ2), "sc")
            p.wait("vector", "sc", p.ctr["sc"])
            p.op("vector", lambda e: e.tensor_scalar(D0, D0, 0.125, None, OP.mult), "dv")
            p.op("vector", lambda e: e.tensor_scalar(BROW, NQ2, kmx[0:2, 31:32], 0.125, OP.mult, OP.mult), "dv")
            p.op("vector", lambda e: e.tensor_scalar(BROW, BROW, -80.0, None, OP.add), "dv")
            p.op("vector", lambda e: e.tensor_tensor(BROW, BROW, D0, OP.max), "dv")
            for qt in range(NT):
                p.op("vector", lambda e, d=sh2[0:2, qt:qt + 1], a=BROW[:, qt * 512:(qt + 1) * 512]:
                     e.reduce_max(d, a, AX), "dv")
            p.op("vector", lambda e: e.tensor_scalar(sh2[0:2, 0:4], sh2[0:2, 0:4], -1.0, None, OP.mult), "dv")
            nsh_done = p.ctr["dv"]
            p.wait("tensor", "dv", nsh_done)
            p.wait("vector", "sc", p.ctr["sc"])
            for h in range(2):
                for qt in range(NT):
                    p.op("tensor", lambda e, ps=pf[6][0:128, 0:1],
                         l=c["h01_f"][:, h * 128:(h + 1) * 128], r=sh2[0:2, qt:qt + 1]:
                         e.matmul(ps, l, r, start=True, stop=True), "pe")
                    p.wait("vector", "pe", p.ctr["pe"])
                    p.op("vector", lambda e, d=biasb[:, h * 4 + qt: h * 4 + qt + 1], ps=pf[6][0:128, 0:1]:
                         e.tensor_copy(d, ps), "dv")
                    p.wait("tensor", "dv", p.ctr["dv"])
            for h in range(2):
                for qt in range(NT):
                    nk = NKC[qt]
                    p.op("vector", lambda e, d=biasM[:, h * 88 + ZOF[qt]: h * 88 + ZOF[qt] + nk],
                         a=zvb[:, ZOF[qt]:ZOF[qt] + nk],
                         b=biasb[:, h * 4 + qt: h * 4 + qt + 1].broadcast_to((128, nk)):
                         e.tensor_tensor(d, a, b, OP.add), "dv")
            bias_done = p.ctr["dv"]
            if l1sub == "hp0k":
                dbg_dump(p, c, "dbg_k", kbuf[:], [128, S], BF16)
                dbg_dump(p, c, "dbg_brow", BROW, [2, 2048], F32)
                break
            p.wait("scalar", "dv", bias_done)
            sc_war = [0, 0, 0]
            pt_war = [0, 0, 0]
            av_war = [0, 0]
            for qt in range(NT):
                nkc = NKC[qt]
                umap = UMAP[qt]
                for h in range(2):
                    avps = pf[4 + h]
                    p.wait("tensor", "dv", av_war[h])
                    pend = []

                    def emit_av(kc, j, rsem, rtgt, last, avps=avps, h=h):
                        p.wait("tensor", rsem, rtgt)
                        p.op("tensor", lambda e, ps=avps[0:65, 0:512],
                             l=vtm[kc][:, h * 65: h * 65 + 65], r=ptb[j][:],
                             s=(kc == 0), t=last:
                             e.matmul(ps, l, r, start=s, stop=t), "pe")
                        pt_war[j] = p.ctr["pe"]
                    for kc in range(nkc):
                        j = kc % 3
                        sps = pf[j]
                        pt = ptb[j][:]
                        p.wait("tensor", "sc", sc_war[j])
                        p.op("tensor", lambda e, ps=sps[:], l=kbuf[h * 64:(h + 1) * 64, kc * 128:(kc + 1) * 128],
                             r=q_all[hp][h * 64:(h + 1) * 64, qt * 512:(qt + 1) * 512], tp=(h * 64, 0):
                             e.matmul(ps, l, r, start=True, stop=True, tile_position=tp), "pe")
                        mmt = p.ctr["pe"]
                        if len(pend) >= 2:
                            emit_av(*pend.pop(0), False)
                        p.wait("scalar", "pe", mmt)
                        p.wait("scalar", "pe", pt_war[j])
                        p.op("scalar", lambda e, d=pt[:], ps=sps[:],
                             b=biasM[:, h * 88 + ZOF[qt] + kc: h * 88 + ZOF[qt] + kc + 1]:
                             e.activation(d, ps, AF.Exp, bias=b, scale=0.125), "sc")
                        sc_war[j] = p.ctr["sc"]
                        if kc in umap:
                            u = umap.index(kc)
                            moff = ((qt // 2) * 8 + u) * 512
                            p.wait("vector", "sc", p.ctr["sc"])
                            p.op("vector", lambda e, d=pt[:], m=masks[:, moff:moff + 512]:
                                 e.tensor_tensor(d, d, m, OP.mult), "dv")
                            pend.append((kc, j, "dv", p.ctr["dv"]))
                        else:
                            pend.append((kc, j, "sc", p.ctr["sc"]))
                    while pend:
                        emit_av(*pend.pop(0), len(pend) == 0)
                    avt = p.ctr["pe"]
                    p.wait("vector", "pe", avt)
                    lb = c["rowpool"][0:1, h * 2048 + qt * 512: h * 2048 + (qt + 1) * 512]
                    p.op("vector", lambda e, d=lb, ps=avps[64:65, 0:512]: e.tensor_copy(d, ps),
                         "dv", nowait=True)
                    p.op("vector", lambda e, d=xh[hp][h * 64:(h + 1) * 64, qt * 512:(qt + 1) * 512],
                         ps=avps[0:64, 0:512]: e.tensor_copy(d, ps), "dv")
                    av_war[h] = p.ctr["dv"]
                    p.wait("tensor", "dv", p.ctr["dv"])
            p.wait("tensor", "dv", p.ctr["dv"])
            for qt in range(NT):
                p.op("tensor", lambda e, r=c["rowpool"][0:1, qt * 512:(qt + 1) * 512]:
                     e.matmul(pf[6][0:64, :], c["onesr_f"][0:1, 0:64], r, start=True, stop=True), "pe")
                p.op("tensor", lambda e, r=c["rowpool"][0:1, 2048 + qt * 512: 2048 + (qt + 1) * 512]:
                     e.matmul(pf[6][64:128, :], c["onesr_f"][0:1, 0:64], r, start=True, stop=True), "pe")
                p.wait("vector", "pe", p.ctr["pe"])
                p.op("vector", lambda e: e.reciprocal(linb[:], pf[6][0:128, :]), "dv")
                p.op("vector", lambda e, d=xh[hp][:, qt * 512:(qt + 1) * 512]:
                     e.tensor_tensor(d, d, linb[:], OP.mult), "dv")
                p.wait("tensor", "dv", p.ctr["dv"])
            if l1sub == "hp0":
                dbg_dump(p, c, "dbg_o0", xh[0][:], [128, T], BF16)
        if l1sub == "oall":
            for i in range(NCH):
                dbg_dump(p, c, f"dbg_o{i}", xh[i][:], [128, T], BF16)
        p.barrier()
    xt2 = [p.sb(f"x2_{k}", [128, T], F32) for k in range(NCH)]
    c["x"] = [t[:] for t in xt2]
    with contextlib.ExitStack() as sc2:
        wo_s = [p.sb(f"wos{i}", [128, 1024], BF16, sc2) for i in range(2)]
        tmpE = [p.sb(f"l1te{i}", [128, 512], F32, sc2) for i in range(2)]
        for k in range(NCH):
            p.dma("sync", c["x"][k], x_reload[k * 128:(k + 1) * 128, :], "lds")
        x_rdy = p.ctr["lds"]
        p.wait("gpsimd", "lds", x_rdy)

        def load_wo(oc):
            buf = wo_s[oc % 2]
            for k in range(NCH):
                p.dma("gpsimd", buf[:, k * 128:(k + 1) * 128],
                      in_wo[k * 128:(k + 1) * 128, oc * 128:(oc + 1) * 128], "ldg")
            return p.ctr["ldg"]
        wo_rdy = [load_wo(0)]
        wo_war = [0, 0]
        nwar = [0, 0]
        gwar = [0, 0]
        for oc in ([] if os.environ.get("WO_XONLY") else range(NCH)):
            if oc + 1 < NCH:
                p.wait("gpsimd", "pe", wo_war[(oc + 1) % 2])
                wo_rdy.append(load_wo(oc + 1))
            p.wait("tensor", "ldg", wo_rdy[oc])
            for n in range(NT):
                sl = slice(n * 512, (n + 1) * 512)
                ps = pf[n % 2]
                p.wait("tensor", "dv", nwar[n % 2])
                for k in range(NCH):
                    p.op("tensor", lambda e, ps=ps[:], w=wo_s[oc % 2][:, k * 128:(k + 1) * 128],
                         a=xh[k][:, sl], s=(k == 0), t=(k == NCH - 1):
                         e.matmul(ps, w, a, start=s, stop=t), "pe")
                omm = p.ctr["pe"]
                p.wait("vector", "pe", omm)
                p.wait("vector", "gp", gwar[n % 2])
                p.op("vector", lambda e, ps=ps[:], t=tmpE[n % 2][:]: e.tensor_copy(t, ps), "dv")
                nwar[n % 2] = p.ctr["dv"]
                p.wait("gpsimd", "dv", p.ctr["dv"])
                xoc = c["x"][oc]
                if os.environ.get("WO_NORES"):
                    p.op("gpsimd", lambda e, d=xoc[:, sl], t=tmpE[n % 2][:]: e.tensor_copy(d, t), "gp")
                else:
                    p.op("gpsimd", lambda e, d=xoc[:, sl], t=tmpE[n % 2][:]: e.tensor_add(d, d, t), "gp")
                gwar[n % 2] = p.ctr["gp"]
            wo_war[oc % 2] = p.ctr["pe"]
        p.wait("tensor", "gp", p.ctr["gp"])
        p.wait("vector", "gp", p.ctr["gp"])
        p.wait("sync", "gp", p.ctr["gp"])



def emit_l1_moe(p, c):
    pf, xh = c["pf"], c["xhat"]
    in_wr = p.inp("moe_wr", [D, EXP], F32).rearrange("(k p) f -> p k f", p=128)
    in_eb = p.inp("moe_ebias", [128, EXP], F32)
    in_wg = p.inp("moe_wg", [EXP, D, FMOE], F32)
    in_wu = p.inp("moe_wu", [EXP, D, FMOE], F32)
    in_wd = p.inp("moe_wd", [EXP, FMOE, D], F32)
    in_wgs = p.inp("moe_wgs", [D, FMOE], F32).rearrange("(k p) f -> p k f", p=128)
    in_wus = p.inp("moe_wus", [D, FMOE], F32).rearrange("(k p) f -> p k f", p=128)
    in_wds = p.inp("moe_wds", [FMOE, D], F32).rearrange("(k p) f -> p k f", p=128)
    with contextlib.ExitStack() as sc:
        wr_sb = p.sb("wr", [128, 64], BF16, sc)
        eb_sb = p.sb("eb", [128, 8], F32, sc)
        lfm = p.sb("lfm", [8, T], BF16, sc)
        ltm = p.sb("ltm", [128, 128], F32, sc)
        sctm = p.sb("sctm", [128, 128], F32, sc)
        val = p.sb("val", [128, 128], F32, sc)
        mA = p.sb("mA", [128, 128], F32, sc)
        mB = p.sb("mB", [128, 128], F32, sc)
        cs = p.sb("cs", [128, 128], F32, sc)
        m12 = p.sb("m12", [128, 32], F32, sc)
        ninf = p.sb("ninf", [128, 128], F32, sc)
        g_fm = p.sb("gfm", [8, T], F32, sc)
        wg_sb = p.sb("wg", [128, NCH * FMOE], BF16, sc)
        wu_sb = p.sb("wu", [128, NCH * FMOE], BF16, sc)
        wd_sb = p.sb("wd", [128, FCH * D], BF16, sc)
        gate_sb = [p.sb(f"gate{i}", [128, 512], BF16, sc) for i in range(FCH)]
        h_sb = [p.sb(f"hs{i}", [128, 512], BF16, sc) for i in range(FCH)]
        Gb = [p.sb("Gb0", [128, T], F32, sc)]
        tmpE = [p.sb(f"m_te{i}", [128, 512], F32, sc) for i in range(2)]
        p.dma("gpsimd", wr_sb[:].rearrange("p (k f) -> p k f", f=8), in_wr, "ldg")
        p.dma("sync", eb_sb[:], in_eb, "lds")
        r_rdy = p.ctr["ldg"]
        e_rdy = p.ctr["lds"]
        p.wait("tensor", "ldg", r_rdy)
        for n in range(NT):
            sl = slice(n * 512, (n + 1) * 512)
            ps = pf[0][0:8, :]
            for k in range(NCH):
                p.op("tensor", lambda e, ps=ps, w=wr_sb[:, k * 8:(k + 1) * 8], a=xh[k][:, sl],
                     s=(k == 0), t=(k == NCH - 1): e.matmul(ps, w, a, start=s, stop=t), "pe")
            p.wait("vector", "pe", p.ctr["pe"])
            p.op("vector", lambda e, d=lfm[0:8, sl], ps=ps: e.tensor_copy(d, ps), "dv")
            p.wait("tensor", "dv", p.ctr["dv"])
        for tc in range(16):
            p.op("tensor", lambda e, ps=c["pb16"][0:128, tc * 8:(tc + 1) * 8], a=lfm[0:8, tc * 128:(tc + 1) * 128]:
                 e.transpose(ps, a, c["id_bf"][0:8, 0:8]), "pe")
        p.wait("vector", "pe", p.ctr["pe"])
        p.op("vector", lambda e: e.tensor_copy(ltm[:], c["pb16"][0:128, 0:128]), "dv")
        p.wait("tensor", "dv", p.ctr["dv"])
        p.wait("scalar", "dv", p.ctr["dv"])
        p.op("scalar", lambda e: e.activation(sctm[:], ltm[:], AF.Sigmoid), "sc")
        p.wait("vector", "sc", p.ctr["sc"])
        p.wait("vector", "lds", e_rdy)
        v3 = val[:].rearrange("p (a b) -> p a b", b=8)
        s3 = sctm[:].rearrange("p (a b) -> p a b", b=8)
        p.op("vector", lambda e: e.tensor_tensor(
            v3, s3, eb_sb[:].unsqueeze(1).broadcast_to((128, 16, 8)), OP.add), "dv")
        p.op("vector", lambda e: e.memset(ninf[:], -1e30), "dv")
        p.op("vector", lambda e: e.tensor_reduce(
            m12[:, 0:16].rearrange("p (a o) -> p a o", o=1), v3, AX, OP.max), "dv")
        p.op("vector", lambda e: e.tensor_tensor(
            mA[:].rearrange("p (a b) -> p a b", b=8), v3,
            m12[:, 0:16].rearrange("p (a o) -> p a o", o=1).broadcast_to((128, 16, 8)), OP.is_ge), "dv")
        for tc in range(16):
            p.op("vector", lambda e, d=cs[:, tc * 8:(tc + 1) * 8], a=mA[:, tc * 8:(tc + 1) * 8]:
                 e.tensor_tensor_scan(d, a, a, 0.0, OP.add, OP.bypass), "dv", nowait=(tc < 15))
        p.op("vector", lambda e: e.tensor_scalar(cs[:], cs[:], 1.0, None, OP.is_equal), "dv")
        p.op("vector", lambda e: e.tensor_tensor(mA[:], mA[:], cs[:], OP.logical_and), "dv")
        p.op("vector", lambda e: e.tensor_scalar(cs[:], mA[:], 1e30, None, OP.mult), "dv")
        p.op("vector", lambda e: e.tensor_tensor(val[:], val[:], cs[:], OP.subtract), "dv")
        p.op("vector", lambda e: e.tensor_reduce(
            m12[:, 16:32].rearrange("p (a o) -> p a o", o=1), v3, AX, OP.max), "dv")
        p.op("vector", lambda e: e.tensor_tensor(
            mB[:].rearrange("p (a b) -> p a b", b=8), v3,
            m12[:, 16:32].rearrange("p (a o) -> p a o", o=1).broadcast_to((128, 16, 8)), OP.is_ge), "dv")
        for tc in range(16):
            p.op("vector", lambda e, d=cs[:, tc * 8:(tc + 1) * 8], a=mB[:, tc * 8:(tc + 1) * 8]:
                 e.tensor_tensor_scan(d, a, a, 0.0, OP.add, OP.bypass), "dv", nowait=(tc < 15))
        p.op("vector", lambda e: e.tensor_scalar(cs[:], cs[:], 1.0, None, OP.is_equal), "dv")
        p.op("vector", lambda e: e.tensor_tensor(mB[:], mB[:], cs[:], OP.logical_and), "dv")
        p.op("vector", lambda e: e.tensor_tensor(mA[:], mA[:], mB[:], OP.add), "dv")
        p.op("vector", lambda e: e.tensor_tensor(val[:], sctm[:], mA[:], OP.mult), "dv")
        p.op("vector", lambda e: e.tensor_reduce(
            m12[:, 0:16].rearrange("p (a o) -> p a o", o=1), v3, AX, OP.add), "dv")
        p.op("vector", lambda e: e.tensor_scalar(m12[:, 0:16], m12[:, 0:16], 1e-20, None, OP.add), "dv")
        p.op("vector", lambda e: e.reciprocal(m12[:, 0:16], m12[:, 0:16]), "dv")
        p.op("vector", lambda e: e.tensor_tensor(
            v3, v3, m12[:, 0:16].rearrange("p (a o) -> p a o", o=1).broadcast_to((128, 16, 8)), OP.mult), "dv")
        g_done = p.ctr["dv"]
        p.wait("tensor", "dv", g_done)
        for tc in range(16):
            p.op("tensor", lambda e, ps=pf[2][0:8, 0:128], a=val[:, tc * 8:(tc + 1) * 8]:
                 e.transpose(ps, a, c["id_f"][:]), "pe")
            p.wait("vector", "pe", p.ctr["pe"])
            p.op("vector", lambda e, d=g_fm[0:8, tc * 128:(tc + 1) * 128], ps=pf[2][0:8, 0:128]:
                 e.tensor_copy(d, ps), "dv")
            p.wait("tensor", "dv", p.ctr["dv"])
        p.barrier()
        def load_expert(e_idx):
            if e_idx == 0:
                p.dma("gpsimd", wg_sb[:].rearrange("p (k f) -> p k f", f=FMOE), in_wgs, "ldg")
                p.dma("gpsimd", wu_sb[:].rearrange("p (k f) -> p k f", f=FMOE), in_wus, "ldg")
                p.dma("gpsimd", wd_sb[:].rearrange("p (k f) -> p k f", f=D), in_wds, "ldg")
            else:
                ee = e_idx - 1
                p.dma("gpsimd", wg_sb[:].rearrange("p (k f) -> p k f", f=FMOE), in_wg[ee].rearrange("(k p) f -> p k f", p=128), "ldg")
                p.dma("gpsimd", wu_sb[:].rearrange("p (k f) -> p k f", f=FMOE), in_wu[ee].rearrange("(k p) f -> p k f", p=128), "ldg")
                p.dma("gpsimd", wd_sb[:].rearrange("p (k f) -> p k f", f=D), in_wd[ee].rearrange("(k p) f -> p k f", p=128), "ldg")
            return p.ctr["ldg"]

        w_rdy = load_expert(0)
        for eix in range(9):
            routed = eix > 0
            Gt = Gb[0]
            if routed:
                ee = eix - 1
                p.wait("tensor", "dv", p.ctr["dv"])
                for n in range(NT):
                    sl = slice(n * 512, (n + 1) * 512)
                    p.op("tensor", lambda e, ps=pf[6][0:128, :], l=c["esel_f"][:, ee * 128:(ee + 1) * 128],
                         r=g_fm[0:8, sl]: e.matmul(ps, l, r, start=True, stop=True), "pe")
                    p.wait("vector", "pe", p.ctr["pe"])
                    p.op("vector", lambda e, d=Gt[:, sl], ps=pf[6][0:128, :]: e.tensor_copy(d, ps), "dv")
                    p.wait("tensor", "dv", p.ctr["dv"])
            p.wait("tensor", "ldg", w_rdy)
            for n in range(NT):
                sl = slice(n * 512, (n + 1) * 512)
                for oc in range(FCH):
                    psA = pf[oc % 2]
                    psB = pf[2 + oc % 2]
                    p.wait("tensor", "sc", c.get("moe_scwar", 0) if oc < 2 and n == 0 and eix == 0 else c.get(f"moe_g{oc%2}", 0))
                    p.wait("tensor", "dv", c.get(f"moe_h{oc%2}", 0))
                    for k in range(NCH):
                        p.op("tensor", lambda e, ps=psA[:], w=wg_sb[:, k * FMOE + oc * 128: k * FMOE + oc * 128 + 128],
                             a=xh[k][:, sl], s=(k == 0), t=(k == NCH - 1):
                             e.matmul(ps, w, a, start=s, stop=t), "pe")
                    gm = p.ctr["pe"]
                    for k in range(NCH):
                        p.op("tensor", lambda e, ps=psB[:], w=wu_sb[:, k * FMOE + oc * 128: k * FMOE + oc * 128 + 128],
                             a=xh[k][:, sl], s=(k == 0), t=(k == NCH - 1):
                             e.matmul(ps, w, a, start=s, stop=t), "pe")
                    um = p.ctr["pe"]
                    p.wait("scalar", "pe", gm)
                    p.op("scalar", lambda e, d=gate_sb[oc][:], ps=psA[:]:
                         e.activation(d, ps, AF.Silu), "sc")
                    c[f"moe_g{oc%2}"] = p.ctr["sc"]
                    p.wait("vector", "pe", um)
                    p.wait("vector", "sc", p.ctr["sc"])
                    p.op("vector", lambda e, d=h_sb[oc][:], ps=psB[:], g=gate_sb[oc][:]:
                         e.scalar_tensor_tensor(d, ps, 1.0, g, OP.mult, OP.mult), "dv")
                    c[f"moe_h{oc%2}"] = p.ctr["dv"]
                if n == NT - 1:
                    c["moe_last_gu"] = p.ctr["pe"]
                h_done = p.ctr["dv"]
                p.wait("tensor", "dv", h_done)
                if n == 0:
                    p.wait("tensor", "ldg", c.get("moe_wd_rdy", 0))
                for oc2 in range(NCH):
                    psC = pf[4 + oc2 % 2]
                    p.wait("tensor", "dv", c.get(f"moe_d{oc2%2}", 0))
                    p.wait("tensor", "gp", c.get(f"moe_dg{oc2%2}", 0))
                    for kf in range(FCH):
                        p.op("tensor", lambda e, ps=psC[:], w=wd_sb[:, kf * D + oc2 * 128: kf * D + oc2 * 128 + 128],
                             a=h_sb[kf][:], s=(kf == 0), t=(kf == FCH - 1):
                             e.matmul(ps, w, a, start=s, stop=t), "pe")
                    dm = p.ctr["pe"]
                    p.wait("vector", "pe", dm)
                    xoc = c["x"][oc2]
                    if routed:
                        p.wait("vector", "gp", c.get(f"moe_dg{oc2%2}", 0))
                        p.op("vector", lambda e, t=tmpE[oc2 % 2][:], ps=psC[:], g=Gt[:, sl]:
                             e.tensor_tensor(t, ps, g, OP.mult), "dv")
                        c[f"moe_d{oc2%2}"] = p.ctr["dv"]
                        p.wait("gpsimd", "dv", p.ctr["dv"])
                        p.op("gpsimd", lambda e, d=xoc[:, sl], t=tmpE[oc2 % 2][:]:
                             e.tensor_add(d, d, t), "gp")
                        c[f"moe_dg{oc2%2}"] = p.ctr["gp"]
                    else:
                        p.op("vector", lambda e, d=xoc[:, sl], ps=psC[:]:
                             e.tensor_tensor(d, d, ps, OP.add), "dv")
                        c[f"moe_d{oc2%2}"] = p.ctr["dv"]
            if eix + 1 < 9:
                p.wait("gpsimd", "pe", c["moe_last_gu"])
                ee2 = eix
                p.dma("gpsimd", wg_sb[:].rearrange("p (k f) -> p k f", f=FMOE),
                      in_wg[ee2].rearrange("(k p) f -> p k f", p=128), "ldg")
                p.dma("gpsimd", wu_sb[:].rearrange("p (k f) -> p k f", f=FMOE),
                      in_wu[ee2].rearrange("(k p) f -> p k f", p=128), "ldg")
                w_rdy = p.ctr["ldg"]
                p.wait("gpsimd", "pe", p.ctr["pe"])
                p.dma("gpsimd", wd_sb[:].rearrange("p (k f) -> p k f", f=D),
                      in_wd[ee2].rearrange("(k p) f -> p k f", p=128), "ldg")
                c["moe_wd_rdy"] = p.ctr["ldg"]
        p.wait("tensor", "gp", p.ctr["gp"])
        p.wait("vector", "gp", p.ctr["gp"])
        p.wait("sync", "gp", p.ctr["gp"])


def build_l1(core, upto=4):
    p = Prog()
    for s in ["lds", "ldg", "lda", "pe", "dv", "sc", "gp", "st", "cc0", "bar"]:
        p.sem(s)
    c = make_common(p)
    x_scope = contextlib.ExitStack()
    xt = [p.sb(f"x{k}", [128, T], F32, x_scope) for k in range(NCH)]
    c["x"] = [t[:] for t in xt]
    x_in = p.inp("x_in", [D, T], F32)
    x_out = p.outp("x_out", [D, T], F32)
    emit_load_consts(p, c)
    for k in range(NCH):
        p.dma("sync", c["x"][k], x_in[k * 128:(k + 1) * 128, :], "lds")
    for eng in ENGS:
        p.wait(eng, "lds", p.ctr["lds"])
    dump_xhat = False
    if upto >= 1:
        emit_ln(p, c)
        p.barrier()
        dump_xhat = (upto == 1)
    if upto >= 2:
        emit_l1_attn(p, c, core, x_scope, x_in)
        p.barrier()
    if upto >= 3:
        emit_ln(p, c)
        p.barrier()
        dump_xhat = (upto == 3)
    if upto >= 4:
        emit_l1_moe(p, c)
        p.barrier()
    if dump_xhat:
        for k in range(NCH):
            p.dma("gpsimd", x_out[k * 128:(k + 1) * 128, :], c["xhat"][k][:], "st")
    else:
        for k in range(NCH):
            p.dma("sync", x_out[k * 128:(k + 1) * 128, :], c["x"][k], "st")
    p.wait("sync", "st", p.ctr["st"])
    p.wait("gpsimd", "st", p.ctr["st"])
    if upto < 2:
        x_scope.close()
    return p.finalize()


def build_fused():
    p = Prog()
    for sm in ["lds", "ldg", "lda", "pe", "dv", "sc", "gp", "st", "sp", "cc0", "bar"]:
        p.sem(sm)
    c = make_common(p)
    x_scope = contextlib.ExitStack()
    xt = [p.sb(f"x{k}", [128, T], F32, x_scope) for k in range(NCH)]
    c["x"] = [t[:] for t in xt]
    x_in = p.inp("x_in", [D, T], F32)
    x_out = p.outp("x_out", [D, T], F32)
    x_spill = p.idram("x_spill", [D, T], F32)
    emit_load_consts(p, c)
    for k in range(NCH):
        p.dma("sync", c["x"][k], x_in[k * 128:(k + 1) * 128, :], "lds")
    for eng in ENGS:
        p.wait(eng, "lds", p.ctr["lds"])
    emit_ln(p, c)
    p.barrier()
    emit_l0_attn(p, c)
    p.barrier()
    emit_ln(p, c)
    p.barrier()
    emit_l0_ffn(p, c)
    p.barrier()
    emit_ln(p, c)
    p.barrier()
    for k in range(NCH):
        p.dma("sync", x_spill[k * 128:(k + 1) * 128, :], c["x"][k], "sp")
    p.wait("sync", "sp", p.ctr["sp"])
    p.barrier()
    emit_l1_attn(p, c, 0, x_scope, x_spill)
    p.barrier()
    emit_ln(p, c)
    p.barrier()
    emit_l1_moe(p, c)
    p.barrier()
    for k in range(NCH):
        p.dma("sync", x_out[k * 128:(k + 1) * 128, :], c["x"][k], "st")
    p.wait("sync", "st", p.ctr["st"])
    return p.finalize()


def rope_tables():
    inv = 10000.0 ** (-np.arange(32) / 32.0)
    ang = np.arange(S)[None, :] * inv[:, None]
    cs, sn = np.cos(ang), np.sin(ang)
    C4 = np.concatenate([cs, cs, cs, cs], axis=0)
    S4 = np.concatenate([-sn, sn, -sn, sn], axis=0)
    return C4.astype(BF), S4.astype(BF)


def prep_core_l1(inputs, core, x_mid):
    d = dict(host_consts())
    d["x_in"] = x_mid
    d["wq"] = np.ascontiguousarray(inputs["L1_attn_W_q"])
    d["wdkv"] = np.ascontiguousarray(inputs["L1_attn_W_dkv"])
    d["wuk"] = np.ascontiguousarray(inputs["L1_attn_W_uk"])
    d["wuv"] = np.ascontiguousarray(inputs["L1_attn_W_uv"])
    d["wo"] = np.ascontiguousarray(inputs["L1_attn_W_o"])
    d["l1_gam"] = np.ascontiguousarray(np.asarray(inputs["L1_attn_ln_gamma"]).reshape(2, 128).T)
    d["l1_bet"] = np.ascontiguousarray(np.asarray(inputs["L1_attn_ln_beta"]).reshape(2, 128).T)
    C4, S4 = rope_tables()
    d["ropeC"], d["ropeS"] = C4, S4
    sg = segs(core)
    d["qropeC"] = np.ascontiguousarray(np.concatenate([C4[:, a:a + l] for (a, l) in sg], axis=1))
    d["qropeS"] = np.ascontiguousarray(np.concatenate([S4[:, a:a + l] for (a, l) in sg], axis=1))
    Q0s = [0, 512, 3072, 3584] if core % 2 == 0 else [1024, 1536, 2048, 2560]
    NKC = [12, 16, 28, 32]
    UMAP = {0: [0, 1, 2, 3, 8, 9, 10, 11], 1: [4, 5, 6, 7, 12, 13, 14, 15],
            2: [16, 17, 18, 19, 24, 25, 26, 27], 3: [20, 21, 22, 23, 28, 29, 30, 31]}
    ZOF = [0, 12, 28, 56]
    zv = np.zeros((88,), np.float32)
    for qt in range(4):
        for kc in range(NKC[qt]):
            if kc * 128 - Q0s[qt] >= 512:
                zv[ZOF[qt] + kc] = -1e30
    d["zvb"] = np.ascontiguousarray(np.broadcast_to(zv[None, :], (128, 88)).copy())
    masks = np.zeros((128, 2 * 8 * 512), np.float32)
    kk = np.arange(128)[:, None]
    jj = np.arange(512)[None, :]
    for g in range(2):
        qt = 2 * g
        for u in range(8):
            kc = UMAP[qt][u]
            m = ((kc * 128 + kk) <= (Q0s[qt] + jj)).astype(np.float32)
            masks[:, (g * 8 + u) * 512:(g * 8 + u + 1) * 512] = m
    d["cmasks"] = masks.astype(BF)
    d["moe_wr"] = np.ascontiguousarray(inputs["L1_ffn_W_gate_router"])
    d["moe_ebias"] = np.ascontiguousarray(
        np.broadcast_to(np.asarray(inputs["L1_ffn_expert_bias"])[None, :], (128, 8)).copy())
    d["moe_wg"] = np.ascontiguousarray(inputs["L1_ffn_Wg"])
    d["moe_wu"] = np.ascontiguousarray(inputs["L1_ffn_Wu"])
    d["moe_wd"] = np.ascontiguousarray(inputs["L1_ffn_Wd"])
    d["moe_wgs"] = np.ascontiguousarray(inputs["L1_ffn_Wg_s"])
    d["moe_wus"] = np.ascontiguousarray(inputs["L1_ffn_Wu_s"])
    d["moe_wds"] = np.ascontiguousarray(inputs["L1_ffn_Wd_s"])
    return d



def build_l0(core, upto=4):
    p = Prog()
    for s in ["lds", "ldg", "lda", "pe", "dv", "sc", "gp", "st", "cc0", "bar"]:
        p.sem(s)
    c = make_common(p)
    x_scope = contextlib.ExitStack()
    xt = [p.sb(f"x{k}", [128, T], F32, x_scope) for k in range(NCH)]
    c["x"] = [t[:] for t in xt]
    x_in = p.inp("x_in", [D, T], F32)
    x_out = p.outp("x_out", [D, T], F32)
    emit_load_consts(p, c)
    for k in range(NCH):
        p.dma("sync", c["x"][k], x_in[k * 128:(k + 1) * 128, :], "lds")
    for eng in ENGS:
        p.wait(eng, "lds", p.ctr["lds"])
    dump_xhat = False
    if upto >= 1:
        emit_ln(p, c)
        p.barrier()
        dump_xhat = (upto == 1)
    if upto >= 2:
        emit_l0_attn(p, c)
        p.barrier()
    if upto >= 3:
        emit_ln(p, c)
        p.barrier()
        dump_xhat = (upto == 3)
    if upto >= 4:
        emit_l0_ffn(p, c)
        p.barrier()
    if dump_xhat:
        for k in range(NCH):
            p.dma("gpsimd", x_out[k * 128:(k + 1) * 128, :], c["xhat"][k][:], "st")
    else:
        for k in range(NCH):
            p.dma("sync", x_out[k * 128:(k + 1) * 128, :], c["x"][k], "st")
    p.wait("sync", "st", p.ctr["st"])
    p.wait("gpsimd", "st", p.ctr["st"])
    x_scope.close()
    return p.finalize()


def prep_core_l0(inputs, core):
    b = core // 2
    x = np.asarray(inputs["L0_attn_x"])
    xs = np.concatenate([x[b, s:s + l, :] for (s, l) in segs(core)], axis=0)
    d = dict(host_consts())
    d["x_in"] = np.ascontiguousarray(xs.T)
    d["w_qkv"] = np.ascontiguousarray(inputs["L0_attn_w_qkv"])
    d["w_out"] = np.ascontiguousarray(inputs["L0_attn_w_out"])
    d["b_out_col"] = np.ascontiguousarray(np.asarray(inputs["L0_attn_b_out"]).reshape(8, 128).T)
    d["ffn_wT"] = np.ascontiguousarray(np.asarray(inputs["L0_ffn_weight"]).T)
    d["ffn_alpha"] = np.ascontiguousarray(np.asarray(inputs["L0_ffn_alpha"]).reshape(1, 1))
    d["ffn_bias_col"] = np.ascontiguousarray(np.asarray(inputs["L0_ffn_bias"]).reshape(8, 128).T)
    return d


import os
def assemble(results, cores):
    out = np.zeros((4, S, D), np.float32)
    for cc in cores:
        b = cc // 2
        xo = results[cc]["x_out"]
        pos = 0
        for (s, l) in segs(cc):
            out[b, s:s + l, :] = xo[:, pos:pos + l].T
            pos += l
    return out


def kernel(**inputs):
    cores = list(range(8))
    if os.environ.get("UNFUSED", "") != "1":
        maps = []
        for cc in cores:
            m = prep_core_l0(inputs, cc)
            m1 = prep_core_l1(inputs, cc, None)
            del m1["x_in"]
            m.update(m1)
            maps.append(m)
        nc = build_fused()
        res = run_bass_kernel_spmd(nc, maps, core_ids=cores)
        return assemble(res.results, cores)
    stage = os.environ.get("STAGE", "all")
    upto0 = int(os.environ.get("UPTO0", os.environ.get("UPTO", "4")))
    upto1 = int(os.environ.get("UPTO1", os.environ.get("UPTO", "4")))
    if stage in ("all", "l0"):
        maps0 = [prep_core_l0(inputs, cc) for cc in cores]
        nc0 = build_l0(0, upto=upto0)
        res0 = run_bass_kernel_spmd(nc0, maps0, core_ids=cores)
        if os.environ.get("DBG"):
            kernel.last_results = res0.results
        xmid = [np.ascontiguousarray(res0.results[cc]["x_out"]) for cc in cores]
        if stage == "l0":
            return assemble(res0.results, cores)
    else:
        xmid = None
    maps1 = [prep_core_l1(inputs, cc, xmid[cc]) for cc in cores]
    nc1 = build_l1(0, upto=upto1)
    res1 = run_bass_kernel_spmd(nc1, maps1, core_ids=cores)
    if os.environ.get("DBG"):
        kernel.last_results = res1.results
    return assemble(res1.results, cores)


# version 13 (final)
# speedup vs baseline: 1.0372x; 1.0063x over previous
import contextlib
import os
import numpy as np
import ml_dtypes
import concourse.bass as bass
import concourse.mybir as mybir
from concourse.bass_utils import run_bass_kernel_spmd
from concourse.alu_op_type import AluOpType as OP

F32 = mybir.dt.float32
BF16 = mybir.dt.bfloat16
AF = mybir.ActivationFunctionType
AX = mybir.AxisListType.X
BF = ml_dtypes.bfloat16

D = 1024
T = 2048
S = 4096
NCH = 8
NT = 4
LAT = 256
EXP = 8
FMOE = 768
FCH = 6
PAIRS = [[0, 1], [2, 3], [4, 5], [6, 7]]

ENGS = ("tensor", "vector", "scalar", "gpsimd", "sync")


def segs(core):
    if core % 2 == 0:
        return [(0, 1024), (3072, 1024)]
    return [(1024, 2048)]


class Prog:
    def __init__(self):
        self.nc = bass.Bass("TRN2", target_bir_lowering=False)
        self.q = {e: [] for e in ENGS}
        self.es = contextlib.ExitStack()
        self.sems = {}
        self.ctr = {}

    def sem(self, name):
        if name not in self.sems:
            self.sems[name] = self.es.enter_context(self.nc.semaphore(name))
            self.ctr[name] = 0
        return self.sems[name]

    _uid = 0

    def sb(self, name, shape, dt, scope=None):
        Prog._uid += 1
        return (scope or self.es).enter_context(
            self.nc.sbuf_tensor(f"{name}_u{Prog._uid}", shape, dt))

    def psum(self, name, shape, dt):
        return self.es.enter_context(self.nc.psum_tensor(name, shape, dt))

    def inp(self, name, shape, dt):
        return self.nc.dram_tensor(name, shape, dt, kind="ExternalInput").ap()

    def outp(self, name, shape, dt):
        return self.nc.dram_tensor(name, shape, dt, kind="ExternalOutput").ap()

    def idram(self, name, shape, dt):
        return self.nc.dram_tensor(name, shape, dt).ap()

    def wait(self, eng, semname, val):
        if val is None or val <= 0:
            return
        s = self.sem(semname)
        self.q[eng].append(lambda e, s=s, v=val: e.wait_ge(s, v))

    SELFWAIT = {"vector", "scalar", "gpsimd"}

    def op(self, eng, fn, inc=None, n=1, nowait=False):
        tgt, s = None, None
        if inc is None and eng in self.SELFWAIT and not nowait:
            inc = {"vector": "dv", "scalar": "sc", "gpsimd": "gp"}[eng]
        if inc is not None:
            s = self.sem(inc)
            self.ctr[inc] += n
            tgt = self.ctr[inc]
        sw = (eng in self.SELFWAIT) and not nowait
        def thunk(e, fn=fn, s=s, n=n, sw=sw, tgt=tgt):
            ins = fn(e)
            if s is not None:
                ins.then_inc(s, n)
                if sw:
                    e.wait_ge(s, tgt)
        self.q[eng].append(thunk)
        return tgt

    def dma(self, eng, out, in_, inc):
        return self.op(eng, lambda e, o=out, i=in_: e.dma_start(o, i), inc, 16)

    def barrier(self):
        s = self.sem("bar")
        self.ctr["bar"] += len(ENGS)
        tgt = self.ctr["bar"]
        for eng in ENGS:
            self.q[eng].append(lambda e, s=s: e.nop().then_inc(s, 1))
            self.q[eng].append(lambda e, s=s, v=tgt: e.wait_ge(s, v))

    def finalize(self):
        with self.nc.Block() as blk:
            for eng in ENGS:
                q = self.q[eng]
                def run(e, q=q):
                    for f in q:
                        f(e)
                getattr(blk, eng)(run)
        self.es.close()
        return self.nc



def make_common(p):
    c = {}
    c["pf"] = [p.psum(f"pf{i}", [128, 512], F32) for i in range(7)]
    c["pb16"] = p.psum("pb16", [128, 1024], BF16)
    for nm, shape, dt in [
        ("id_bf", [128, 128], BF16), ("id_f", [128, 128], F32),
        ("id8_f", [8, 8], F32), ("ones1_bf", [128, 1], BF16),
        ("onesr_f", [1, 128], F32), ("onesc_f", [128, 1], F32),
        ("sel2_bf", [128, 2], BF16),
        ("bsel_f", [2, 128], F32), ("h01_f", [2, 256], F32),
        ("tri_bf", [128, 128], BF16), ("esel_f", [8, 1024], F32),
    ]:
        c[nm] = p.sb(nm, shape, dt)
        c["in_" + nm] = p.inp("c_" + nm, shape, dt)
    c["rowpool"] = p.sb("rowpool", [128, 6144], F32)
    c["xhat"] = [p.sb(f"xh{k}", [128, T], BF16) for k in range(NCH)]
    c["ln_count"] = 0
    c["dbg"] = []
    return c


def dbg_dump(p, c, name, ap, shape, dt):
    if not os.environ.get("DBG"):
        return
    for sname in ("dv", "gp", "sc", "pe"):
        p.wait("sync", sname, p.ctr[sname])
    o = p.outp(name, shape, dt)
    p.dma("sync", o, ap, "st")
    p.wait("sync", "st", p.ctr["st"])


def row1(c, i):
    assert i < 9
    base = (i // 3) * 32
    col = (i % 3) * 2048
    return c["rowpool"][base:base + 1, col:col + 2048]


def row2(c, i):
    assert i == 0
    return c["rowpool"][0:2, 4096:6144]


def host_consts():
    id128 = np.eye(128)
    sel2 = np.zeros((128, 2), np.float32); sel2[0:64, 0] = 1; sel2[64:128, 1] = 1
    bsel = np.zeros((2, 128), np.float32); bsel[0, 0:64] = 1; bsel[1, 64:128] = 1
    h01 = np.zeros((2, 256), np.float32); h01[0, 0:128] = 1; h01[1, 128:256] = 1
    tri = (np.arange(128)[:, None] <= np.arange(128)[None, :]).astype(BF)
    esel = np.zeros((8, 1024), np.float32)
    for e in range(8):
        esel[e, e * 128:(e + 1) * 128] = 1
    return {
        "c_id_bf": id128.astype(BF), "c_id_f": id128.astype(np.float32),
        "c_id8_f": np.eye(8, dtype=np.float32),
        "c_ones1_bf": np.ones((128, 1), BF),
        "c_onesr_f": np.ones((1, 128), np.float32),
        "c_onesc_f": np.ones((128, 1), np.float32),
        "c_sel2_bf": sel2.astype(BF), "c_bsel_f": bsel,
        "c_h01_f": h01, "c_tri_bf": tri, "c_esel_f": esel,
    }


def emit_load_consts(p, c):
    for nm in ["id_bf", "id_f", "id8_f", "ones1_bf", "onesr_f", "onesc_f",
               "sel2_bf", "bsel_f", "h01_f", "tri_bf", "esel_f"]:
        p.dma("sync", c[nm][:], c["in_" + nm], "lds")
    c["consts_ready"] = p.ctr["lds"]



def emit_ln(p, c):
    c["ln_count"] += 1
    x, xh, pf = c["x"], c["xhat"], c["pf"]
    with contextlib.ExitStack() as sc:
        rb = p.sb("ln_rb", [128, T], F32, sc)
        mrb = p.sb("ln_mrb", [128, T], F32, sc)
        x2b = [p.sb(f"ln_x2_{i}", [128, 512], BF16, sc) for i in range(2)]
        ntmp = [p.sb(f"ln_nt_{i}", [128, 512], F32, sc) for i in range(2)]
        RR, RMR = row1(c, 0), row1(c, 1)
        RA, RB_, RM = row1(c, 3), row1(c, 4), row1(c, 5)
        for k in range(NCH):
            xk = x[k]
            p.op("vector", lambda e, a=xh[k][:], b=xk: e.tensor_copy(a, b), "dv", nowait=(k < NCH - 1))
        xb_done = p.ctr["dv"]
        p.wait("gpsimd", "dv", xb_done)
        war = [0, 0]
        x2war = [0, 0]
        for n in range(NT):
            sl = slice(n * 512, (n + 1) * 512)
            psA = pf[(n % 2) * 2][0:1, :]
            psB = pf[(n % 2) * 2 + 1][0:1, :]
            x2tgt = []
            for k in range(NCH):
                j = (n * NCH + k) % 2
                p.wait("gpsimd", "pe", x2war[j])
                p.op("gpsimd", lambda e, b=x2b[j][:], a=xh[k][:, sl]: e.tensor_mul(b, a, a), "gp", nowait=True)
                x2tgt.append(p.ctr["gp"])
            p.wait("tensor", "dv", max(xb_done, war[n % 2]))
            for k in range(NCH):
                j = (n * NCH + k) % 2
                p.op("tensor", lambda e, ps=psA, a=xh[k][:, sl], s=(k == 0), t=(k == NCH - 1):
                     e.matmul(ps, c["ones1_bf"][:], a, start=s, stop=t), "pe")
                p.wait("tensor", "gp", x2tgt[k])
                p.op("tensor", lambda e, ps=psB, b=x2b[j][:], s=(k == 0), t=(k == NCH - 1):
                     e.matmul(ps, c["ones1_bf"][:], b, start=s, stop=t), "pe")
                x2war[j] = p.ctr["pe"]
            mm = p.ctr["pe"]
            p.wait("vector", "pe", mm)
            p.op("vector", lambda e, ps=psA, d=RA[0:1, sl]: e.tensor_copy(d, ps), "dv")
            p.op("vector", lambda e, ps=psB, d=RB_[0:1, sl]: e.tensor_copy(d, ps), "dv")
            war[n % 2] = p.ctr["dv"]
        p.op("vector", lambda e: e.tensor_scalar(RM, RA, 1.0 / D, None, OP.mult), "dv")
        p.op("vector", lambda e: e.tensor_scalar(RB_, RB_, 1.0 / D, None, OP.mult), "dv")
        p.op("vector", lambda e: e.tensor_tensor(RA, RM, RM, OP.mult), "dv")
        p.op("vector", lambda e: e.tensor_tensor(RB_, RB_, RA, OP.subtract), "dv")
        p.op("vector", lambda e: e.tensor_scalar(RB_, RB_, 1e-5, None, OP.add), "dv")
        p.wait("scalar", "dv", p.ctr["dv"])
        p.op("scalar", lambda e: e.sqrt(RB_, RB_), "sc")
        p.wait("vector", "sc", p.ctr["sc"])
        p.op("vector", lambda e: e.reciprocal(RA, RB_), "dv")
        p.op("vector", lambda e: e.tensor_tensor(RM, RM, RA, OP.mult), "dv")
        p.op("vector", lambda e: e.tensor_copy(RR, RA), "dv")
        p.op("vector", lambda e: e.tensor_copy(RMR, RM), "dv")
        rows_done = p.ctr["dv"]
        p.wait("tensor", "dv", rows_done)
        bcwar = [0, 0]
        for n in range(NT):
            sl = slice(n * 512, (n + 1) * 512)
            ps = pf[4 + (n % 2)]
            p.wait("tensor", "dv", bcwar[n % 2])
            p.op("tensor", lambda e, ps=ps[0:128, :], r=RR[0:1, sl]:
                 e.matmul(ps, c["onesr_f"][:], r, start=True, stop=True), "pe")
            p.wait("vector", "pe", p.ctr["pe"])
            p.op("vector", lambda e, ps=ps[0:128, :], d=rb[:, sl]: e.tensor_copy(d, ps), "dv")
            p.wait("tensor", "dv", p.ctr["dv"])
            p.op("tensor", lambda e, ps=ps[0:128, :], r=RMR[0:1, sl]:
                 e.matmul(ps, c["onesr_f"][:], r, start=True, stop=True), "pe")
            p.wait("vector", "pe", p.ctr["pe"])
            p.op("vector", lambda e, ps=ps[0:128, :], d=mrb[:, sl]: e.tensor_copy(d, ps), "dv")
            bcwar[n % 2] = p.ctr["dv"]
        bc_done = p.ctr["dv"]
        stats_pe = p.ctr["pe"]
        p.wait("gpsimd", "dv", bc_done)
        p.wait("gpsimd", "pe", stats_pe)
        p.wait("vector", "pe", stats_pe)
        for k in range(NCH):
            xk = x[k]
            for n in range(NT):
                sl = slice(n * 512, (n + 1) * 512)
                eng, sem = ("vector", "dv") if k % 2 == 0 else ("gpsimd", "gp")
                j = 0 if k % 2 == 0 else 1
                tmp = ntmp[j][:]
                p.op(eng, lambda e, t=tmp, a=xk[:, sl], r=rb[:, sl]:
                     e.tensor_tensor(t, a, r, OP.mult), sem)
                p.op(eng, lambda e, t=tmp, d=xh[k][:, sl], m=mrb[:, sl]:
                     e.tensor_tensor(d, t, m, OP.subtract), sem)
        c["ln_dv"] = p.ctr["dv"]
        c["ln_gp"] = p.ctr["gp"]
    p.wait("tensor", "dv", c["ln_dv"])
    p.wait("tensor", "gp", c["ln_gp"])



def emit_l0_attn(p, c):
    pf, pb16 = c["pf"], c["pb16"]
    xh = c["xhat"]
    in_wqkv = p.inp("w_qkv", [D, 3072], F32)
    in_wout = p.inp("w_out", [D, D], F32)
    in_bout = p.inp("b_out_col", [128, NCH], F32)
    with contextlib.ExitStack() as sc:
        qb = [p.sb(f"l0q{i}", [128, T], BF16, sc) for i in range(NCH)]
        kb = p.sb("l0k", [128, T], BF16, sc)
        vb = p.sb("l0v", [128, T], BF16, sc)
        kT = [p.sb(f"l0kT{i}", [128, 128], BF16, sc) for i in range(2)]
        vT = [p.sb(f"l0vT{i}", [128, 130], BF16, sc) for i in range(2)]
        elu = [p.sb(f"l0e{i}", [128, 512], F32, sc) for i in range(4)]
        NW = 130
        ccbuf = p.sb("l0cc", [128, NCH * NW + 8], F32, sc)
        kvb = p.sb("l0kvb", [128, NCH * 128], BF16, sc)
        qzb = [p.sb(f"l0qz{i}", [128, 512], BF16, sc) for i in range(2)]
        wq_s = [p.sb(f"l0wq{i}", [128, 8 * 384], BF16, sc) for i in range(2)]
        wo_s = [p.sb(f"l0wo{i}", [128, 8 * 128], BF16, sc) for i in range(2)]
        boutc = p.sb("l0bout", [128, NCH], F32, sc)
        tmpE = [p.sb(f"l0te{i}", [128, 512], F32, sc) for i in range(2)]
        attn = xh
        den = row2(c, 0)
        cc_in = p.idram("l0cc_in", [128, NCH * NW + 8], F32)
        cc_out = p.idram("l0cc_out", [128, NCH * NW + 8], F32)

        p.dma("sync", boutc[:], in_bout, "lds")
        b_ready = p.ctr["lds"]

        def load_wqkv(cc):
            buf = wq_s[cc % 2]
            for k in range(NCH):
                for j, ofs in enumerate((cc * 128, 1024 + cc * 128, 2048 + cc * 128)):
                    src = in_wqkv[k * 128:(k + 1) * 128, ofs:ofs + 128]
                    p.dma("gpsimd", buf[:, k * 384 + j * 128: k * 384 + (j + 1) * 128], src, "ldg")
            return p.ctr["ldg"]

        wq_ready = [load_wqkv(0)]
        wq_war = [0, 0]
        p.op("vector", lambda e: e.memset(vT[0][:, 128:129], 1.0), "dv", nowait=True)
        p.op("vector", lambda e: e.memset(vT[1][:, 128:129], 1.0), "dv")
        ones_set = p.ctr["dv"]
        elu_sc = [0, 0, 0, 0]
        evwar = [0, 0, 0, 0]
        twar = 0
        for cc in range(NCH):
            if cc + 1 < NCH:
                p.wait("gpsimd", "pe", wq_war[(cc + 1) % 2])
                wq_ready.append(load_wqkv(cc + 1))
            wbuf = wq_s[cc % 2]
            p.wait("tensor", "ldg", wq_ready[cc])
            for wj, dst in ((0, qb[cc]), (1, kb), (2, vb)):
                for n in range(NT):
                    sl = slice(n * 512, (n + 1) * 512)
                    jr = (wj * NT + n) % 4
                    ps = pf[jr]
                    p.wait("tensor", "dv", evwar[jr])
                    for k in range(NCH):
                        p.op("tensor", lambda e, ps=ps[:], w=wbuf[:, k * 384 + wj * 128: k * 384 + wj * 128 + 128],
                             a=xh[k][:, sl], s=(k == 0), t=(k == NCH - 1):
                             e.matmul(ps, w, a, start=s, stop=t), "pe")
                    mm = p.ctr["pe"]
                    if wj == 2:
                        p.wait("vector", "pe", mm)
                        p.op("vector", lambda e, ps=ps[:], d=dst[:, sl]: e.tensor_copy(d, ps), "dv")
                        evwar[jr] = p.ctr["dv"]
                    else:
                        eb = elu[jr][:]
                        p.wait("vector", "pe", mm)
                        p.wait("vector", "sc", elu_sc[jr])
                        p.op("vector", lambda e, ps=ps[:], b=eb: e.tensor_scalar(b, ps, 0.0, None, OP.min), "dv")
                        p.wait("scalar", "dv", p.ctr["dv"])
                        p.op("scalar", lambda e, b=eb: e.activation(b, b, AF.Exp), "sc")
                        elu_sc[jr] = p.ctr["sc"]
                        p.wait("vector", "sc", elu_sc[jr])
                        p.op("vector", lambda e, ps=ps[:], b=eb, d=dst[:, sl]:
                             e.scalar_tensor_tensor(d, ps, 0.0, b, OP.max, OP.add), "dv")
                        evwar[jr] = p.ctr["dv"]
            wq_war[cc % 2] = p.ctr["pe"]
            kvev = p.ctr["dv"]
            p.wait("vector", "dv", kvev)
            p.op("vector", lambda e, d=ccbuf[:, NCH * NW + cc: NCH * NW + cc + 1]:
                 e.reduce_sum(d, kb[:], AX), "dv")
            kvps = pf[4]
            tev = [0, 0]
            tkv = [0, 0]
            for tc in range(16):
                tsl = slice(tc * 128, (tc + 1) * 128)
                j = tc % 2
                p.wait("tensor", "dv", tev[j])
                p.op("tensor", lambda e, a=kb[:, tsl], d=pb16[:, j * 256: j * 256 + 128]:
                     e.transpose(d, a, c["id_bf"][:]), "pe")
                p.op("tensor", lambda e, a=vb[:, tsl], d=pb16[:, j * 256 + 128: j * 256 + 256]:
                     e.transpose(d, a, c["id_bf"][:]), "pe")
                ttr = p.ctr["pe"]
                p.wait("vector", "pe", ttr)
                if tc < 2:
                    p.wait("vector", "dv", ones_set)
                p.op("vector", lambda e, d=kT[j][:, 0:128], ps=pb16[:, j * 256: j * 256 + 128]:
                     e.tensor_copy(d, ps), "dv")
                p.op("vector", lambda e, d=vT[j][:, 0:128], ps=pb16[:, j * 256 + 128: j * 256 + 256]:
                     e.tensor_copy(d, ps), "dv")
                tev[j] = p.ctr["dv"]
                p.wait("tensor", "dv", tev[j])
                p.op("tensor", lambda e, s=(tc == 0), t=(tc == 15), l=kT[j][:], r=vT[j][:, 0:129]:
                     e.matmul(kvps[0:128, 0:129], l, r, start=s, stop=t), "pe")
                tkv[j] = p.ctr["pe"]
            p.wait("vector", "pe", p.ctr["pe"])
            p.op("vector", lambda e, d=ccbuf[:, cc * NW: cc * NW + 129]:
                 e.tensor_copy(d, kvps[0:128, 0:129]), "dv")
        p.wait("gpsimd", "dv", p.ctr["dv"])
        p.dma("gpsimd", cc_in, ccbuf[:], "ldg")
        p.wait("gpsimd", "ldg", p.ctr["ldg"])
        p.op("gpsimd", lambda e: e.collective_compute(
            "AllReduce", OP.add, replica_groups=PAIRS,
            ins=[cc_in.opt()], outs=[cc_out.opt()]), "cc0")
        p.wait("gpsimd", "cc0", 1)
        p.dma("gpsimd", ccbuf[:], cc_out, "ldg")
        p.wait("vector", "ldg", p.ctr["ldg"])
        for cc in range(NCH):
            p.op("vector", lambda e, d=ccbuf[0:64, cc * NW + 64: cc * NW + 128]: e.memset(d, 0.0), "dv")
            p.op("vector", lambda e, d=ccbuf[64:128, cc * NW: cc * NW + 64]: e.memset(d, 0.0), "dv")
            p.op("vector", lambda e, d=kvb[:, cc * 128:(cc + 1) * 128], a=ccbuf[:, cc * NW: cc * NW + 128]:
                 e.tensor_copy(d, a), "dv")
        kv_ready = p.ctr["dv"]
        p.wait("tensor", "dv", kv_ready)
        nwar = [0, 0]
        qz_war = 0
        for cc in range(NCH):
            for n in range(NT):
                sl = slice(n * 512, (n + 1) * 512)
                p.wait("vector", "pe", qz_war if n == 0 else 0)
                p.op("vector", lambda e, b=qzb[n % 2][:], a=qb[cc][:, sl],
                     z=ccbuf[:, NCH * NW + cc: NCH * NW + cc + 1]:
                     e.tensor_scalar(b, a, z, None, OP.mult), "dv")
                p.wait("tensor", "dv", p.ctr["dv"])
                dps = pf[5][0:2, :]
                p.op("tensor", lambda e, dps=dps, a=qzb[n % 2][:]:
                     e.matmul(dps, c["sel2_bf"][:], a, start=True, stop=True), "pe")
                p.wait("vector", "pe", p.ctr["pe"])
                p.op("vector", lambda e, dps=dps, d=den[:, sl]:
                     e.tensor_scalar(d, dps, 1e-6, None, OP.add), "dv")
                p.wait("tensor", "dv", p.ctr["dv"])
            p.op("vector", lambda e, d=den: e.reciprocal(d, d), "dv")
            rec_done = p.ctr["dv"]
            p.wait("tensor", "dv", rec_done)
            for n in range(NT):
                sl = slice(n * 512, (n + 1) * 512)
                nps = pf[n % 2]
                bps = pf[2 + n % 2]
                p.wait("tensor", "dv", nwar[n % 2])
                p.op("tensor", lambda e, ps=nps[:], w=kvb[:, cc * 128:(cc + 1) * 128], a=qb[cc][:, sl]:
                     e.matmul(ps, w, a, start=True, stop=True), "pe")
                p.op("tensor", lambda e, ps=bps[:], r=den[:, sl]:
                     e.matmul(ps, c["bsel_f"][:], r, start=True, stop=True), "pe")
                nmm = p.ctr["pe"]
                p.wait("vector", "pe", nmm)
                p.op("vector", lambda e, ps=bps[:]: e.tensor_copy(tmpE[0][:], ps), "dv")
                p.op("vector", lambda e, ps=nps[:], d=attn[cc][:, sl]:
                     e.tensor_tensor(d, ps, tmpE[0][:], OP.mult), "dv")
                nwar[n % 2] = p.ctr["dv"]
            qz_war = p.ctr["pe"]
        attn_done = p.ctr["dv"]
        def load_wout(oc):
            buf = wo_s[oc % 2]
            for k in range(NCH):
                src = in_wout[k * 128:(k + 1) * 128, oc * 128:(oc + 1) * 128]
                p.dma("gpsimd", buf[:, k * 128:(k + 1) * 128], src, "ldg")
            return p.ctr["ldg"]
        wo_ready = [load_wout(0)]
        wo_war = [0, 0]
        p.wait("tensor", "dv", attn_done)
        p.wait("vector", "lds", b_ready)
        gwar = [0, 0]
        for oc in range(NCH):
            if oc + 1 < NCH:
                p.wait("gpsimd", "pe", wo_war[(oc + 1) % 2])
                wo_ready.append(load_wout(oc + 1))
            p.wait("tensor", "ldg", wo_ready[oc])
            for n in range(NT):
                sl = slice(n * 512, (n + 1) * 512)
                ps = pf[n % 2]
                p.wait("tensor", "dv", nwar[n % 2])
                for k in range(NCH):
                    p.op("tensor", lambda e, ps=ps[:], w=wo_s[oc % 2][:, k * 128:(k + 1) * 128],
                         a=attn[k][:, sl], s=(k == 0), t=(k == NCH - 1):
                         e.matmul(ps, w, a, start=s, stop=t), "pe")
                omm = p.ctr["pe"]
                p.wait("vector", "pe", omm)
                p.wait("vector", "gp", gwar[n % 2])
                p.op("vector", lambda e, ps=ps[:], t=tmpE[n % 2][:], b=boutc[:, oc:oc + 1]:
                     e.tensor_scalar(t, ps, b, None, OP.add), "dv")
                nwar[n % 2] = p.ctr["dv"]
                p.wait("gpsimd", "dv", p.ctr["dv"])
                xoc = c["x"][oc]
                p.op("gpsimd", lambda e, d=xoc[:, sl], t=tmpE[n % 2][:]:
                     e.tensor_add(d, d, t), "gp", nowait=True)
                gwar[n % 2] = p.ctr["gp"]
            wo_war[oc % 2] = p.ctr["pe"]
        p.wait("tensor", "gp", p.ctr["gp"])
        p.wait("vector", "gp", p.ctr["gp"])
        p.wait("sync", "gp", p.ctr["gp"])



def emit_l0_ffn(p, c):
    pf = c["pf"]
    xh = c["xhat"]
    in_w = p.inp("ffn_wT", [D, D], F32)
    in_alpha = p.inp("ffn_alpha", [1, 1], F32)
    in_bias = p.inp("ffn_bias_col", [128, NCH], F32)
    with contextlib.ExitStack() as sc:
        wf = [p.sb(f"f_w{i}", [128, D], F32, sc) for i in range(2)]
        tern = [p.sb(f"f_t{k}", [128, D], BF16, sc) for k in range(NCH)]
        t1 = p.sb("f_t1", [128, D], F32, sc)
        asum = p.sb("f_asum", [128, 8], F32, sc)
        row8 = p.sb("f_row8", [8, 128], F32, sc)
        col8 = p.sb("f_col8", [8, 1], F32, sc)
        rowt = p.sb("f_rowt", [1, 8], F32, sc)
        sc3 = p.sb("f_sc3", [1, 3], F32, sc)
        bc3 = p.sb("f_bc3", [128, 3], F32, sc)
        biasc = p.sb("f_bias", [128, NCH], F32, sc)
        tmpE = [p.sb(f"f_te{i}", [128, 512], F32, sc) for i in range(4)]
        p.dma("sync", biasc[:], in_bias, "lds")
        p.dma("sync", sc3[0:1, 1:2], in_alpha, "lds")
        misc_ready = p.ctr["lds"]
        red_tgt = [0, 0]
        for k in range(NCH):
            if k >= 2:
                p.wait("sync", "dv", red_tgt[k % 2])
            wt = p.dma("sync", wf[k % 2][:], in_w[k * 128:(k + 1) * 128, :], "lds")
            p.wait("vector", "lds", wt)
            p.op("vector", lambda e, d=asum[:, k:k + 1], a=wf[k % 2][:]:
                 e.reduce_sum(d, a, AX, apply_absolute_value=True), "dv")
            red_tgt[k % 2] = p.ctr["dv"]
        s1 = p.ctr["dv"]
        p.wait("tensor", "dv", s1)
        p.op("tensor", lambda e: e.matmul(pf[0][0:1, 0:8], c["onesc_f"][:], asum[:], start=True, stop=True), "pe")
        p.wait("vector", "pe", p.ctr["pe"])
        p.wait("vector", "lds", misc_ready)
        p.op("vector", lambda e: e.tensor_copy(rowt[:], pf[0][0:1, 0:8]), "dv")
        p.op("vector", lambda e: e.reduce_sum(sc3[0:1, 0:1], rowt[:], AX), "dv")
        p.op("vector", lambda e: e.tensor_scalar(sc3[0:1, 0:1], sc3[0:1, 0:1], 0.7 / (D * D), None, OP.mult), "dv")
        p.op("vector", lambda e: e.tensor_scalar(sc3[0:1, 2:3], sc3[0:1, 0:1], -1.0, None, OP.mult), "dv")
        p.wait("tensor", "dv", p.ctr["dv"])
        p.op("tensor", lambda e: e.matmul(pf[2][0:128, 0:3], c["onesr_f"][:], sc3[0:1, 0:3], start=True, stop=True), "pe")
        p.wait("vector", "pe", p.ctr["pe"])
        p.op("vector", lambda e: e.tensor_copy(bc3[:], pf[2][0:128, 0:3]), "dv")
        bc_done = p.ctr["dv"]
        dbg_dump(p, c, "dbg_bc3", bc3[:], [128, 3], F32)
        dbg_dump(p, c, "dbg_asum", asum[:], [128, 8], F32)
        tern_tgt = [bc_done, bc_done]
        for k in range(NCH):
            p.wait("sync", "dv", tern_tgt[k % 2])
            wt = p.dma("sync", wf[k % 2][:], in_w[k * 128:(k + 1) * 128, :], "lds")
            p.wait("vector", "lds", wt)
            p.op("vector", lambda e, a=wf[k % 2][:]: e.tensor_scalar(t1[:], a, bc3[:, 0:1], None, OP.is_gt), "dv")
            p.op("vector", lambda e, a=wf[k % 2][:]: e.tensor_scalar(a, a, bc3[:, 2:3], None, OP.is_lt), "dv")
            p.op("vector", lambda e, a=wf[k % 2][:], d=tern[k][:]: e.tensor_tensor(d, t1[:], a, OP.subtract), "dv")
            tern_tgt[k % 2] = p.ctr["dv"]
        tern_done = p.ctr["dv"]
        dbg_dump(p, c, "dbg_tern0", tern[0][:], [128, D], BF16)
        p.wait("tensor", "dv", tern_done)
        gwar = [0, 0, 0, 0]
        dwar = [0, 0, 0, 0]
        for oc in range(NCH):
            for n in range(NT):
                sl = slice(n * 512, (n + 1) * 512)
                jr = (oc * NT + n) % 4
                ps = pf[jr]
                p.wait("tensor", "dv", dwar[jr])
                for k in range(NCH):
                    p.op("tensor", lambda e, ps=ps[:], w=tern[k][:, oc * 128:(oc + 1) * 128],
                         a=xh[k][:, sl], s=(k == 0), t=(k == NCH - 1):
                         e.matmul(ps, w, a, start=s, stop=t), "pe")
                mm = p.ctr["pe"]
                p.wait("vector", "pe", mm)
                p.wait("vector", "gp", gwar[jr])
                p.op("vector", lambda e, ps=ps[:], t=tmpE[jr][:], b=biasc[:, oc:oc + 1]:
                     e.tensor_scalar(t, ps, bc3[:, 1:2], b, OP.mult, OP.add), "dv")
                dwar[jr] = p.ctr["dv"]
                p.wait("gpsimd", "dv", p.ctr["dv"])
                xoc = c["x"][oc]
                p.op("gpsimd", lambda e, d=xoc[:, sl], t=tmpE[jr][:]: e.tensor_add(d, d, t), "gp", nowait=True)
                gwar[jr] = p.ctr["gp"]
        p.wait("tensor", "gp", p.ctr["gp"])
        p.wait("vector", "gp", p.ctr["gp"])
        p.wait("sync", "gp", p.ctr["gp"])




def emit_l1_attn(p, c, core, x_scope, x_reload):
    pf, xh = c["pf"], c["xhat"]
    NKC = [12, 16, 28, 32]
    UMAP = {0: [0, 1, 2, 3, 8, 9, 10, 11], 1: [4, 5, 6, 7, 12, 13, 14, 15],
            2: [16, 17, 18, 19, 24, 25, 26, 27], 3: [20, 21, 22, 23, 28, 29, 30, 31]}
    ZOF = [0, 12, 28, 56]

    in_wq = p.inp("wq", [D, D], F32)
    in_wdkv = p.inp("wdkv", [D, LAT], F32).rearrange("(k p) f -> p k f", p=128)
    in_wuk = p.inp("wuk", [LAT, D], F32).rearrange("(k p) f -> p k f", p=128)
    in_wuv = p.inp("wuv", [LAT, D], F32).rearrange("(k p) f -> p k f", p=128)
    in_wo = p.inp("wo", [D, D], F32)
    in_gb = p.inp("l1_gam", [128, 2], F32)
    in_bb = p.inp("l1_bet", [128, 2], F32)
    in_c4 = p.inp("ropeC", [128, S], BF16)
    in_s4 = p.inp("ropeS", [128, S], BF16)
    in_qc4 = p.inp("qropeC", [128, T], BF16)
    in_qs4 = p.inp("qropeS", [128, T], BF16)
    in_masks = p.inp("cmasks", [128, 2 * 8 * 512], BF16)
    in_zvb = p.inp("zvb", [128, 88], F32)
    ag_in = p.idram("ag_in", [2 * 128, T], BF16)
    ag_out = p.idram("ag_out", [4 * 128, T], BF16)

    x_scope.close()
    with contextlib.ExitStack() as sc:
        c4 = p.sb("c4", [128, S], BF16, sc)
        s4 = p.sb("s4", [128, S], BF16, sc)
        wuk_sb = p.sb("wuk", [128, 2 * D], BF16, sc)
        wuv_sb = p.sb("wuv", [128, 2 * D], BF16, sc)
        latful = [p.sb(f"latf{i}", [128, S], BF16, sc) for i in range(2)]
        gbc = p.sb("gbc", [128, 2], F32, sc)
        bbc = p.sb("bbc", [128, 2], F32, sc)
        masks = p.sb("cmask", [128, 2 * 8 * 512], BF16, sc)
        zvb = p.sb("zvb", [128, 88], F32, sc)
        q_all = [p.sb(f"q{i}", [128, T], BF16, sc) for i in range(NCH)]
        ktmp = [p.sb(f"ktmp{i}", [128, 512], F32, sc) for i in range(2)]
        ksw = [p.sb(f"ksw{i}", [128, 512], F32, sc) for i in range(2)]
        q2b = [p.sb(f"q2b{i}", [128, 512], BF16, sc) for i in range(2)]
        p.dma("gpsimd", c4[:], in_c4, "ldg")
        p.dma("gpsimd", s4[:], in_s4, "ldg")
        p.dma("gpsimd", wuk_sb[:].rearrange("p (k f) -> p k f", f=D), in_wuk, "ldg")
        p.dma("gpsimd", wuv_sb[:].rearrange("p (k f) -> p k f", f=D), in_wuv, "ldg")
        p.dma("gpsimd", masks[:], in_masks, "ldg")
        p.dma("sync", gbc[:], in_gb, "lds")
        p.dma("sync", bbc[:], in_bb, "lds")
        p.dma("sync", zvb[:], in_zvb, "lds")
        w1rdy = p.ctr["ldg"]
        g_rdy = p.ctr["lds"]

        def rope_evac(ps, dst, ctab, stab, off, j):
            kt, kw = ktmp[j][:], ksw[j][:]
            csl = slice(off, off + 512)
            p.op("vector", lambda e, ps=ps[:], t=kt: e.tensor_copy(t, ps), "dv")
            for g in range(4):
                src = slice((g ^ 1) * 32, (g ^ 1) * 32 + 32)
                dstp = slice(g * 32, g * 32 + 32)
                p.op("vector", lambda e, a=kw[dstp, :], b=kt[src, :]: e.tensor_copy(a, b),
                     "dv", nowait=True)
            p.wait("gpsimd", "dv", p.ctr["dv"])
            p.op("gpsimd", lambda e, t=kt, cc=ctab[:, csl]: e.tensor_mul(t, t, cc), "gp", nowait=True)
            p.op("gpsimd", lambda e, w=kw, ss=stab[:, csl]: e.tensor_mul(w, w, ss), "gp")
            p.wait("vector", "gp", p.ctr["gp"])
            p.op("vector", lambda e, d=dst, t=kt, w=kw: e.tensor_tensor(d, t, w, OP.add), "dv")
            return p.ctr["dv"]
        with contextlib.ExitStack() as ssc:
            wdkv_sb = p.sb("wdkv", [128, 8 * LAT], BF16, ssc)
            p.dma("gpsimd", wdkv_sb[:].rearrange("p (k f) -> p k f", f=LAT), in_wdkv, "ldg")
            w1rdy = p.ctr["ldg"]
            latf = [p.sb(f"latraw{i}", [128, T], F32, ssc) for i in range(2)]
            latloc = [p.sb(f"latloc{i}", [128, T], BF16, ssc) for i in range(2)]
            rb2 = p.sb("l1rb", [128, T], F32, ssc)
            mrb2 = p.sb("l1mrb", [128, T], F32, ssc)
            x2b = [p.sb(f"l1x2_{i}", [128, 512], BF16, ssc) for i in range(2)]
            RR, RMR = row1(c, 0), row1(c, 1)
            RA, RB_, RM = row1(c, 3), row1(c, 4), row1(c, 5)
            p.wait("tensor", "ldg", w1rdy)
            evwar = [0, 0]
            for ch in range(2):
                for n in range(NT):
                    sl = slice(n * 512, (n + 1) * 512)
                    ps = pf[n % 2]
                    p.wait("tensor", "dv", evwar[n % 2])
                    for k in range(NCH):
                        p.op("tensor", lambda e, ps=ps[:], w=wdkv_sb[:, k * LAT + ch * 128: k * LAT + ch * 128 + 128],
                             a=xh[k][:, sl], s=(k == 0), t=(k == NCH - 1):
                             e.matmul(ps, w, a, start=s, stop=t), "pe")
                    p.wait("vector", "pe", p.ctr["pe"])
                    p.op("vector", lambda e, ps=ps[:], d=latf[ch][:, sl]: e.tensor_copy(d, ps), "dv")
                    evwar[n % 2] = p.ctr["dv"]
            lat_done = p.ctr["dv"]
            for ch in range(2):
                p.op("vector", lambda e, a=latloc[ch][:], b=latf[ch][:]: e.tensor_copy(a, b), "dv")
            p.wait("gpsimd", "dv", p.ctr["dv"])
            war = [0]
            x2war = [0, 0]
            for n in range(NT):
                sl = slice(n * 512, (n + 1) * 512)
                psA = pf[2][0:1, :]
                psB = pf[3][0:1, :]
                x2t = []
                for ch in range(2):
                    p.wait("gpsimd", "pe", x2war[ch])
                    p.op("gpsimd", lambda e, b=x2b[ch][:], a=latloc[ch][:, sl]: e.tensor_mul(b, a, a), "gp")
                    x2t.append(p.ctr["gp"])
                p.wait("tensor", "gp", x2t[-1])
                p.wait("tensor", "dv", war[0])
                for ch in range(2):
                    p.op("tensor", lambda e, ps=psA, a=latloc[ch][:, sl], s=(ch == 0), t=(ch == 1):
                         e.matmul(ps, c["ones1_bf"][:], a, start=s, stop=t), "pe")
                    p.op("tensor", lambda e, ps=psB, b=x2b[ch][:], s=(ch == 0), t=(ch == 1):
                         e.matmul(ps, c["ones1_bf"][:], b, start=s, stop=t), "pe")
                x2war[0] = x2war[1] = p.ctr["pe"]
                p.wait("vector", "pe", p.ctr["pe"])
                p.op("vector", lambda e, ps=psA, d=RA[0:1, sl]: e.tensor_copy(d, ps), "dv")
                p.op("vector", lambda e, ps=psB, d=RB_[0:1, sl]: e.tensor_copy(d, ps), "dv")
                war[0] = p.ctr["dv"]
            p.op("vector", lambda e: e.tensor_scalar(RM, RA, 1.0 / LAT, None, OP.mult), "dv")
            p.op("vector", lambda e: e.tensor_scalar(RB_, RB_, 1.0 / LAT, None, OP.mult), "dv")
            p.op("vector", lambda e: e.tensor_tensor(RA, RM, RM, OP.mult), "dv")
            p.op("vector", lambda e: e.tensor_tensor(RB_, RB_, RA, OP.subtract), "dv")
            p.op("vector", lambda e: e.tensor_scalar(RB_, RB_, 1e-5, None, OP.add), "dv")
            p.wait("scalar", "dv", p.ctr["dv"])
            p.op("scalar", lambda e: e.sqrt(RB_, RB_), "sc")
            p.wait("vector", "sc", p.ctr["sc"])
            p.op("vector", lambda e: e.reciprocal(RA, RB_), "dv")
            p.op("vector", lambda e: e.tensor_tensor(RM, RM, RA, OP.mult), "dv")
            p.op("vector", lambda e: e.tensor_copy(RR, RA), "dv")
            p.op("vector", lambda e: e.tensor_copy(RMR, RM), "dv")
            rows_done = p.ctr["dv"]
            p.wait("tensor", "dv", rows_done)
            for n in range(NT):
                sl = slice(n * 512, (n + 1) * 512)
                ps = pf[4]
                p.op("tensor", lambda e, ps=ps[0:128, :], r=RR[0:1, sl]:
                     e.matmul(ps, c["onesr_f"][:], r, start=True, stop=True), "pe")
                p.wait("vector", "pe", p.ctr["pe"])
                p.op("vector", lambda e, ps=ps[0:128, :], d=rb2[:, sl]: e.tensor_copy(d, ps), "dv")
                p.wait("tensor", "dv", p.ctr["dv"])
                p.op("tensor", lambda e, ps=ps[0:128, :], r=RMR[0:1, sl]:
                     e.matmul(ps, c["onesr_f"][:], r, start=True, stop=True), "pe")
                p.wait("vector", "pe", p.ctr["pe"])
                p.op("vector", lambda e, ps=ps[0:128, :], d=mrb2[:, sl]: e.tensor_copy(d, ps), "dv")
                p.wait("tensor", "dv", p.ctr["dv"])
            bc_done = p.ctr["dv"]
            p.wait("vector", "lds", g_rdy)
            p.wait("gpsimd", "dv", bc_done)
            for ch in range(2):
                eng, sem = ("vector", "dv") if ch == 0 else ("gpsimd", "gp")
                tb = latf[ch]
                p.op(eng, lambda e, t=tb[:], r=rb2[:]: e.tensor_tensor(t, t, r, OP.mult), sem)
                p.op(eng, lambda e, t=tb[:], m=mrb2[:]: e.tensor_tensor(t, t, m, OP.subtract), sem)
                p.op(eng, lambda e, d=latloc[ch][:], t=tb[:], g=gbc[:, ch:ch + 1], b=bbc[:, ch:ch + 1]:
                     e.tensor_scalar(d, t, g, b, OP.mult, OP.add), sem)
            p.wait("gpsimd", "dv", p.ctr["dv"])
            for ch in range(2):
                p.dma("gpsimd", ag_in[ch * 128:(ch + 1) * 128, :], latloc[ch][:], "ldg")
            p.wait("gpsimd", "ldg", p.ctr["ldg"])
            cc_tgt = p.op("gpsimd", lambda e: e.collective_compute(
                "AllGather", OP.bypass, replica_groups=PAIRS,
                ins=[ag_in.opt()], outs=[ag_out.opt()]), "cc0", nowait=True)
            p.wait("gpsimd", "cc0", cc_tgt)
            for ch in range(2):
                p.dma("gpsimd", latful[ch][:, 0:1024], ag_out[ch * 128:(ch + 1) * 128, 0:1024], "ldg")
                p.dma("gpsimd", latful[ch][:, 3072:4096], ag_out[ch * 128:(ch + 1) * 128, 1024:2048], "ldg")
                p.dma("gpsimd", latful[ch][:, 1024:3072], ag_out[256 + ch * 128: 256 + (ch + 1) * 128, :], "ldg")
            lat_ready = p.ctr["ldg"]
            p.wait("tensor", "ldg", lat_ready)
            p.barrier()
            if os.environ.get("L1SUB", "") == "lat":
                dbg_dump(p, c, "dbg_lat0", latful[0][:], [128, S], BF16)
                dbg_dump(p, c, "dbg_lat1", latful[1][:], [128, S], BF16)
        with contextlib.ExitStack() as qsc:
            qc4 = p.sb("qc4", [128, T], BF16, qsc)
            qs4 = p.sb("qs4", [128, T], BF16, qsc)
            wq_s = [p.sb(f"wqs{i}", [128, 1024], BF16, qsc) for i in range(2)]
            p.dma("gpsimd", qc4[:], in_qc4, "ldg")
            p.dma("gpsimd", qs4[:], in_qs4, "ldg")
            qt_rdy = p.ctr["ldg"]

            def load_wq(hp):
                buf = wq_s[hp % 2]
                for k in range(NCH):
                    p.dma("gpsimd", buf[:, k * 128:(k + 1) * 128],
                          in_wq[k * 128:(k + 1) * 128, hp * 128:(hp + 1) * 128], "ldg")
                return p.ctr["ldg"]

            wq_rdy = [load_wq(0)]
            wq_war = [0, 0]
            qwar = [0, 0]
            p.wait("gpsimd", "ldg", qt_rdy)
            for hp in range(NCH):
                if hp + 1 < NCH:
                    p.wait("gpsimd", "pe", wq_war[(hp + 1) % 2])
                    wq_rdy.append(load_wq(hp + 1))
                p.wait("tensor", "ldg", wq_rdy[hp])
                for n in range(NT):
                    sl = slice(n * 512, (n + 1) * 512)
                    ps = pf[n % 2]
                    p.wait("tensor", "dv", qwar[n % 2])
                    for k in range(NCH):
                        p.op("tensor", lambda e, ps=ps[:], w=wq_s[hp % 2][:, k * 128:(k + 1) * 128],
                             a=xh[k][:, sl], s=(k == 0), t=(k == NCH - 1):
                             e.matmul(ps, w, a, start=s, stop=t), "pe")
                    p.wait("vector", "pe", p.ctr["pe"])
                    p.wait("gpsimd", "pe", p.ctr["pe"])
                    qwar[n % 2] = rope_evac(ps, q_all[hp][:, sl], qc4, qs4, n * 512, n % 2)
                wq_war[hp % 2] = p.ctr["pe"]
            p.barrier()
            if os.environ.get("L1SUB", "") == "q":
                for i in range(NCH):
                    dbg_dump(p, c, f"dbg_q{i}", q_all[i][:], [128, T], BF16)
        vtm = [p.sb(f"vtm{i}", [128, 130], BF16, sc) for i in range(32)]
        kbuf = p.sb("kbuf", [128, S], BF16, sc)
        k0c = p.sb("k0c", [128, 1], F32, sc)
        sh2 = p.sb("sh2", [2, 8], F32, sc)
        kmx = p.sb("kmx", [2, 32], F32, sc)
        biasb = p.sb("biasb", [128, 8], F32, sc)
        biasM = p.sb("biasM", [128, 2 * 88], F32, sc)
        linb = p.sb("linb", [128, 512], F32, sc)
        ptb = [p.sb(f"pt{i}", [128, 512], BF16, sc) for i in range(3)]
        NQ2 = c["rowpool"][0:2, 0:2048]
        D0 = c["rowpool"][0:2, 2048:4096]
        BROW = c["rowpool"][0:2, 4096:6144]
        l1sub = os.environ.get("L1SUB", "")
        hp_list = [] if l1sub in ("lat", "q") else ([0] if l1sub == "hp0" else list(range(NCH)))
        for hp in hp_list:
            p.wait("tensor", "dv", p.ctr["dv"])
            p.wait("tensor", "sc", p.ctr["sc"])
            vwar = [0, 0]
            for tc in range(32):
                ps = pf[tc % 2]
                p.wait("tensor", "dv", vwar[tc % 2])
                for ch in range(2):
                    p.op("tensor", lambda e, ps=ps[0:128, 0:128], a=latful[ch][:, tc * 128:(tc + 1) * 128],
                         w=wuv_sb[:, ch * D + hp * 128: ch * D + hp * 128 + 128],
                         s=(ch == 0), t=(ch == 1):
                         e.matmul(ps, a, w, start=s, stop=t), "pe")
                p.wait("vector", "pe", p.ctr["pe"])
                v3 = vtm[tc][:].rearrange("p (h w) -> p h w", w=65)
                p.op("vector", lambda e, d=v3[:, :, 64:65]: e.memset(d, 1.0), "dv")
                p.op("vector", lambda e, d=v3[:, :, 0:64],
                     ps=ps[0:128, 0:128].rearrange("p (h w) -> p h w", w=64):
                     e.tensor_copy(d, ps), "dv")
                vwar[tc % 2] = p.ctr["dv"]
            if l1sub == "hp0v":
                dbg_dump(p, c, "dbg_v0", vtm[0][:], [128, 130], BF16)
                dbg_dump(p, c, "dbg_v5", vtm[5][:], [128, 130], BF16)
                break
            kwar2 = [vwar[0], vwar[1]]
            for kt in range(8):
                ps = pf[kt % 2]
                p.wait("tensor", "dv", kwar2[kt % 2])
                p.wait("tensor", "gp", p.ctr["gp"])
                for ch in range(2):
                    p.op("tensor", lambda e, ps=ps[:],
                         w=wuk_sb[:, ch * D + hp * 128: ch * D + hp * 128 + 128],
                         a=latful[ch][:, kt * 512:(kt + 1) * 512],
                         s=(ch == 0), t=(ch == 1):
                         e.matmul(ps, w, a, start=s, stop=t), "pe")
                p.wait("vector", "pe", p.ctr["pe"])
                p.wait("gpsimd", "pe", p.ctr["pe"])
                kwar2[kt % 2] = rope_evac(ps, kbuf[:, kt * 512:(kt + 1) * 512], c4, s4, kt * 512, kt % 2)
                p.wait("gpsimd", "dv", kwar2[kt % 2])
                p.op("gpsimd", lambda e, b=q2b[kt % 2][:], a=kbuf[:, kt * 512:(kt + 1) * 512]:
                     e.tensor_mul(b, a, a), "gp")
                p.wait("tensor", "gp", p.ctr["gp"])
                p.op("tensor", lambda e, ps=pf[6][0:2, :], a=q2b[kt % 2][:]:
                     e.matmul(ps, c["sel2_bf"][:], a, start=True, stop=True), "pe")
                p.wait("vector", "pe", p.ctr["pe"])
                p.op("vector", lambda e, d=kmx[0:2, kt:kt + 1], ps=pf[6][0:2, :]:
                     e.reduce_max(d, ps, AX), "dv")
                p.wait("tensor", "dv", p.ctr["dv"])
            for n in range(NT):
                sl = slice(n * 512, (n + 1) * 512)
                p.wait("gpsimd", "pe", p.ctr["pe"])
                p.op("gpsimd", lambda e, b=q2b[0][:], a=q_all[hp][:, sl]: e.tensor_mul(b, a, a), "gp")
                p.wait("tensor", "gp", p.ctr["gp"])
                p.op("tensor", lambda e, ps=pf[6][0:2, :], a=q2b[0][:]:
                     e.matmul(ps, c["sel2_bf"][:], a, start=True, stop=True), "pe")
                p.wait("vector", "pe", p.ctr["pe"])
                p.op("vector", lambda e, d=NQ2[:, sl], ps=pf[6][0:2, :]: e.tensor_copy(d, ps), "dv")
                p.wait("tensor", "dv", p.ctr["dv"])
                if n == 0:
                    p.op("vector", lambda e: e.tensor_copy(k0c[:], kbuf[:, 0:1]), "dv")
                p.op("vector", lambda e, b=q2b[1][:], a=q_all[hp][:, sl]:
                     e.tensor_scalar(b, a, k0c[:], None, OP.mult), "dv")
                p.wait("tensor", "dv", p.ctr["dv"])
                p.op("tensor", lambda e, ps=pf[6][0:2, :], a=q2b[1][:]:
                     e.matmul(ps, c["sel2_bf"][:], a, start=True, stop=True), "pe")
                p.wait("vector", "pe", p.ctr["pe"])
                p.op("vector", lambda e, d=D0[:, sl], ps=pf[6][0:2, :]: e.tensor_copy(d, ps), "dv")
                p.wait("tensor", "dv", p.ctr["dv"])
            p.op("vector", lambda e: e.reduce_max(kmx[0:2, 31:32], kmx[0:2, 0:8], AX), "dv")
            p.wait("scalar", "dv", p.ctr["dv"])
            p.op("scalar", lambda e: e.sqrt(kmx[0:2, 31:32], kmx[0:2, 31:32]), "sc")
            p.op("scalar", lambda e: e.sqrt(NQ2, NQ2), "sc")
            p.wait("vector", "sc", p.ctr["sc"])
            p.op("vector", lambda e: e.tensor_scalar(D0, D0, 0.125, None, OP.mult), "dv")
            p.op("vector", lambda e: e.tensor_scalar(BROW, NQ2, kmx[0:2, 31:32], 0.125, OP.mult, OP.mult), "dv")
            p.op("vector", lambda e: e.tensor_scalar(BROW, BROW, -80.0, None, OP.add), "dv")
            p.op("vector", lambda e: e.tensor_tensor(BROW, BROW, D0, OP.max), "dv")
            for qt in range(NT):
                p.op("vector", lambda e, d=sh2[0:2, qt:qt + 1], a=BROW[:, qt * 512:(qt + 1) * 512]:
                     e.reduce_max(d, a, AX), "dv")
            p.op("vector", lambda e: e.tensor_scalar(sh2[0:2, 0:4], sh2[0:2, 0:4], -1.0, None, OP.mult), "dv")
            nsh_done = p.ctr["dv"]
            p.wait("tensor", "dv", nsh_done)
            p.wait("vector", "sc", p.ctr["sc"])
            for h in range(2):
                for qt in range(NT):
                    p.op("tensor", lambda e, ps=pf[6][0:128, 0:1],
                         l=c["h01_f"][:, h * 128:(h + 1) * 128], r=sh2[0:2, qt:qt + 1]:
                         e.matmul(ps, l, r, start=True, stop=True), "pe")
                    p.wait("vector", "pe", p.ctr["pe"])
                    p.op("vector", lambda e, d=biasb[:, h * 4 + qt: h * 4 + qt + 1], ps=pf[6][0:128, 0:1]:
                         e.tensor_copy(d, ps), "dv")
                    p.wait("tensor", "dv", p.ctr["dv"])
            for h in range(2):
                for qt in range(NT):
                    nk = NKC[qt]
                    p.op("vector", lambda e, d=biasM[:, h * 88 + ZOF[qt]: h * 88 + ZOF[qt] + nk],
                         a=zvb[:, ZOF[qt]:ZOF[qt] + nk],
                         b=biasb[:, h * 4 + qt: h * 4 + qt + 1].broadcast_to((128, nk)):
                         e.tensor_tensor(d, a, b, OP.add), "dv")
            bias_done = p.ctr["dv"]
            if l1sub == "hp0k":
                dbg_dump(p, c, "dbg_k", kbuf[:], [128, S], BF16)
                dbg_dump(p, c, "dbg_brow", BROW, [2, 2048], F32)
                break
            p.wait("scalar", "dv", bias_done)
            sc_war = [0, 0, 0]
            pt_war = [0, 0, 0]
            av_war = [0, 0]
            for qt in range(NT):
                nkc = NKC[qt]
                umap = UMAP[qt]
                for h in range(2):
                    avps = pf[4 + h]
                    p.wait("tensor", "dv", av_war[h])
                    pend = []

                    def emit_av(kc, j, rsem, rtgt, last, avps=avps, h=h):
                        p.wait("tensor", rsem, rtgt)
                        p.op("tensor", lambda e, ps=avps[0:65, 0:512],
                             l=vtm[kc][:, h * 65: h * 65 + 65], r=ptb[j][:],
                             s=(kc == 0), t=last:
                             e.matmul(ps, l, r, start=s, stop=t), "pe")
                        pt_war[j] = p.ctr["pe"]
                    for kc in range(nkc):
                        j = kc % 3
                        sps = pf[j]
                        pt = ptb[j][:]
                        p.wait("tensor", "sc", sc_war[j])
                        p.op("tensor", lambda e, ps=sps[:], l=kbuf[h * 64:(h + 1) * 64, kc * 128:(kc + 1) * 128],
                             r=q_all[hp][h * 64:(h + 1) * 64, qt * 512:(qt + 1) * 512], tp=(h * 64, 0):
                             e.matmul(ps, l, r, start=True, stop=True, tile_position=tp), "pe")
                        mmt = p.ctr["pe"]
                        if len(pend) >= 2:
                            emit_av(*pend.pop(0), False)
                        p.wait("scalar", "pe", mmt)
                        p.wait("scalar", "pe", pt_war[j])
                        p.op("scalar", lambda e, d=pt[:], ps=sps[:],
                             b=biasM[:, h * 88 + ZOF[qt] + kc: h * 88 + ZOF[qt] + kc + 1]:
                             e.activation(d, ps, AF.Exp, bias=b, scale=0.125), "sc")
                        sc_war[j] = p.ctr["sc"]
                        if kc in umap:
                            u = umap.index(kc)
                            moff = ((qt // 2) * 8 + u) * 512
                            p.wait("vector", "sc", p.ctr["sc"])
                            p.op("vector", lambda e, d=pt[:], m=masks[:, moff:moff + 512]:
                                 e.tensor_tensor(d, d, m, OP.mult), "dv")
                            pend.append((kc, j, "dv", p.ctr["dv"]))
                        else:
                            pend.append((kc, j, "sc", p.ctr["sc"]))
                    while pend:
                        emit_av(*pend.pop(0), len(pend) == 0)
                    avt = p.ctr["pe"]
                    p.wait("vector", "pe", avt)
                    lb = c["rowpool"][0:1, h * 2048 + qt * 512: h * 2048 + (qt + 1) * 512]
                    p.op("vector", lambda e, d=lb, ps=avps[64:65, 0:512]: e.tensor_copy(d, ps),
                         "dv", nowait=True)
                    p.op("vector", lambda e, d=xh[hp][h * 64:(h + 1) * 64, qt * 512:(qt + 1) * 512],
                         ps=avps[0:64, 0:512]: e.tensor_copy(d, ps), "dv")
                    av_war[h] = p.ctr["dv"]
                    p.wait("tensor", "dv", p.ctr["dv"])
            p.wait("tensor", "dv", p.ctr["dv"])
            for qt in range(NT):
                p.op("tensor", lambda e, r=c["rowpool"][0:1, qt * 512:(qt + 1) * 512]:
                     e.matmul(pf[6][0:64, :], c["onesr_f"][0:1, 0:64], r, start=True, stop=True), "pe")
                p.op("tensor", lambda e, r=c["rowpool"][0:1, 2048 + qt * 512: 2048 + (qt + 1) * 512]:
                     e.matmul(pf[6][64:128, :], c["onesr_f"][0:1, 0:64], r, start=True, stop=True), "pe")
                p.wait("vector", "pe", p.ctr["pe"])
                p.op("vector", lambda e: e.reciprocal(linb[:], pf[6][0:128, :]), "dv")
                p.op("vector", lambda e, d=xh[hp][:, qt * 512:(qt + 1) * 512]:
                     e.tensor_tensor(d, d, linb[:], OP.mult), "dv")
                p.wait("tensor", "dv", p.ctr["dv"])
            if l1sub == "hp0":
                dbg_dump(p, c, "dbg_o0", xh[0][:], [128, T], BF16)
        if l1sub == "oall":
            for i in range(NCH):
                dbg_dump(p, c, f"dbg_o{i}", xh[i][:], [128, T], BF16)
        p.barrier()
    xt2 = [p.sb(f"x2_{k}", [128, T], F32) for k in range(NCH)]
    c["x"] = [t[:] for t in xt2]
    with contextlib.ExitStack() as sc2:
        wo_s = [p.sb(f"wos{i}", [128, 1024], BF16, sc2) for i in range(2)]
        tmpE = [p.sb(f"l1te{i}", [128, 512], F32, sc2) for i in range(2)]
        for k in range(NCH):
            p.dma("sync", c["x"][k], x_reload[k * 128:(k + 1) * 128, :], "lds")
        x_rdy = p.ctr["lds"]
        p.wait("gpsimd", "lds", x_rdy)

        def load_wo(oc):
            buf = wo_s[oc % 2]
            for k in range(NCH):
                p.dma("gpsimd", buf[:, k * 128:(k + 1) * 128],
                      in_wo[k * 128:(k + 1) * 128, oc * 128:(oc + 1) * 128], "ldg")
            return p.ctr["ldg"]
        wo_rdy = [load_wo(0)]
        wo_war = [0, 0]
        nwar = [0, 0]
        gwar = [0, 0]
        for oc in ([] if os.environ.get("WO_XONLY") else range(NCH)):
            if oc + 1 < NCH:
                p.wait("gpsimd", "pe", wo_war[(oc + 1) % 2])
                wo_rdy.append(load_wo(oc + 1))
            p.wait("tensor", "ldg", wo_rdy[oc])
            for n in range(NT):
                sl = slice(n * 512, (n + 1) * 512)
                ps = pf[n % 2]
                p.wait("tensor", "dv", nwar[n % 2])
                for k in range(NCH):
                    p.op("tensor", lambda e, ps=ps[:], w=wo_s[oc % 2][:, k * 128:(k + 1) * 128],
                         a=xh[k][:, sl], s=(k == 0), t=(k == NCH - 1):
                         e.matmul(ps, w, a, start=s, stop=t), "pe")
                omm = p.ctr["pe"]
                p.wait("vector", "pe", omm)
                p.wait("vector", "gp", gwar[n % 2])
                p.op("vector", lambda e, ps=ps[:], t=tmpE[n % 2][:]: e.tensor_copy(t, ps), "dv")
                nwar[n % 2] = p.ctr["dv"]
                p.wait("gpsimd", "dv", p.ctr["dv"])
                xoc = c["x"][oc]
                if os.environ.get("WO_NORES"):
                    p.op("gpsimd", lambda e, d=xoc[:, sl], t=tmpE[n % 2][:]: e.tensor_copy(d, t), "gp")
                else:
                    p.op("gpsimd", lambda e, d=xoc[:, sl], t=tmpE[n % 2][:]: e.tensor_add(d, d, t), "gp", nowait=True)
                gwar[n % 2] = p.ctr["gp"]
            wo_war[oc % 2] = p.ctr["pe"]
        p.wait("tensor", "gp", p.ctr["gp"])
        p.wait("vector", "gp", p.ctr["gp"])
        p.wait("sync", "gp", p.ctr["gp"])



def emit_l1_moe(p, c):
    pf, xh = c["pf"], c["xhat"]
    in_wr = p.inp("moe_wr", [D, EXP], F32).rearrange("(k p) f -> p k f", p=128)
    in_eb = p.inp("moe_ebias", [128, EXP], F32)
    in_wg = p.inp("moe_wg", [EXP, D, FMOE], F32)
    in_wu = p.inp("moe_wu", [EXP, D, FMOE], F32)
    in_wd = p.inp("moe_wd", [EXP, FMOE, D], F32)
    in_wgs = p.inp("moe_wgs", [D, FMOE], F32).rearrange("(k p) f -> p k f", p=128)
    in_wus = p.inp("moe_wus", [D, FMOE], F32).rearrange("(k p) f -> p k f", p=128)
    in_wds = p.inp("moe_wds", [FMOE, D], F32).rearrange("(k p) f -> p k f", p=128)
    with contextlib.ExitStack() as sc:
        wr_sb = p.sb("wr", [128, 64], BF16, sc)
        eb_sb = p.sb("eb", [128, 8], F32, sc)
        lfm = p.sb("lfm", [8, T], BF16, sc)
        ltm = p.sb("ltm", [128, 128], F32, sc)
        sctm = p.sb("sctm", [128, 128], F32, sc)
        val = p.sb("val", [128, 128], F32, sc)
        mA = p.sb("mA", [128, 128], F32, sc)
        mB = p.sb("mB", [128, 128], F32, sc)
        cs = p.sb("cs", [128, 128], F32, sc)
        m12 = p.sb("m12", [128, 32], F32, sc)
        ninf = p.sb("ninf", [128, 128], F32, sc)
        g_fm = p.sb("gfm", [8, T], F32, sc)
        wg_sb = p.sb("wg", [128, NCH * FMOE], BF16, sc)
        wu_sb = p.sb("wu", [128, NCH * FMOE], BF16, sc)
        wd_sb = p.sb("wd", [128, FCH * D], BF16, sc)
        gate_sb = [p.sb(f"gate{i}", [128, 512], BF16, sc) for i in range(FCH)]
        h_sb = [p.sb(f"hs{i}", [128, 512], BF16, sc) for i in range(FCH)]
        Gb = [p.sb("Gb0", [128, T], F32, sc)]
        tmpE = [p.sb(f"m_te{i}", [128, 512], F32, sc) for i in range(2)]
        p.dma("gpsimd", wr_sb[:].rearrange("p (k f) -> p k f", f=8), in_wr, "ldg")
        p.dma("sync", eb_sb[:], in_eb, "lds")
        r_rdy = p.ctr["ldg"]
        e_rdy = p.ctr["lds"]
        p.wait("tensor", "ldg", r_rdy)
        for n in range(NT):
            sl = slice(n * 512, (n + 1) * 512)
            ps = pf[0][0:8, :]
            for k in range(NCH):
                p.op("tensor", lambda e, ps=ps, w=wr_sb[:, k * 8:(k + 1) * 8], a=xh[k][:, sl],
                     s=(k == 0), t=(k == NCH - 1): e.matmul(ps, w, a, start=s, stop=t), "pe")
            p.wait("vector", "pe", p.ctr["pe"])
            p.op("vector", lambda e, d=lfm[0:8, sl], ps=ps: e.tensor_copy(d, ps), "dv")
            p.wait("tensor", "dv", p.ctr["dv"])
        for tc in range(16):
            p.op("tensor", lambda e, ps=c["pb16"][0:128, tc * 8:(tc + 1) * 8], a=lfm[0:8, tc * 128:(tc + 1) * 128]:
                 e.transpose(ps, a, c["id_bf"][0:8, 0:8]), "pe")
        p.wait("vector", "pe", p.ctr["pe"])
        p.op("vector", lambda e: e.tensor_copy(ltm[:], c["pb16"][0:128, 0:128]), "dv")
        p.wait("tensor", "dv", p.ctr["dv"])
        p.wait("scalar", "dv", p.ctr["dv"])
        p.op("scalar", lambda e: e.activation(sctm[:], ltm[:], AF.Sigmoid), "sc")
        p.wait("vector", "sc", p.ctr["sc"])
        p.wait("vector", "lds", e_rdy)
        v3 = val[:].rearrange("p (a b) -> p a b", b=8)
        s3 = sctm[:].rearrange("p (a b) -> p a b", b=8)
        p.op("vector", lambda e: e.tensor_tensor(
            v3, s3, eb_sb[:].unsqueeze(1).broadcast_to((128, 16, 8)), OP.add), "dv")
        p.op("vector", lambda e: e.memset(ninf[:], -1e30), "dv")
        p.op("vector", lambda e: e.tensor_reduce(
            m12[:, 0:16].rearrange("p (a o) -> p a o", o=1), v3, AX, OP.max), "dv")
        p.op("vector", lambda e: e.tensor_tensor(
            mA[:].rearrange("p (a b) -> p a b", b=8), v3,
            m12[:, 0:16].rearrange("p (a o) -> p a o", o=1).broadcast_to((128, 16, 8)), OP.is_ge), "dv")
        for tc in range(16):
            p.op("vector", lambda e, d=cs[:, tc * 8:(tc + 1) * 8], a=mA[:, tc * 8:(tc + 1) * 8]:
                 e.tensor_tensor_scan(d, a, a, 0.0, OP.add, OP.bypass), "dv", nowait=(tc < 15))
        p.op("vector", lambda e: e.tensor_scalar(cs[:], cs[:], 1.0, None, OP.is_equal), "dv")
        p.op("vector", lambda e: e.tensor_tensor(mA[:], mA[:], cs[:], OP.logical_and), "dv")
        p.op("vector", lambda e: e.tensor_scalar(cs[:], mA[:], 1e30, None, OP.mult), "dv")
        p.op("vector", lambda e: e.tensor_tensor(val[:], val[:], cs[:], OP.subtract), "dv")
        p.op("vector", lambda e: e.tensor_reduce(
            m12[:, 16:32].rearrange("p (a o) -> p a o", o=1), v3, AX, OP.max), "dv")
        p.op("vector", lambda e: e.tensor_tensor(
            mB[:].rearrange("p (a b) -> p a b", b=8), v3,
            m12[:, 16:32].rearrange("p (a o) -> p a o", o=1).broadcast_to((128, 16, 8)), OP.is_ge), "dv")
        for tc in range(16):
            p.op("vector", lambda e, d=cs[:, tc * 8:(tc + 1) * 8], a=mB[:, tc * 8:(tc + 1) * 8]:
                 e.tensor_tensor_scan(d, a, a, 0.0, OP.add, OP.bypass), "dv", nowait=(tc < 15))
        p.op("vector", lambda e: e.tensor_scalar(cs[:], cs[:], 1.0, None, OP.is_equal), "dv")
        p.op("vector", lambda e: e.tensor_tensor(mB[:], mB[:], cs[:], OP.logical_and), "dv")
        p.op("vector", lambda e: e.tensor_tensor(mA[:], mA[:], mB[:], OP.add), "dv")
        p.op("vector", lambda e: e.tensor_tensor(val[:], sctm[:], mA[:], OP.mult), "dv")
        p.op("vector", lambda e: e.tensor_reduce(
            m12[:, 0:16].rearrange("p (a o) -> p a o", o=1), v3, AX, OP.add), "dv")
        p.op("vector", lambda e: e.tensor_scalar(m12[:, 0:16], m12[:, 0:16], 1e-20, None, OP.add), "dv")
        p.op("vector", lambda e: e.reciprocal(m12[:, 0:16], m12[:, 0:16]), "dv")
        p.op("vector", lambda e: e.tensor_tensor(
            v3, v3, m12[:, 0:16].rearrange("p (a o) -> p a o", o=1).broadcast_to((128, 16, 8)), OP.mult), "dv")
        g_done = p.ctr["dv"]
        p.wait("tensor", "dv", g_done)
        for tc in range(16):
            p.op("tensor", lambda e, ps=pf[2][0:8, 0:128], a=val[:, tc * 8:(tc + 1) * 8]:
                 e.transpose(ps, a, c["id_f"][:]), "pe")
            p.wait("vector", "pe", p.ctr["pe"])
            p.op("vector", lambda e, d=g_fm[0:8, tc * 128:(tc + 1) * 128], ps=pf[2][0:8, 0:128]:
                 e.tensor_copy(d, ps), "dv")
            p.wait("tensor", "dv", p.ctr["dv"])
        p.barrier()
        def load_expert(e_idx):
            if e_idx == 0:
                p.dma("gpsimd", wg_sb[:].rearrange("p (k f) -> p k f", f=FMOE), in_wgs, "ldg")
                p.dma("gpsimd", wu_sb[:].rearrange("p (k f) -> p k f", f=FMOE), in_wus, "ldg")
                p.dma("gpsimd", wd_sb[:].rearrange("p (k f) -> p k f", f=D), in_wds, "ldg")
            else:
                ee = e_idx - 1
                p.dma("gpsimd", wg_sb[:].rearrange("p (k f) -> p k f", f=FMOE), in_wg[ee].rearrange("(k p) f -> p k f", p=128), "ldg")
                p.dma("gpsimd", wu_sb[:].rearrange("p (k f) -> p k f", f=FMOE), in_wu[ee].rearrange("(k p) f -> p k f", p=128), "ldg")
                p.dma("gpsimd", wd_sb[:].rearrange("p (k f) -> p k f", f=D), in_wd[ee].rearrange("(k p) f -> p k f", p=128), "ldg")
            return p.ctr["ldg"]

        w_rdy = load_expert(0)
        for eix in range(9):
            routed = eix > 0
            Gt = Gb[0]
            if routed:
                ee = eix - 1
                p.wait("tensor", "dv", p.ctr["dv"])
                for n in range(NT):
                    sl = slice(n * 512, (n + 1) * 512)
                    p.op("tensor", lambda e, ps=pf[6][0:128, :], l=c["esel_f"][:, ee * 128:(ee + 1) * 128],
                         r=g_fm[0:8, sl]: e.matmul(ps, l, r, start=True, stop=True), "pe")
                    p.wait("vector", "pe", p.ctr["pe"])
                    p.op("vector", lambda e, d=Gt[:, sl], ps=pf[6][0:128, :]: e.tensor_copy(d, ps), "dv")
                    p.wait("tensor", "dv", p.ctr["dv"])
            p.wait("tensor", "ldg", w_rdy)
            for n in range(NT):
                sl = slice(n * 512, (n + 1) * 512)
                for oc in range(FCH):
                    psA = pf[oc % 2]
                    psB = pf[2 + oc % 2]
                    p.wait("tensor", "sc", c.get("moe_scwar", 0) if oc < 2 and n == 0 and eix == 0 else c.get(f"moe_g{oc%2}", 0))
                    p.wait("tensor", "dv", c.get(f"moe_h{oc%2}", 0))
                    for k in range(NCH):
                        p.op("tensor", lambda e, ps=psA[:], w=wg_sb[:, k * FMOE + oc * 128: k * FMOE + oc * 128 + 128],
                             a=xh[k][:, sl], s=(k == 0), t=(k == NCH - 1):
                             e.matmul(ps, w, a, start=s, stop=t), "pe")
                    gm = p.ctr["pe"]
                    for k in range(NCH):
                        p.op("tensor", lambda e, ps=psB[:], w=wu_sb[:, k * FMOE + oc * 128: k * FMOE + oc * 128 + 128],
                             a=xh[k][:, sl], s=(k == 0), t=(k == NCH - 1):
                             e.matmul(ps, w, a, start=s, stop=t), "pe")
                    um = p.ctr["pe"]
                    p.wait("scalar", "pe", gm)
                    p.op("scalar", lambda e, d=gate_sb[oc][:], ps=psA[:]:
                         e.activation(d, ps, AF.Silu), "sc")
                    c[f"moe_g{oc%2}"] = p.ctr["sc"]
                    p.wait("vector", "pe", um)
                    p.wait("vector", "sc", p.ctr["sc"])
                    p.op("vector", lambda e, d=h_sb[oc][:], ps=psB[:], g=gate_sb[oc][:]:
                         e.scalar_tensor_tensor(d, ps, 1.0, g, OP.mult, OP.mult), "dv")
                    c[f"moe_h{oc%2}"] = p.ctr["dv"]
                if n == NT - 1:
                    c["moe_last_gu"] = p.ctr["pe"]
                h_done = p.ctr["dv"]
                p.wait("tensor", "dv", h_done)
                if n == 0:
                    p.wait("tensor", "ldg", c.get("moe_wd_rdy", 0))
                for oc2 in range(NCH):
                    psC = pf[4 + oc2 % 2]
                    p.wait("tensor", "dv", c.get(f"moe_d{oc2%2}", 0))
                    p.wait("tensor", "gp", c.get(f"moe_dg{oc2%2}", 0))
                    for kf in range(FCH):
                        p.op("tensor", lambda e, ps=psC[:], w=wd_sb[:, kf * D + oc2 * 128: kf * D + oc2 * 128 + 128],
                             a=h_sb[kf][:], s=(kf == 0), t=(kf == FCH - 1):
                             e.matmul(ps, w, a, start=s, stop=t), "pe")
                    dm = p.ctr["pe"]
                    p.wait("vector", "pe", dm)
                    xoc = c["x"][oc2]
                    if routed:
                        p.wait("vector", "gp", c.get(f"moe_dg{oc2%2}", 0))
                        p.op("vector", lambda e, t=tmpE[oc2 % 2][:], ps=psC[:], g=Gt[:, sl]:
                             e.tensor_tensor(t, ps, g, OP.mult), "dv")
                        c[f"moe_d{oc2%2}"] = p.ctr["dv"]
                        p.wait("gpsimd", "dv", p.ctr["dv"])
                        p.op("gpsimd", lambda e, d=xoc[:, sl], t=tmpE[oc2 % 2][:]:
                             e.tensor_add(d, d, t), "gp", nowait=True)
                        c[f"moe_dg{oc2%2}"] = p.ctr["gp"]
                    else:
                        p.op("vector", lambda e, d=xoc[:, sl], ps=psC[:]:
                             e.tensor_tensor(d, d, ps, OP.add), "dv")
                        c[f"moe_d{oc2%2}"] = p.ctr["dv"]
            if eix + 1 < 9:
                p.wait("gpsimd", "pe", c["moe_last_gu"])
                ee2 = eix
                p.dma("gpsimd", wg_sb[:].rearrange("p (k f) -> p k f", f=FMOE),
                      in_wg[ee2].rearrange("(k p) f -> p k f", p=128), "ldg")
                p.dma("gpsimd", wu_sb[:].rearrange("p (k f) -> p k f", f=FMOE),
                      in_wu[ee2].rearrange("(k p) f -> p k f", p=128), "ldg")
                w_rdy = p.ctr["ldg"]
                p.wait("gpsimd", "pe", p.ctr["pe"])
                p.dma("gpsimd", wd_sb[:].rearrange("p (k f) -> p k f", f=D),
                      in_wd[ee2].rearrange("(k p) f -> p k f", p=128), "ldg")
                c["moe_wd_rdy"] = p.ctr["ldg"]
        p.wait("tensor", "gp", p.ctr["gp"])
        p.wait("vector", "gp", p.ctr["gp"])
        p.wait("sync", "gp", p.ctr["gp"])


def build_l1(core, upto=4):
    p = Prog()
    for s in ["lds", "ldg", "lda", "pe", "dv", "sc", "gp", "st", "cc0", "bar"]:
        p.sem(s)
    c = make_common(p)
    x_scope = contextlib.ExitStack()
    xt = [p.sb(f"x{k}", [128, T], F32, x_scope) for k in range(NCH)]
    c["x"] = [t[:] for t in xt]
    x_in = p.inp("x_in", [D, T], F32)
    x_out = p.outp("x_out", [D, T], F32)
    emit_load_consts(p, c)
    for k in range(NCH):
        p.dma("sync", c["x"][k], x_in[k * 128:(k + 1) * 128, :], "lds")
    for eng in ENGS:
        p.wait(eng, "lds", p.ctr["lds"])
    dump_xhat = False
    if upto >= 1:
        emit_ln(p, c)
        p.barrier()
        dump_xhat = (upto == 1)
    if upto >= 2:
        emit_l1_attn(p, c, core, x_scope, x_in)
        p.barrier()
    if upto >= 3:
        emit_ln(p, c)
        p.barrier()
        dump_xhat = (upto == 3)
    if upto >= 4:
        emit_l1_moe(p, c)
        p.barrier()
    if dump_xhat:
        for k in range(NCH):
            p.dma("gpsimd", x_out[k * 128:(k + 1) * 128, :], c["xhat"][k][:], "st")
    else:
        for k in range(NCH):
            p.dma("sync", x_out[k * 128:(k + 1) * 128, :], c["x"][k], "st")
    p.wait("sync", "st", p.ctr["st"])
    p.wait("gpsimd", "st", p.ctr["st"])
    if upto < 2:
        x_scope.close()
    return p.finalize()


def build_fused():
    p = Prog()
    for sm in ["lds", "ldg", "lda", "pe", "dv", "sc", "gp", "st", "sp", "cc0", "bar"]:
        p.sem(sm)
    c = make_common(p)
    x_scope = contextlib.ExitStack()
    xt = [p.sb(f"x{k}", [128, T], F32, x_scope) for k in range(NCH)]
    c["x"] = [t[:] for t in xt]
    x_in = p.inp("x_in", [D, T], F32)
    x_out = p.outp("x_out", [D, T], F32)
    x_spill = p.idram("x_spill", [D, T], F32)
    emit_load_consts(p, c)
    for k in range(NCH):
        p.dma("sync", c["x"][k], x_in[k * 128:(k + 1) * 128, :], "lds")
    for eng in ENGS:
        p.wait(eng, "lds", p.ctr["lds"])
    emit_ln(p, c)
    p.barrier()
    emit_l0_attn(p, c)
    p.barrier()
    emit_ln(p, c)
    p.barrier()
    emit_l0_ffn(p, c)
    p.barrier()
    emit_ln(p, c)
    p.barrier()
    for k in range(NCH):
        p.dma("sync", x_spill[k * 128:(k + 1) * 128, :], c["x"][k], "sp")
    p.wait("sync", "sp", p.ctr["sp"])
    p.barrier()
    emit_l1_attn(p, c, 0, x_scope, x_spill)
    p.barrier()
    emit_ln(p, c)
    p.barrier()
    emit_l1_moe(p, c)
    p.barrier()
    for k in range(NCH):
        p.dma("sync", x_out[k * 128:(k + 1) * 128, :], c["x"][k], "st")
    p.wait("sync", "st", p.ctr["st"])
    return p.finalize()


def rope_tables():
    inv = 10000.0 ** (-np.arange(32) / 32.0)
    ang = np.arange(S)[None, :] * inv[:, None]
    cs, sn = np.cos(ang), np.sin(ang)
    C4 = np.concatenate([cs, cs, cs, cs], axis=0)
    S4 = np.concatenate([-sn, sn, -sn, sn], axis=0)
    return C4.astype(BF), S4.astype(BF)


def prep_core_l1(inputs, core, x_mid):
    d = dict(host_consts())
    d["x_in"] = x_mid
    d["wq"] = np.ascontiguousarray(inputs["L1_attn_W_q"])
    d["wdkv"] = np.ascontiguousarray(inputs["L1_attn_W_dkv"])
    d["wuk"] = np.ascontiguousarray(inputs["L1_attn_W_uk"])
    d["wuv"] = np.ascontiguousarray(inputs["L1_attn_W_uv"])
    d["wo"] = np.ascontiguousarray(inputs["L1_attn_W_o"])
    d["l1_gam"] = np.ascontiguousarray(np.asarray(inputs["L1_attn_ln_gamma"]).reshape(2, 128).T)
    d["l1_bet"] = np.ascontiguousarray(np.asarray(inputs["L1_attn_ln_beta"]).reshape(2, 128).T)
    C4, S4 = rope_tables()
    d["ropeC"], d["ropeS"] = C4, S4
    sg = segs(core)
    d["qropeC"] = np.ascontiguousarray(np.concatenate([C4[:, a:a + l] for (a, l) in sg], axis=1))
    d["qropeS"] = np.ascontiguousarray(np.concatenate([S4[:, a:a + l] for (a, l) in sg], axis=1))
    Q0s = [0, 512, 3072, 3584] if core % 2 == 0 else [1024, 1536, 2048, 2560]
    NKC = [12, 16, 28, 32]
    UMAP = {0: [0, 1, 2, 3, 8, 9, 10, 11], 1: [4, 5, 6, 7, 12, 13, 14, 15],
            2: [16, 17, 18, 19, 24, 25, 26, 27], 3: [20, 21, 22, 23, 28, 29, 30, 31]}
    ZOF = [0, 12, 28, 56]
    zv = np.zeros((88,), np.float32)
    for qt in range(4):
        for kc in range(NKC[qt]):
            if kc * 128 - Q0s[qt] >= 512:
                zv[ZOF[qt] + kc] = -1e30
    d["zvb"] = np.ascontiguousarray(np.broadcast_to(zv[None, :], (128, 88)).copy())
    masks = np.zeros((128, 2 * 8 * 512), np.float32)
    kk = np.arange(128)[:, None]
    jj = np.arange(512)[None, :]
    for g in range(2):
        qt = 2 * g
        for u in range(8):
            kc = UMAP[qt][u]
            m = ((kc * 128 + kk) <= (Q0s[qt] + jj)).astype(np.float32)
            masks[:, (g * 8 + u) * 512:(g * 8 + u + 1) * 512] = m
    d["cmasks"] = masks.astype(BF)
    d["moe_wr"] = np.ascontiguousarray(inputs["L1_ffn_W_gate_router"])
    d["moe_ebias"] = np.ascontiguousarray(
        np.broadcast_to(np.asarray(inputs["L1_ffn_expert_bias"])[None, :], (128, 8)).copy())
    d["moe_wg"] = np.ascontiguousarray(inputs["L1_ffn_Wg"])
    d["moe_wu"] = np.ascontiguousarray(inputs["L1_ffn_Wu"])
    d["moe_wd"] = np.ascontiguousarray(inputs["L1_ffn_Wd"])
    d["moe_wgs"] = np.ascontiguousarray(inputs["L1_ffn_Wg_s"])
    d["moe_wus"] = np.ascontiguousarray(inputs["L1_ffn_Wu_s"])
    d["moe_wds"] = np.ascontiguousarray(inputs["L1_ffn_Wd_s"])
    return d



def build_l0(core, upto=4):
    p = Prog()
    for s in ["lds", "ldg", "lda", "pe", "dv", "sc", "gp", "st", "cc0", "bar"]:
        p.sem(s)
    c = make_common(p)
    x_scope = contextlib.ExitStack()
    xt = [p.sb(f"x{k}", [128, T], F32, x_scope) for k in range(NCH)]
    c["x"] = [t[:] for t in xt]
    x_in = p.inp("x_in", [D, T], F32)
    x_out = p.outp("x_out", [D, T], F32)
    emit_load_consts(p, c)
    for k in range(NCH):
        p.dma("sync", c["x"][k], x_in[k * 128:(k + 1) * 128, :], "lds")
    for eng in ENGS:
        p.wait(eng, "lds", p.ctr["lds"])
    dump_xhat = False
    if upto >= 1:
        emit_ln(p, c)
        p.barrier()
        dump_xhat = (upto == 1)
    if upto >= 2:
        emit_l0_attn(p, c)
        p.barrier()
    if upto >= 3:
        emit_ln(p, c)
        p.barrier()
        dump_xhat = (upto == 3)
    if upto >= 4:
        emit_l0_ffn(p, c)
        p.barrier()
    if dump_xhat:
        for k in range(NCH):
            p.dma("gpsimd", x_out[k * 128:(k + 1) * 128, :], c["xhat"][k][:], "st")
    else:
        for k in range(NCH):
            p.dma("sync", x_out[k * 128:(k + 1) * 128, :], c["x"][k], "st")
    p.wait("sync", "st", p.ctr["st"])
    p.wait("gpsimd", "st", p.ctr["st"])
    x_scope.close()
    return p.finalize()


def prep_core_l0(inputs, core):
    b = core // 2
    x = np.asarray(inputs["L0_attn_x"])
    xs = np.concatenate([x[b, s:s + l, :] for (s, l) in segs(core)], axis=0)
    d = dict(host_consts())
    d["x_in"] = np.ascontiguousarray(xs.T)
    d["w_qkv"] = np.ascontiguousarray(inputs["L0_attn_w_qkv"])
    d["w_out"] = np.ascontiguousarray(inputs["L0_attn_w_out"])
    d["b_out_col"] = np.ascontiguousarray(np.asarray(inputs["L0_attn_b_out"]).reshape(8, 128).T)
    d["ffn_wT"] = np.ascontiguousarray(np.asarray(inputs["L0_ffn_weight"]).T)
    d["ffn_alpha"] = np.ascontiguousarray(np.asarray(inputs["L0_ffn_alpha"]).reshape(1, 1))
    d["ffn_bias_col"] = np.ascontiguousarray(np.asarray(inputs["L0_ffn_bias"]).reshape(8, 128).T)
    return d


import os
def assemble(results, cores):
    out = np.zeros((4, S, D), np.float32)
    for cc in cores:
        b = cc // 2
        xo = results[cc]["x_out"]
        pos = 0
        for (s, l) in segs(cc):
            out[b, s:s + l, :] = xo[:, pos:pos + l].T
            pos += l
    return out


def kernel(**inputs):
    cores = list(range(8))
    if os.environ.get("UNFUSED", "") != "1":
        maps = []
        for cc in cores:
            m = prep_core_l0(inputs, cc)
            m1 = prep_core_l1(inputs, cc, None)
            del m1["x_in"]
            m.update(m1)
            maps.append(m)
        nc = build_fused()
        res = run_bass_kernel_spmd(nc, maps, core_ids=cores)
        return assemble(res.results, cores)
    stage = os.environ.get("STAGE", "all")
    upto0 = int(os.environ.get("UPTO0", os.environ.get("UPTO", "4")))
    upto1 = int(os.environ.get("UPTO1", os.environ.get("UPTO", "4")))
    if stage in ("all", "l0"):
        maps0 = [prep_core_l0(inputs, cc) for cc in cores]
        nc0 = build_l0(0, upto=upto0)
        res0 = run_bass_kernel_spmd(nc0, maps0, core_ids=cores)
        if os.environ.get("DBG"):
            kernel.last_results = res0.results
        xmid = [np.ascontiguousarray(res0.results[cc]["x_out"]) for cc in cores]
        if stage == "l0":
            return assemble(res0.results, cores)
    else:
        xmid = None
    maps1 = [prep_core_l1(inputs, cc, xmid[cc]) for cc in cores]
    nc1 = build_l1(0, upto=upto1)
    res1 = run_bass_kernel_spmd(nc1, maps1, core_ids=cores)
    if os.environ.get("DBG"):
        kernel.last_results = res1.results
    return assemble(res1.results, cores)
